# version 20
# baseline (speedup 1.0000x reference)
import contextlib
import numpy as np
import ml_dtypes
import concourse.bass as bass
import concourse.mybir as mybir
from concourse.bass_utils import run_bass_kernel_spmd
from concourse.bass_core_rust import IndirectOffsetOnAxis

F32 = mybir.dt.float32
U8 = mybir.dt.uint8
BF16 = mybir.dt.bfloat16
I32 = mybir.dt.int32
U32 = mybir.dt.uint32
AFT = mybir.ActivationFunctionType
ALU = mybir.AluOpType

T = 4096
DM = 2048
NH = 8
D = 128
EQ = NH * D
HID = 5632
NHC = HID // 128
TH = 2048
TB = 512
NTB = T // TB
QB = 512
NQB = T // QB
FTB = 1024
EPS = 1e-5
DEPS = float(128 * EPS)
C1 = float(EPS / 128.0)
C2 = float(EPS * EPS)


def _r(ap, pat, **kw):
    return ap.rearrange(pat, **kw)


def build(phases=(1, 2, 3, 4, 5), debug_outs=()):
    nc = bass.Bass("TRN2", target_bir_lowering=False)

    xT = nc.dram_tensor("xT", [128, NTB * 16 * TB], F32, kind="ExternalInput")
    xres = nc.dram_tensor("xres", [TH, DM], F32, kind="ExternalInput")
    wqd_ = nc.dram_tensor("wq", [EQ, DM], F32, kind="ExternalInput")
    wkd_ = nc.dram_tensor("wk", [EQ, DM], F32, kind="ExternalInput")
    wvT = nc.dram_tensor("wvT", [128, 16 * EQ], F32, kind="ExternalInput")
    woT = nc.dram_tensor("woT", [128, 16 * DM], F32, kind="ExternalInput")
    wgT = nc.dram_tensor("wgT", [128, 22 * 16 * 256], U8, kind="ExternalInput")
    wuT = nc.dram_tensor("wuT", [128, 22 * 16 * 256], U8, kind="ExternalInput")
    wdT = nc.dram_tensor("wdT", [128, 4 * NHC * 512], U8, kind="ExternalInput")
    sgd = nc.dram_tensor("sg", [128, NHC], F32, kind="ExternalInput")
    sud = nc.dram_tensor("su", [128, NHC], F32, kind="ExternalInput")
    sdrow = nc.dram_tensor("sdrow", [1, DM], F32, kind="ExternalInput")
    gamd = nc.dram_tensor("gam", [128, 1], F32, kind="ExternalInput")
    betd = nc.dram_tensor("bet", [128, 1], F32, kind="ExternalInput")
    cmatd = nc.dram_tensor("cmat", [128, 128], BF16, kind="ExternalInput")
    identd = nc.dram_tensor("identm", [128, 128], BF16, kind="ExternalInput")
    masksd = nc.dram_tensor("masks", [128, 4 * QB], BF16, kind="ExternalInput")
    identEd = nc.dram_tensor("identE", [128, 4 * 512], BF16, kind="ExternalInput")
    onescd = nc.dram_tensor("onescol", [128, 1], BF16, kind="ExternalInput")
    onesrd = nc.dram_tensor("onesrow", [1, 128], BF16, kind="ExternalInput")
    onesrfd = nc.dram_tensor("onesrowf", [1, 128], F32, kind="ExternalInput")
    oidxd = nc.dram_tensor("oidx", [128, 16], U32, kind="ExternalInput")
    out = nc.dram_tensor("out", [TH, DM], F32, kind="ExternalOutput")

    qs = nc.dram_tensor("qs", [EQ, T], BF16)
    ks = nc.dram_tensor("ks", [EQ, T], BF16)
    vs = nc.dram_tensor("vs", [T, EQ], BF16)
    vqh = nc.dram_tensor("vqh", [NH, T], F32)
    agin = nc.dram_tensor("agin", [2 * EQ, TH], BF16)
    agout = nc.dram_tensor("agout", [8 * 2 * EQ, TH], BF16)
    x1f = nc.dram_tensor("x1f", [DM, TH], BF16)
    opre = nc.dram_tensor("opre", [TH, DM], F32)

    dbg = {}
    dbg_specs = {"d_qs": (qs, BF16), "d_ks": (ks, BF16), "d_vs": (vs, BF16),
                 "d_vqh": (vqh, F32), "d_agin": (agin, BF16),
                 "d_agout": (agout, BF16), "d_x1f": (x1f, BF16),
                 "d_opre": (opre, F32)}
    for name in debug_outs:
        src, dt = dbg_specs[name]
        dbg[name] = (nc.dram_tensor(name, list(src.shape), dt,
                                    kind="ExternalOutput"), src)

    keep = []

    def sb(name, shape, dtype):
        h = nc.alloc_sbuf_tensor(name, shape, dtype)
        keep.append(h)
        return h

    SEM_NAMES = [
        "c_ld", "c_mm", "c_ev", "wld", "wsc", "wmm", "wev",
        "xld", "p1mm", "sq2", "vev", "vqm", "vqe", "ksm", "rsa", "rsq", "rsd",
        "spq", "spk", "spv", "spvq",
        "kvld", "qld", "vqld", "p2ms", "sqa", "mka", "p2oz", "zrv", "denv",
        "bcm", "recv", "oev", "ost",
        "agc", "gth", "wol", "xrl", "p3mm", "z1v", "lnv", "p3sq", "apv",
        "trm", "trv", "x1st",
        "x1l", "wldg", "deq", "gumm", "sil", "gml",
        "wdl", "wdq", "dmm", "dev", "dst",
        "p5l", "p5bn", "p5sq", "p5ap", "p5st", "l3u", "l5u", "sqv", "kvv", "wdc", "mld", "ield", "gth1", "gth2", "gth3", "rcs",
        "dbgs",
    ]
    S = {n: nc.alloc_semaphore(name=n) for n in SEM_NAMES}

    cmat = sb("cmat_s", [128, 128], BF16)
    ident = sb("ident_s", [128, 128], BF16)
    onesc = sb("onesc_s", [128, 1], BF16)
    onesr = sb("onesr_s", [1, 128], BF16)
    onesrf = sb("onesrf_s", [1, 128], F32)
    oidx = sb("oidx_s", [128, 16], U32)
    gam = sb("gam_s", [128, 1], F32)
    betbf = sb("betbf_s", [128, 1], BF16)
    cbeta = sb("cbeta_s", [128, 1], F32)
    sg = sb("sg_s", [128, NHC], F32)
    su = sb("su_s", [128, NHC], F32)
    sdbc = sb("sdbc_s", [128, DM], BF16)
    rsk = sb("rsk_s", [128, NH * 32], F32)
    bst5 = sb("bst5_s", [128, 64 * 6], F32)

    with (
        nc.sbuf_tensor("wtq", [128, 16 * EQ], BF16) as wtq,
        nc.sbuf_tensor("wtk", [128, 16 * EQ], BF16) as wtk,
        nc.sbuf_tensor("wvs", [128, 16 * EQ], BF16) as wvs,
    ):
        with (
            nc.sbuf_tensor("sdbf_s", [1, DM], BF16) as sdbf,
            nc.sbuf_tensor("wraw", [128, DM], BF16) as wraw,
            nc.sbuf_tensor("wgam", [128, DM], BF16) as wgam,
            nc.psum_tensor("p0a", [128, 128], F32) as p0a,
            nc.psum_tensor("p0b", [128, 128], F32) as p0b,
            nc.psum_tensor("p0c", [128, 512], F32) as p0c,
            nc.Block() as block,
        ):
            @block.gpsimd
            def _(g):
                nld = 0
                for dst, srcd in [(cmat, cmatd), (ident, identd),
                                  (onesc, onescd),
                                  (onesr, onesrd), (oidx, oidxd), (gam, gamd),
                                  (sg, sgd), (su, sud), (betbf, betd),
                                  (sdbf, sdrow), (onesrf, onesrfd)]:
                    if nld:
                        g.wait_ge(S["c_ld"], nld)
                    g.dma_start(dst[:, :], srcd[:, :]).then_inc(S["c_ld"], 16)
                    nld += 16
                g.wait_ge(S["c_ld"], nld)
                g.dma_start(wvs[:, :], wvT[:, :]).then_inc(S["c_ld"], 16)
                for k in range(16):
                    wsrc = wqd_ if k < 8 else wkd_
                    h = k % 8
                    if k >= 1:
                        g.wait_ge(S["wsc"], k)
                        g.wait_ge(S["wld"], 16 * k)
                    g.dma_start(wraw[:, :], wsrc[h * 128:(h + 1) * 128, :]
                                ).then_inc(S["wld"], 16)

            @block.vector
            def _(v):
                v.wait_ge(S["c_mm"], 1)
                v.tensor_copy(cbeta[:, :], p0a[:, 0:1])
                v.engine_nop().then_inc(S["c_ev"], 1)
                for ebk in range(4):
                    v.wait_ge(S["c_mm"], 2 + ebk)
                    v.tensor_copy(sdbc[:, ebk * 512:(ebk + 1) * 512], p0c[:, :]
                                  ).then_inc(S["c_ev"], 1)
                for k in range(16):
                    v.wait_ge(S["wld"], 16 * (k + 1))
                    if k >= 1:
                        v.wait_ge(S["wmm"], 16 * k)
                    v.tensor_scalar(wgam[:, :], wraw[:, :], gam[:, :], None,
                                    ALU.mult, ALU.bypass).then_inc(S["wsc"], 1)
                    wt = wtq if k < 8 else wtk
                    h = k % 8
                    for fc in range(16):
                        v.wait_ge(S["wmm"], 16 * k + fc + 1)
                        bank = p0a if fc % 2 == 0 else p0b
                        v.tensor_copy(
                            wt[:, fc * EQ + h * 128: fc * EQ + (h + 1) * 128],
                            bank[:, :]).then_inc(S["wev"], 1)

            @block.tensor
            def _(t):
                t.wait_ge(S["c_ld"], 9 * 16)
                t.matmul(p0a[:, 0:1], cmat[:, :], betbf[:, :],
                         start=True, stop=True).then_inc(S["c_mm"], 1)
                t.wait_ge(S["c_ld"], 10 * 16)
                for ebk in range(4):
                    if ebk >= 1:
                        t.wait_ge(S["c_ev"], 1 + ebk)
                    t.matmul(p0c[:, :], onesr[:, :],
                             sdbf[:, ebk * 512:(ebk + 1) * 512],
                             start=True, stop=True).then_inc(S["c_mm"], 1)
                t.wait_ge(S["c_ev"], 5)
                for k in range(16):
                    t.wait_ge(S["wsc"], k + 1)
                    for fc in range(16):
                        gidx = 16 * k + fc
                        if gidx >= 2:
                            t.wait_ge(S["wev"], gidx - 1)
                        bank = p0a if fc % 2 == 0 else p0b
                        t.matmul(bank[:, :], wgam[:, fc * 128:(fc + 1) * 128],
                                 cmat[:, :], start=True, stop=True
                                 ).then_inc(S["wmm"], 1)

        if 1 in phases:
            with (
                nc.sbuf_tensor("xTs", [128, 2 * 16 * TB], BF16) as xTs,
                nc.sbuf_tensor("qcs", [128, 8 * TB], BF16) as qcs,
                nc.sbuf_tensor("kcs", [128, 8 * TB], BF16) as kcs,
                nc.sbuf_tensor("vvs", [128, 8 * TB], BF16) as vvs,
                nc.sbuf_tensor("q2s", [128, 16 * TB], BF16) as q2s,
                nc.sbuf_tensor("k2s", [128, 16 * TB], BF16) as k2s,
                nc.sbuf_tensor("vqr", [1, 2 * TB], F32) as vqr,
                nc.sbuf_tensor("rs1", [128, 32], F32) as rs1,
                nc.sbuf_tensor("rs2", [128, 32], F32) as rs2,
                nc.psum_tensor("pA", [128, TB], F32) as pA,
                nc.psum_tensor("pB", [128, TB], F32) as pB,
                nc.psum_tensor("pC", [128, TB], F32) as pC,
                nc.psum_tensor("pD", [128, TB], F32) as pD,
                nc.psum_tensor("pKS", [128, 32], F32) as pKS,
                nc.psum_tensor("pVQ0", [1, TB], F32) as pVQ0,
                nc.psum_tensor("pVQ1", [1, TB], F32) as pVQ1,
                nc.Block() as block,
            ):
                banks = [pA, pB, pC, pD]
                pVQs = [pVQ0, pVQ1]

                def xv(buf, fc, lo, hi):
                    base = buf * 16 * TB + fc * TB
                    return xTs[:, base + lo: base + hi]

                @block.gpsimd
                def _(g):
                    for tb in range(NTB):
                        if tb >= 2:
                            g.wait_ge(S["p1mm"], 384 * (tb - 1))
                        if tb >= 1:
                            g.wait_ge(S["xld"], 16 * tb)
                        g.dma_start(
                            xTs[:, (tb % 2) * 16 * TB:(tb % 2 + 1) * 16 * TB],
                            xT[:, tb * 16 * TB:(tb + 1) * 16 * TB]
                            ).then_inc(S["xld"], 16)

                @block.tensor
                def _(t):
                    t.wait_ge(S["c_ld"], 12 * 16)
                    for tb in range(NTB):
                        t.wait_ge(S["xld"], 16 * (tb + 1))
                        buf = tb % 2
                        for i in range(24):
                            gidx = tb * 24 + i
                            bank = banks[gidx % 4]
                            if gidx >= 4:
                                t.wait_ge(S["vev"], gidx - 3)
                            if i < 8:
                                for fc in range(16):
                                    t.matmul(bank[:, :],
                                             wtq[:, fc * EQ + i * 128:
                                                 fc * EQ + (i + 1) * 128],
                                             xv(buf, fc, 0, TB),
                                             start=(fc == 0), stop=(fc == 15)
                                             ).then_inc(S["p1mm"], 1)
                            elif i < 16:
                                e = i - 8
                                for fc in range(16):
                                    t.matmul(bank[:, :],
                                             wtk[:, fc * EQ + e * 128:
                                                 fc * EQ + (e + 1) * 128],
                                             xv(buf, fc, 0, TB),
                                             start=(fc == 0), stop=(fc == 15)
                                             ).then_inc(S["p1mm"], 1)
                            else:
                                tt = (i - 16) // 2
                                eb = (i - 16) % 2
                                for fc in range(16):
                                    t.matmul(bank[:, :],
                                             xv(buf, fc, tt * 128, (tt + 1) * 128),
                                             wvs[:, fc * EQ + eb * 512:
                                                 fc * EQ + (eb + 1) * 512],
                                             start=(fc == 0), stop=(fc == 15)
                                             ).then_inc(S["p1mm"], 1)
                        for h in range(NH):
                            t.wait_ge(S["sq2"], 16 * tb + h + 1)
                            if tb * 8 + h >= 2:
                                t.wait_ge(S["vqe"], tb * 8 + h - 1)
                            t.matmul(pVQs[(tb * 8 + h) % 2][:, :], onesc[:, :],
                                     q2s[:, (tb % 2) * 8 * TB + h * TB:
                                         (tb % 2) * 8 * TB + (h + 1) * TB],
                                     start=True, stop=True).then_inc(S["vqm"], 1)
                        if tb >= 1:
                            t.wait_ge(S["rsd"], tb)
                        for h in range(NH):
                            t.wait_ge(S["sq2"], 16 * tb + 8 + h + 1)
                            for tt in range(4):
                                col = h * 4 + tt
                                t.matmul(pKS[:, col:col + 1],
                                         k2s[:, (tb % 2) * 8 * TB + h * TB + tt * 128:
                                             (tb % 2) * 8 * TB + h * TB + (tt + 1) * 128],
                                         onesc[:, :], start=True, stop=True
                                         ).then_inc(S["ksm"], 1)

                @block.scalar
                def _(s):
                    for tb in range(NTB):
                        for i in range(16):
                            gidx = tb * 24 + i
                            h = i % 8
                            s.wait_ge(S["p1mm"], gidx * 16 + 16)
                            if tb >= 2:
                                if i < 8:
                                    s.wait_ge(S["vqm"], 8 * (tb - 2) + h + 1)
                                else:
                                    s.wait_ge(S["ksm"], 32 * (tb - 2) + 4 * (h + 1))
                            bank = banks[gidx % 4]
                            st = q2s if i < 8 else k2s
                            so = (tb % 2) * 8 * TB
                            s.activation(st[:, so + h * TB:so + (h + 1) * TB],
                                         bank[:, :],
                                         AFT.Square, bias=cbeta[:, :], scale=1.0
                                         ).then_inc(S["sq2"], 1)
                        for h in range(NH):
                            gq = tb * 8 + h
                            s.wait_ge(S["vqe"], gq + 1)
                            if gq >= 1:
                                s.wait_ge(S["spvq"], 16 * gq)
                            s.dma_start(vqh[h:h + 1, tb * TB:(tb + 1) * TB],
                                        vqr[:, (gq % 2) * TB:(gq % 2 + 1) * TB]
                                        ).then_inc(S["spvq"], 16)
                        s.wait_ge(S["rsa"], tb + 1)
                        if tb >= 1:
                            s.wait_ge(S["rsd"], tb)
                        s.sqrt(rs2[:, :], rs1[:, :]).then_inc(S["rsq"], 1)

                @block.vector
                def _(v):
                    for tb in range(NTB):
                        for i in range(24):
                            gidx = tb * 24 + i
                            bank = banks[gidx % 4]
                            if i < 16:
                                h = i % 8
                                v.wait_ge(S["sq2"], 16 * tb + i + 1)
                                if tb >= 1:
                                    sp = S["spq"] if i < 8 else S["spk"]
                                    v.wait_ge(sp, 16 * (8 * (tb - 1) + h + 1))
                                st = qcs if i < 8 else kcs
                                v.tensor_scalar(st[:, h * TB:(h + 1) * TB],
                                                bank[:, :], cbeta[:, :], None,
                                                ALU.add, ALU.bypass
                                                ).then_inc(S["vev"], 1)
                            else:
                                j = i - 16
                                v.wait_ge(S["p1mm"], gidx * 16 + 16)
                                if tb >= 1:
                                    v.wait_ge(S["spv"],
                                              16 * (8 * (tb - 1) + j + 1))
                                v.tensor_copy(vvs[:, j * TB:(j + 1) * TB],
                                              bank[:, :]).then_inc(S["vev"], 1)
                        for h in range(NH):
                            gq = tb * 8 + h
                            v.wait_ge(S["vqm"], gq + 1)
                            if gq >= 2:
                                v.wait_ge(S["spvq"], 16 * (gq - 1))
                            v.tensor_copy(vqr[:, (gq % 2) * TB:(gq % 2 + 1) * TB],
                                          pVQs[gq % 2][:, :]).then_inc(S["vqe"], 1)
                        v.wait_ge(S["ksm"], 32 * (tb + 1))
                        v.tensor_scalar(rs1[:, :], pKS[:, :], DEPS, None,
                                        ALU.add, ALU.bypass).then_inc(S["rsa"], 1)
                        v.wait_ge(S["rsq"], tb + 1)
                        v.reciprocal(
                            _r(rsk[:, :], "p (h k) -> p h k", h=NH)
                            [:, :, tb * 4:(tb + 1) * 4],
                            _r(rs2[:, :], "p (h k) -> p h k", h=NH)
                        ).then_inc(S["rsd"], 1)

                @block.sync
                def _(sy):
                    for tb in range(NTB):
                        for i in range(24):
                            sy.wait_ge(S["vev"], tb * 24 + i + 1)
                            if i < 8:
                                if tb * 8 + i >= 1:
                                    sy.wait_ge(S["spq"], 16 * (tb * 8 + i))
                                sy.dma_start(qs[i * 128:(i + 1) * 128,
                                                tb * TB:(tb + 1) * TB],
                                             qcs[:, i * TB:(i + 1) * TB]
                                             ).then_inc(S["spq"], 16)
                            elif i < 16:
                                h = i - 8
                                if tb * 8 + h >= 1:
                                    sy.wait_ge(S["spk"], 16 * (tb * 8 + h))
                                sy.dma_start(ks[h * 128:(h + 1) * 128,
                                                tb * TB:(tb + 1) * TB],
                                             kcs[:, h * TB:(h + 1) * TB]
                                             ).then_inc(S["spk"], 16)
                            else:
                                j = i - 16
                                tt, eb = j // 2, j % 2
                                if tb * 8 + j >= 1:
                                    sy.wait_ge(S["spv"], 16 * (tb * 8 + j))
                                sy.dma_start(vs[tb * TB + tt * 128:
                                                tb * TB + (tt + 1) * 128,
                                                eb * 512:(eb + 1) * 512],
                                             vvs[:, j * TB:(j + 1) * TB]
                                             ).then_inc(S["spv"], 16)
                    sy.wait_ge(S["spq"], 16 * 64)
                    sy.wait_ge(S["spk"], 16 * 64)
                    sy.wait_ge(S["spv"], 16 * 64)
                    sy.wait_ge(S["spvq"], 16 * 64)

    wos_cm = (nc.sbuf_tensor("wos", [128, 16 * DM], BF16)
              if (2 in phases or 3 in phases) else None)
    wos = wos_cm.__enter__() if wos_cm is not None else None
    if 2 in phases:
        seq = [(h, half * 4 + j) for half in range(2)
               for h in range(NH) for j in range(4)]
        nkt = {qb: 4 * (qb + 1) for qb in range(NQB)}
        ga_base = {}
        oz_end = {}
        run = 0
        ozr = 0
        for (h, qb) in seq:
            ga_base[(h, qb)] = run
            run += nkt[qb]
            ozr += 2 * nkt[qb]
            oz_end[(h, qb)] = ozr
        mk_idx = {}
        mrun = 0
        for (h, qb) in seq:
            for kt in range(nkt[qb] - 4, nkt[qb]):
                mk_idx[(h, qb, kt)] = mrun
                mrun += 1

        def is_masked(qb, kt):
            return kt >= nkt[qb] - 4

        def sq_on_vector(kt):
            return kt % 3 == 2 and kt > 5

        cntS, cntV = {}, {}
        _cs = _cv = 0
        for (_h, _qb) in seq:
            for _kt in range(nkt[_qb]):
                if sq_on_vector(_kt):
                    _cv += 1
                else:
                    _cs += 1
                _ga = ga_base[(_h, _qb)] + _kt
                cntS[_ga], cntV[_ga] = _cs, _cv

        sq_owner_vec = {}
        for (_h, _qb) in seq:
            for _kt in range(nkt[_qb]):
                sq_owner_vec[ga_base[(_h, _qb)] + _kt] = sq_on_vector(_kt)

        def wait_sq(eng, ga):
            if sq_owner_vec[ga]:
                eng.wait_ge(S["sqv"], cntV[ga])
            else:
                eng.wait_ge(S["sqa"], cntS[ga])

        with contextlib.ExitStack() as _st:
            _e = _st.enter_context
            masks = _e(nc.sbuf_tensor("masks_s", [128, 4 * QB], BF16))
            khs = _e(nc.sbuf_tensor("khs", [128, 2 * T], BF16))
            vhs = _e(nc.sbuf_tensor("vhs", [128, 2 * 32 * 128], BF16))
            qbs = _e(nc.sbuf_tensor("qbs", [128, 4 * QB], BF16))
            a_s = _e(nc.sbuf_tensor("a_s", [128, 3 * QB], BF16))
            ams = _e(nc.sbuf_tensor("ams", [128, 3 * QB], BF16))
            vqhd = _e(nc.sbuf_tensor("vqhd", [1, T], F32))
            zrow = _e(nc.sbuf_tensor("zrow", [1, QB], F32))
            lnr = _e(nc.sbuf_tensor("lnr", [1, QB], F32))
            rcr = _e(nc.sbuf_tensor("rcr", [1, QB], F32))
            den1 = _e(nc.sbuf_tensor("den1", [1, QB], F32))
            den2 = _e(nc.sbuf_tensor("den2", [1, QB], F32))
            recb = _e(nc.sbuf_tensor("recb", [128, QB], F32))
            vsq = _e(nc.sbuf_tensor("vsq", [128, QB], BF16))
            obs = _e(nc.sbuf_tensor("obs", [128, 2 * QB], BF16))
            pS0 = _e(nc.psum_tensor("pS0", [128, QB], F32))
            pS1 = _e(nc.psum_tensor("pS1", [128, QB], F32))
            pS2 = _e(nc.psum_tensor("pS2", [128, QB], F32))
            pO0 = _e(nc.psum_tensor("pO0", [128, QB], F32))
            pO1 = _e(nc.psum_tensor("pO1", [128, QB], F32))
            pZ0 = _e(nc.psum_tensor("pZ0", [1, QB], F32))
            pZ1 = _e(nc.psum_tensor("pZ1", [1, QB], F32))
            pBC = _e(nc.psum_tensor("pBC", [128, QB], F32))
            block = _e(nc.Block())
            True
            pS = [pS0, pS1, pS2]
            pOs = [pO0, pO1]
            pZs = [pZ0, pZ1]

            @block.gpsimd
            def _(g):
                if 3 not in phases:
                    return
                g.dma_start(wos[:, :], woT[:, :]).then_inc(S["wol"], 16)
                for q4 in range(4):
                    g.wait_ge(S["ost"], 16 * 16 * (q4 + 1))
                    g.collective_compute(
                        "AllGather", ALU.bypass,
                        replica_groups=[[0, 1, 2, 3, 4, 5, 6, 7]],
                        ins=[agin[q4 * 512:(q4 + 1) * 512, :].opt()],
                        outs=[agout[q4 * 8 * 512:(q4 + 1) * 8 * 512, :].opt()],
                    ).then_inc(S["agc"], 1)

            @block.sync
            def _(sy):
                sy.dma_start(masks[:, :], masksd[:, :]).then_inc(S["mld"], 16)
                def qload(gl):
                    hp, qp = seq[gl]
                    if gl >= 4:
                        hx, qx = seq[gl - 4]
                        sy.wait_ge(S["p2ms"], ga_base[(hx, qx)] + nkt[qx])
                    if gl >= 1:
                        sy.wait_ge(S["qld"], 16 * gl)
                    sy.dma_start(qbs[:, (gl % 4) * QB:(gl % 4 + 1) * QB],
                                 qs[hp * 128:(hp + 1) * 128,
                                    qp * QB:(qp + 1) * QB]
                                 ).then_inc(S["qld"], 16)

                qload(0)
                qload(1)
                for gq in range(64):
                    if gq + 2 < 64:
                        qload(gq + 2)
                    if gq >= 1:
                        sy.wait_ge(S["oev"], gq)
                        if gq >= 2:
                            sy.wait_ge(S["ost"], 16 * (gq - 1))
                        hp, qp = seq[gq - 1]
                        hfp, qq = qp // 4, qp % 4
                        sy.dma_start(
                            agin[hfp * EQ + hp * 128:
                                 hfp * EQ + (hp + 1) * 128,
                                 qq * QB:(qq + 1) * QB],
                            obs[:, ((gq - 1) % 2) * QB:((gq - 1) % 2 + 1) * QB]
                            ).then_inc(S["ost"], 16)
                sy.wait_ge(S["oev"], 64)
                sy.wait_ge(S["ost"], 16 * 63)
                sy.dma_start(agin[EQ + 7 * 128: EQ + 8 * 128, 3 * QB:4 * QB],
                             obs[:, (63 % 2) * QB:(63 % 2 + 1) * QB]
                             ).then_inc(S["ost"], 16)
                sy.wait_ge(S["ost"], 16 * 64)

            def kv_load(s, grp):
                h = grp % 8
                if grp >= 2:
                    hp, qp = seq[(grp - 1) * 4 - 1]
                    s.wait_ge(S["p2oz"], oz_end[(hp, qp)])
                if grp >= 1:
                    s.wait_ge(S["kvld"], 16 * grp)
                    s.wait_ge(S["kvv"], 16 * grp)
                kb = (grp % 2) * T
                vb = (grp % 2) * 32 * 128
                s.dma_start(khs[:, kb:kb + T], ks[h * 128:(h + 1) * 128, :]
                            ).then_inc(S["kvld"], 16)
                s.dma_start(_r(vhs[:, vb:vb + 32 * 128],
                               "p (k d) -> p k d", k=32),
                            _r(vs[:, :], "(k p) e -> p k e", p=128)
                            [:, :, h * 128:(h + 1) * 128]
                            ).then_inc(S["kvv"], 16)

            @block.scalar
            def _(s):
                kv_load(s, 0)
                for grp in range(16):
                    half, h = grp // 8, grp % 8
                    if grp >= 1:
                        s.wait_ge(S["denv"], 4 * grp)
                        s.wait_ge(S["vqld"], 16 * grp)
                    s.dma_start(vqhd[:, :], vqh[h:h + 1, :]
                                ).then_inc(S["vqld"], 16)
                    for j in range(4):
                        gq = grp * 4 + j
                        if j == 1 and grp + 1 < 16:
                            kv_load(s, grp + 1)
                        qb = half * 4 + j
                        for kt in range(nkt[qb]):
                            if sq_on_vector(kt):
                                continue
                            ga = ga_base[(h, qb)] + kt
                            s.wait_ge(S["p2ms"], ga + 1)
                            if ga >= 3:
                                s.wait_ge(S["p2oz"], 2 * (ga - 2))
                            s.activation(a_s[:, (ga % 3) * QB:(ga % 3 + 1) * QB],
                                         pS[ga % 3][:, :], AFT.Square,
                                         bias=0.0,
                                         scale=rsk[:, h * 32 + kt:
                                                   h * 32 + kt + 1]
                                         ).then_inc(S["sqa"], 1)
                            if kt == 1 and gq >= 1:
                                s.wait_ge(S["denv"], gq)
                                if gq >= 2:
                                    s.wait_ge(S["bcm"], gq - 1)
                                s.activation(lnr[:, :], den2[:, :], AFT.Ln,
                                             bias=0.0, scale=1.0)
                                s.drain()
                                s.activation(rcr[:, :], lnr[:, :], AFT.Exp,
                                             bias=0.0, scale=-1.0
                                             ).then_inc(S["rcs"], 1)
                s.wait_ge(S["denv"], 64)
                s.wait_ge(S["bcm"], 63)
                s.activation(lnr[:, :], den2[:, :], AFT.Ln, bias=0.0, scale=1.0)
                s.drain()
                s.activation(rcr[:, :], lnr[:, :], AFT.Exp, bias=0.0,
                             scale=-1.0).then_inc(S["rcs"], 1)

            @block.vector
            def _(v):
                v.wait_ge(S["mld"], 16)
                for grp in range(16):
                    half, h = grp // 8, grp % 8
                    for j in range(4):
                        qb = half * 4 + j
                        gq = grp * 4 + j
                        v.drain()
                        for kt in range(nkt[qb]):
                            ga = ga_base[(h, qb)] + kt
                            if sq_on_vector(kt):
                                v.wait_ge(S["p2ms"], ga + 1)
                                if ga >= 3:
                                    v.wait_ge(S["p2oz"], 2 * (ga - 2))
                                v.tensor_scalar(vsq[:, :], pS[ga % 3][:, :],
                                                rsk[:, h * 32 + kt:
                                                    h * 32 + kt + 1],
                                                None, ALU.mult, ALU.bypass)
                                v.drain()
                                v.tensor_tensor(
                                    a_s[:, (ga % 3) * QB:(ga % 3 + 1) * QB],
                                    vsq[:, :], vsq[:, :], ALU.mult
                                    ).then_inc(S["sqv"], 1)
                                v.drain()
                            if not is_masked(qb, kt):
                                continue
                            jm = kt - (nkt[qb] - 4)
                            if not sq_on_vector(kt):
                                wait_sq(v, ga)
                            v.tensor_tensor(
                                ams[:, (ga % 3) * QB:(ga % 3 + 1) * QB],
                                a_s[:, (ga % 3) * QB:(ga % 3 + 1) * QB],
                                masks[:, jm * QB:(jm + 1) * QB],
                                ALU.mult).then_inc(S["mka"], 1)
                        v.wait_ge(S["p2oz"], oz_end[(h, qb)])
                        v.tensor_copy(zrow[:, :], pZs[gq % 2][:, :]
                                      ).then_inc(S["zrv"], 1)
                        if gq >= 1:
                            v.wait_ge(S["rcs"], gq)
                        v.wait_ge(S["vqld"], 16 * (grp + 1))
                        v.tensor_scalar(den1[:, :],
                                        vqhd[:, qb * QB:(qb + 1) * QB],
                                        C1, C2, ALU.mult, ALU.add)
                        v.drain()
                        v.tensor_tensor(den2[:, :], den1[:, :], zrow[:, :],
                                        ALU.add).then_inc(S["denv"], 1)
                        if gq >= 1:
                            gp = gq - 1
                            v.wait_ge(S["bcm"], gp + 1)
                            v.tensor_copy(recb[:, :], pBC[:, :]
                                          ).then_inc(S["recv"], 1)
                            v.drain()
                            if gp >= 2:
                                v.wait_ge(S["ost"], 16 * (gp - 1))
                            v.tensor_tensor(obs[:, (gp % 2) * QB:
                                                (gp % 2 + 1) * QB],
                                            pOs[gp % 2][:, :], recb[:, :],
                                            ALU.mult).then_inc(S["oev"], 1)
                gp = 63
                v.drain()
                v.wait_ge(S["bcm"], gp + 1)
                v.tensor_copy(recb[:, :], pBC[:, :]).then_inc(S["recv"], 1)
                v.drain()
                v.wait_ge(S["ost"], 16 * (gp - 1))
                v.tensor_tensor(obs[:, (gp % 2) * QB:(gp % 2 + 1) * QB],
                                pOs[gp % 2][:, :], recb[:, :],
                                ALU.mult).then_inc(S["oev"], 1)

            @block.tensor
            def _(t):
                pend = []

                def mmoz(grp, h, qb, gq, kt):
                    ga = ga_base[(h, qb)] + kt
                    n = nkt[qb]
                    wait_sq(t, ga)
                    if is_masked(qb, kt):
                        t.wait_ge(S["mka"], mk_idx[(h, qb, kt)] + 1)
                        asrc = ams
                    else:
                        asrc = a_s
                    if kt == 0 and gq >= 2:
                        t.wait_ge(S["oev"], gq - 1)
                        t.wait_ge(S["zrv"], gq - 1)
                    t.matmul(pOs[gq % 2][:, :],
                             vhs[:, (grp % 2) * 32 * 128 + kt * 128:
                                 (grp % 2) * 32 * 128 + (kt + 1) * 128],
                             asrc[:, (ga % 3) * QB:(ga % 3 + 1) * QB],
                             start=(kt == 0), stop=(kt == n - 1)
                             ).then_inc(S["p2oz"], 1)
                    t.matmul(pZs[gq % 2][:, :], onesc[:, :],
                             asrc[:, (ga % 3) * QB:(ga % 3 + 1) * QB],
                             start=(kt == 0), stop=(kt == n - 1)
                             ).then_inc(S["p2oz"], 1)

                for grp in range(16):
                    half, h = grp // 8, grp % 8
                    t.wait_ge(S["kvld"], 16 * (grp + 1))
                    t.wait_ge(S["kvv"], 16 * (grp + 1))
                    for j in range(4):
                        qb = half * 4 + j
                        gq = grp * 4 + j
                        t.wait_ge(S["qld"], 16 * (gq + 1))
                        n = nkt[qb]

                        def bc(gq=gq):
                            t.wait_ge(S["rcs"], gq)
                            if gq >= 2:
                                t.wait_ge(S["recv"], gq - 1)
                            t.matmul(pBC[:, :], onesrf[:, :], rcr[:, :],
                                     start=True, stop=True
                                     ).then_inc(S["bcm"], 1)

                        for step in range(n):
                            if step == 4 and gq >= 1:
                                bc()
                            kt = step
                            ga = ga_base[(h, qb)] + kt
                            if ga >= 3:
                                wait_sq(t, ga - 3)
                            t.matmul(pS[ga % 3][:, :],
                                     khs[:, (grp % 2) * T + kt * 128:
                                         (grp % 2) * T + (kt + 1) * 128],
                                     qbs[:, (gq % 4) * QB:(gq % 4 + 1) * QB],
                                     start=True, stop=True
                                     ).then_inc(S["p2ms"], 1)
                            if step < 2 and pend:
                                pend.pop(0)()
                            if step >= 2:
                                mmoz(grp, h, qb, gq, step - 2)
                        if n <= 4 and gq >= 1:
                            bc()
                        pend.append(
                            (lambda a, b, c, d, e:
                             lambda: mmoz(a, b, c, d, e))(grp, h, qb, gq, n - 2))
                        pend.append(
                            (lambda a, b, c, d, e:
                             lambda: mmoz(a, b, c, d, e))(grp, h, qb, gq, n - 1))
                for f in pend:
                    f()
                gq = 64
                t.wait_ge(S["rcs"], gq)
                t.wait_ge(S["recv"], gq - 1)
                t.matmul(pBC[:, :], onesrf[:, :], rcr[:, :],
                         start=True, stop=True).then_inc(S["bcm"], 1)

    if 3 in phases:
        NT3 = TH // 128
        with (
            nc.sbuf_tensor("oms", [128, 16 * TH], BF16) as oms,
            nc.sbuf_tensor("xrs", [128, 2 * DM], F32) as xrs,
            nc.sbuf_tensor("z1s", [128, DM], F32) as z1s,
            nc.sbuf_tensor("bns", [128, 24], F32) as bns,
            nc.sbuf_tensor("mvs", [128, 2], F32) as mvs,
            nc.sbuf_tensor("l3a", [128, 1], F32) as l3a,
            nc.sbuf_tensor("l3b", [128, 1], F32) as l3b,
            nc.sbuf_tensor("l3c", [128, 1], F32) as l3c,
            nc.sbuf_tensor("l3d", [128, 1], F32) as l3d,
            nc.sbuf_tensor("x1s", [128, 2 * DM], BF16) as x1s,
            nc.sbuf_tensor("x1t", [128, DM], BF16) as x1t,
            nc.psum_tensor("pW0", [128, 512], F32) as pW0,
            nc.psum_tensor("pW1", [128, 512], F32) as pW1,
            nc.psum_tensor("pW2", [128, 512], F32) as pW2,
            nc.psum_tensor("pW3", [128, 512], F32) as pW3,
            nc.psum_tensor("pT0", [128, 128], BF16) as pT0,
            nc.psum_tensor("pT1", [128, 128], BF16) as pT1,
            nc.Block() as block,
        ):
            pW = [pW0, pW1, pW2, pW3]
            pT = [pT0, pT1]

            @block.gpsimd
            def _(g):
                g.wait_ge(S["agc"], 4)
                gsems = [S["gth"], S["gth1"], S["gth2"], S["gth3"]]
                for gi in range(16):
                    if gi >= 4:
                        g.wait_ge(gsems[gi % 4], 16 * (gi // 4))
                    g.indirect_dma_start(
                        oms[:, gi * TH:(gi + 1) * TH], None,
                        agout[:, :],
                        IndirectOffsetOnAxis(ap=oidx[:, gi:gi + 1], axis=0),
                    ).then_inc(gsems[gi % 4], 16)

            @block.sync
            def _(sy):
                for tt in range(NT3):
                    if tt >= 2:
                        sy.wait_ge(S["z1v"], 4 * (tt - 1))
                    if tt >= 1:
                        sy.wait_ge(S["xrl"], 16 * tt)
                    sy.dma_start(xrs[:, (tt % 2) * DM:(tt % 2 + 1) * DM],
                                 xres[tt * 128:(tt + 1) * 128, :]
                                 ).then_inc(S["xrl"], 16)

            @block.tensor
            def _(t):
                t.wait_ge(S["gth"], 16 * 4)
                t.wait_ge(S["gth1"], 16 * 4)
                t.wait_ge(S["gth2"], 16 * 4)
                t.wait_ge(S["gth3"], 16 * 4)
                t.wait_ge(S["wol"], 16)
                for tt in range(NT3):
                    for eb in range(4):
                        if tt >= 1:
                            t.wait_ge(S["z1v"], 4 * (tt - 1) + eb + 1)
                        for fc in range(16):
                            t.matmul(pW[eb][:, :],
                                     oms[:, fc * TH + tt * 128:
                                         fc * TH + (tt + 1) * 128],
                                     wos[:, fc * DM + eb * 512:
                                         fc * DM + (eb + 1) * 512],
                                     start=(fc == 0), stop=(fc == 15)
                                     ).then_inc(S["p3mm"], 1)
                    if tt >= 2:
                        t.wait_ge(S["apv"], tt - 1)
                        for dc in range(16):
                            gtr = 16 * (tt - 2) + dc
                            if gtr >= 2:
                                t.wait_ge(S["trv"], gtr - 1)
                            t.transpose(pT[gtr % 2][:, :],
                                        x1s[:, ((tt - 2) % 2) * DM + dc * 128:
                                            ((tt - 2) % 2) * DM + (dc + 1) * 128],
                                        ident[:, :]).then_inc(S["trm"], 1)
                for tl in (NT3 - 2, NT3 - 1):
                    t.wait_ge(S["apv"], tl + 1)
                    for dc in range(16):
                        gtr = 16 * tl + dc
                        if gtr >= 2:
                            t.wait_ge(S["trv"], gtr - 1)
                        t.transpose(pT[gtr % 2][:, :],
                                    x1s[:, (tl % 2) * DM + dc * 128:
                                        (tl % 2) * DM + (dc + 1) * 128],
                                    ident[:, :]).then_inc(S["trm"], 1)

            @block.vector
            def _(v):
                for tt in range(NT3):
                    v.drain()
                    v.wait_ge(S["xrl"], 16 * (tt + 1))
                    for eb in range(4):
                        v.wait_ge(S["p3mm"], (tt * 4 + eb + 1) * 16)
                        v.tensor_tensor(z1s[:, eb * 512:(eb + 1) * 512],
                                        pW[eb][:, :],
                                        xrs[:, (tt % 2) * DM + eb * 512:
                                            (tt % 2) * DM + (eb + 1) * 512],
                                        ALU.add).then_inc(S["z1v"], 1)
                    v.drain()
                    for eb in range(4):
                        v.bn_stats(bns[:, eb * 6:(eb + 1) * 6],
                                   z1s[:, eb * 512:(eb + 1) * 512])
                    v.drain()
                    v.bn_aggr(mvs[:, :], bns[:, :])
                    v.drain()
                    v.tensor_scalar(l3a[:, :], mvs[:, 1:2], EPS, None,
                                    ALU.add, ALU.bypass).then_inc(S["lnv"], 1)
                    v.wait_ge(S["p3sq"], tt + 1)
                    v.reciprocal(l3c[:, :], l3b[:, :]).then_inc(S["l3u"], 1)
                    v.drain()
                    v.tensor_scalar(l3d[:, :], mvs[:, 0:1], l3c[:, :], -1.0,
                                    ALU.mult, ALU.mult)
                    v.drain()
                    if tt >= 2:
                        v.wait_ge(S["trm"], 16 * (tt - 1))
                    for eb in range(4):
                        v.tensor_scalar(x1s[:, (tt % 2) * DM + eb * 512:
                                            (tt % 2) * DM + (eb + 1) * 512],
                                        z1s[:, eb * 512:(eb + 1) * 512],
                                        l3c[:, :], l3d[:, :],
                                        ALU.mult, ALU.add)
                    v.engine_nop().then_inc(S["apv"], 1)


            @block.scalar
            def _(s):
                def trevac(s, tl):
                    if tl >= 1:
                        s.wait_ge(S["x1st"], 16 * tl)
                    for dc in range(16):
                        gtr = 16 * tl + dc
                        s.wait_ge(S["trm"], gtr + 1)
                        s.copy(x1t[:, dc * 128:(dc + 1) * 128],
                               pT[gtr % 2][:, :]).then_inc(S["trv"], 1)
                    s.wait_ge(S["trv"], 16 * (tl + 1))
                    s.dma_start(
                        _r(x1f[:, tl * 128:(tl + 1) * 128],
                           "(c p) t -> p c t", p=128),
                        _r(x1t[:, :], "p (c t) -> p c t", c=16)
                    ).then_inc(S["x1st"], 16)

                for tt in range(NT3):
                    s.wait_ge(S["lnv"], tt + 1)
                    if tt >= 1:
                        s.wait_ge(S["l3u"], tt)
                    s.sqrt(l3b[:, :], l3a[:, :]).then_inc(S["p3sq"], 1)
                    if tt >= 2:
                        trevac(s, tt - 2)
                trevac(s, NT3 - 2)
                trevac(s, NT3 - 1)
                s.wait_ge(S["x1st"], 16 * NT3)

    if wos_cm is not None:
        wos_cm.__exit__(None, None, None)

    if 4 in phases:
        NHC2 = 22
        NEB = 4

        def p4_dmm_after_chunk4(gw):
            tbw, rem = divmod(gw, NEB * 11)
            ebw, hgw = divmod(rem, 11)
            return tbw * 1536 + ebw * 384 + (hgw + 1) * 32

        def p4_dmm_group_end(tb, eb, tt):
            return tb * 1536 + eb * 384 + 352 + (tt + 1) * 4

        def gu_done(tb, hc2, t2, hsub, part):
            return tb * 2816 + hc2 * 128 + t2 * 64 + hsub * 32 + 16 * part

        with contextlib.ExitStack() as _st:
            _e = _st.enter_context
            identE = _e(nc.sbuf_tensor("identE_s", [128, 4 * 512], BF16))
            x1b = _e(nc.sbuf_tensor("x1b", [128, 16 * FTB], BF16))
            wgs = _e(nc.sbuf_tensor("wgs", [128, 3 * 16 * 256], BF16))
            wgu8 = _e(nc.sbuf_tensor("wgu8", [128, 16 * 256], U8))
            wus = _e(nc.sbuf_tensor("wus", [128, 3 * 16 * 256], BF16))
            wuu8 = _e(nc.sbuf_tensor("wuu8", [128, 16 * 256], U8))
            gsb = _e(nc.sbuf_tensor("gsb", [128, NHC * FTB], BF16))
            gst = _e(nc.sbuf_tensor("gst", [128, 2 * 512], BF16))
            ust = _e(nc.sbuf_tensor("ust", [128, 2 * 512], BF16))
            wds = _e(nc.sbuf_tensor("wds", [128, 2 * 4 * 512], BF16))
            wdu8 = _e(nc.sbuf_tensor("wdu8", [128, 4 * 512], U8))
            dstg = _e(nc.sbuf_tensor("dstg", [128, 2 * 512], F32))
            pP0 = _e(nc.psum_tensor("pP0", [128, 512], F32))
            pP1 = _e(nc.psum_tensor("pP1", [128, 512], F32))
            pP2 = _e(nc.psum_tensor("pP2", [128, 512], F32))
            pP3 = _e(nc.psum_tensor("pP3", [128, 512], F32))
            pP4 = _e(nc.psum_tensor("pP4", [128, 512], F32))
            pP5 = _e(nc.psum_tensor("pP5", [128, 512], F32))
            pP6 = _e(nc.psum_tensor("pP6", [128, 512], F32))
            pP7 = _e(nc.psum_tensor("pP7", [128, 512], F32))
            block = _e(nc.Block())
            True
            pP = [pP0, pP1, pP2, pP3, pP4, pP5, pP6, pP7]
            pGs = [pP0, pP1, pP2]
            pUs = [pP3, pP4, pP5]

            @block.scalar
            def _(s):
                s.dma_start(identE[:, :], identEd[:, :]).then_inc(S["ield"], 16)
                for tb in range(2):
                    if tb >= 1:
                        s.wait_ge(S["gumm"], 2816)
                        s.wait_ge(S["dmm"], 1536)
                        s.wait_ge(S["x1l"], 16)
                    s.dma_start(
                        _r(x1b[:, :], "p (c t) -> p c t", c=16),
                        _r(x1f[:, tb * FTB:(tb + 1) * FTB],
                           "(c p) t -> p c t", p=128)).then_inc(S["x1l"], 16)
                    for hc2 in range(NHC2):
                        for t2 in range(2):
                            for hsub in range(2):
                                gt = tb * 88 + hc2 * 4 + t2 * 2 + hsub
                                hc = hc2 * 2 + hsub
                                s.wait_ge(S["gumm"],
                                          gu_done(tb, hc2, t2, hsub, 1))
                                if gt >= 2:
                                    s.wait_ge(S["gml"], gt - 1)
                                s.activation(gst[:, (gt % 2) * 512:
                                                 (gt % 2 + 1) * 512],
                                             pGs[gt % 3][:, :], AFT.Silu,
                                             bias=0.0, scale=sg[:, hc:hc + 1]
                                             ).then_inc(S["sil"], 1)

            @block.gpsimd
            def _(g):
                for tb in range(2):
                    for hc2 in range(NHC2):
                        gc = tb * NHC2 + hc2
                        if gc >= 1:
                            g.wait_ge(S["deq"], gc)
                            g.wait_ge(S["wldg"], 32 * gc)
                        sl8 = 0
                        g.dma_start(
                            wgu8[:, sl8:sl8 + 16 * 256],
                            wgT[:, hc2 * 16 * 256:(hc2 + 1) * 16 * 256]
                            ).then_inc(S["wldg"], 16)
                        g.dma_start(
                            wuu8[:, sl8:sl8 + 16 * 256],
                            wuT[:, hc2 * 16 * 256:(hc2 + 1) * 16 * 256]
                            ).then_inc(S["wldg"], 16)
                    for eb in range(NEB):
                        for hg4 in range(11):
                            gw = tb * NEB * 11 + eb * 11 + hg4
                            if gw >= 1:
                                g.wait_ge(S["wdc"], gw)
                                g.wait_ge(S["wdl"], 16 * gw)
                            g.dma_start(
                                wdu8[:, 0:2048],
                                wdT[:, (eb * NHC + hg4 * 4) * 512:
                                    (eb * NHC + hg4 * 4 + 4) * 512]
                                ).then_inc(S["wdl"], 16)

            @block.vector
            def _(v):
                def deq_chunk(gc):
                    v.wait_ge(S["wldg"], 32 * (gc + 1))
                    if gc >= 3:
                        v.wait_ge(S["gumm"], (gc - 2) * 128)
                    sl8 = 0
                    slb = (gc % 3) * 16 * 256
                    v.tensor_scalar(wgs[:, slb:slb + 16 * 256],
                                    wgu8[:, sl8:sl8 + 16 * 256],
                                    -128.0, None, ALU.add, ALU.bypass)
                    v.drain()
                    v.tensor_scalar(wus[:, slb:slb + 16 * 256],
                                    wuu8[:, sl8:sl8 + 16 * 256],
                                    -128.0, None, ALU.add, ALU.bypass
                                    ).then_inc(S["deq"], 1)

                for tb in range(2):
                    deq_chunk(tb * NHC2 + 0)
                    deq_chunk(tb * NHC2 + 1)
                    deq_chunk(tb * NHC2 + 2)
                    for hc2 in range(NHC2):
                        gc = tb * NHC2 + hc2
                        for t2 in range(2):
                            for hsub in range(2):
                                gt = tb * 88 + hc2 * 4 + t2 * 2 + hsub
                                hc = hc2 * 2 + hsub
                                v.wait_ge(S["gumm"],
                                          gu_done(tb, hc2, t2, hsub, 2))
                                if gt >= 2:
                                    v.wait_ge(S["gml"], gt - 1)
                                v.tensor_scalar(ust[:, (gt % 2) * 512:
                                                    (gt % 2 + 1) * 512],
                                                pUs[gt % 3][:, :],
                                                su[:, hc:hc + 1], None,
                                                ALU.mult, ALU.bypass)
                                v.wait_ge(S["sil"], gt + 1)
                                v.drain()
                                v.tensor_tensor(
                                    gsb[:, hc * FTB + t2 * 512:
                                        hc * FTB + (t2 + 1) * 512],
                                    gst[:, (gt % 2) * 512:(gt % 2 + 1) * 512],
                                    ust[:, (gt % 2) * 512:(gt % 2 + 1) * 512],
                                    ALU.mult).then_inc(S["gml"], 1)
                        if hc2 + 3 < NHC2:
                            deq_chunk(gc + 3)
                    for eb in range(NEB):
                        for hg4 in range(11):
                            gw = tb * NEB * 11 + eb * 11 + hg4
                            v.wait_ge(S["wdl"], 16 * (gw + 1))
                            if gw >= 2:
                                v.wait_ge(S["dmm"],
                                          p4_dmm_after_chunk4(gw - 2))
                            sw = (gw % 2) * 2048
                            v.tensor_scalar(wds[:, sw:sw + 2048],
                                            wdu8[:, 0:2048],
                                            -128.0, None, ALU.add, ALU.bypass
                                            ).then_inc(S["wdc"], 1)
                            v.drain()
                            for q4 in range(4):
                                v.tensor_tensor(
                                    wds[:, sw + q4 * 512:sw + (q4 + 1) * 512],
                                    wds[:, sw + q4 * 512:sw + (q4 + 1) * 512],
                                    sdbc[:, eb * 512:(eb + 1) * 512],
                                    ALU.mult).then_inc(S["wdq"], 1)
                        for tt in range(8):
                            ge = tb * 32 + eb * 8 + tt
                            v.wait_ge(S["dmm"], p4_dmm_group_end(tb, eb, tt))
                            if ge >= 2:
                                v.wait_ge(S["dst"], 16 * (ge - 1))
                            sidx = ((tb * 8 + tt) * 4 + eb) * 6
                            v.bn_stats(bst5[:, sidx:sidx + 6], pP[tt][:, :])
                            v.drain()
                            v.tensor_copy(dstg[:, (ge % 2) * 512:
                                              (ge % 2 + 1) * 512],
                                          pP[tt][:, :]).then_inc(S["dev"], 1)

            @block.tensor
            def _(t):
                t.wait_ge(S["ield"], 16)
                for tb in range(2):
                    t.wait_ge(S["x1l"], 16 * (tb + 1))
                    if tb >= 1:
                        t.wait_ge(S["dev"], 32 * tb)
                    for hc2 in range(NHC2):
                        gc = tb * NHC2 + hc2
                        t.wait_ge(S["deq"], gc + 1)
                        sl = (gc % 3) * 16 * 256
                        for t2 in range(2):
                            for hsub in range(2):
                                gt = tb * 88 + hc2 * 4 + t2 * 2 + hsub
                                if gt >= 3:
                                    t.wait_ge(S["sil"], gt - 2)
                                    t.wait_ge(S["gml"], gt - 2)
                                for fc in range(16):
                                    t.matmul(pGs[gt % 3][:, :],
                                             wgs[:, sl + fc * 256 + hsub * 128:
                                                 sl + fc * 256 + (hsub + 1) * 128],
                                             x1b[:, fc * FTB + t2 * 512:
                                                 fc * FTB + (t2 + 1) * 512],
                                             start=(fc == 0), stop=(fc == 15)
                                             ).then_inc(S["gumm"], 1)
                                for fc in range(16):
                                    t.matmul(pUs[gt % 3][:, :],
                                             wus[:, sl + fc * 256 + hsub * 128:
                                                 sl + fc * 256 + (hsub + 1) * 128],
                                             x1b[:, fc * FTB + t2 * 512:
                                                 fc * FTB + (t2 + 1) * 512],
                                             start=(fc == 0), stop=(fc == 15)
                                             ).then_inc(S["gumm"], 1)
                    for eb in range(NEB):
                        for hc in range(NHC):
                            hg4, q4 = divmod(hc, 4)
                            gw = tb * NEB * 11 + eb * 11 + hg4
                            t.wait_ge(S["wdq"], 4 * gw + q4 + 1)
                            if hc == 0:
                                geb = tb * NEB + eb
                                if geb >= 1:
                                    t.wait_ge(S["dev"], (geb - 1) * 8 + 8)
                                t.wait_ge(S["gml"], tb * 88 + 88)
                            sw = (gw % 2) * 2048 + q4 * 512
                            for tt in range(8):
                                t.matmul(pP[tt][:, :],
                                         gsb[:, hc * FTB + tt * 128:
                                             hc * FTB + (tt + 1) * 128],
                                         wds[:, sw:sw + 512],
                                         start=(hc == 0), stop=False
                                         ).then_inc(S["dmm"], 1)
                        for tt in range(8):
                            for j in range(4):
                                dc = eb * 4 + j
                                t.matmul(pP[tt][:, :],
                                         x1b[:, dc * FTB + tt * 128:
                                             dc * FTB + (tt + 1) * 128],
                                         identE[:, j * 512:(j + 1) * 512],
                                         start=False, stop=(j == 3)
                                         ).then_inc(S["dmm"], 1)

            @block.sync
            def _(sy):
                for tb in range(2):
                    for eb in range(NEB):
                        for tt in range(8):
                            ge = tb * 32 + eb * 8 + tt
                            sy.wait_ge(S["dev"], ge + 1)
                            if ge >= 1:
                                sy.wait_ge(S["dst"], 16 * ge)
                            sy.dma_start(
                                opre[tb * FTB + tt * 128:
                                     tb * FTB + (tt + 1) * 128,
                                     eb * 512:(eb + 1) * 512],
                                dstg[:, (ge % 2) * 512:(ge % 2 + 1) * 512]
                                ).then_inc(S["dst"], 16)
                sy.wait_ge(S["dst"], 16 * 64)

    if 5 in phases:
        NT5 = TH // 128
        with (
            nc.sbuf_tensor("ops", [128, 2 * DM], F32) as ops,
            nc.sbuf_tensor("bn5", [128, 24], F32) as bn5,
            nc.sbuf_tensor("mv5", [128, 2], F32) as mv5,
            nc.sbuf_tensor("l5a", [128, 1], F32) as l5a,
            nc.sbuf_tensor("l5b", [128, 1], F32) as l5b,
            nc.sbuf_tensor("l5c", [128, 1], F32) as l5c,
            nc.sbuf_tensor("l5d", [128, 1], F32) as l5d,
            nc.sbuf_tensor("o5s", [128, 2 * DM], F32) as o5s,
            nc.Block() as block,
        ):
            @block.sync
            def _(sy):
                for tt in range(NT5):
                    if tt >= 1:
                        sy.wait_ge(S["p5ap"], tt)
                        if tt >= 2:
                            sy.wait_ge(S["p5st"], 16 * (tt - 1))
                        sy.dma_start(out[(tt - 1) * 128: tt * 128, :],
                                     o5s[:, ((tt - 1) % 2) * DM:
                                         ((tt - 1) % 2 + 1) * DM]
                                     ).then_inc(S["p5st"], 16)
                sy.wait_ge(S["p5ap"], NT5)
                sy.wait_ge(S["p5st"], 16 * (NT5 - 1))
                sy.dma_start(out[(NT5 - 1) * 128: NT5 * 128, :],
                             o5s[:, ((NT5 - 1) % 2) * DM:
                                 ((NT5 - 1) % 2 + 1) * DM]
                             ).then_inc(S["p5st"], 16)
                sy.wait_ge(S["p5st"], 16 * NT5)

            @block.vector
            def _(v):
                for tt in range(NT5):
                    v.drain()
                    v.wait_ge(S["p5l"], 16 * (tt + 1))
                    v.bn_aggr(mv5[:, :], bst5[:, tt * 24:(tt + 1) * 24])
                    v.drain()
                    v.tensor_scalar(l5a[:, :], mv5[:, 1:2], EPS, None,
                                    ALU.add, ALU.bypass).then_inc(S["p5bn"], 1)
                    v.wait_ge(S["p5sq"], tt + 1)
                    v.reciprocal(l5c[:, :], l5b[:, :]).then_inc(S["l5u"], 1)
                    v.drain()
                    v.tensor_scalar(l5d[:, :], mv5[:, 0:1], l5c[:, :], -1.0,
                                    ALU.mult, ALU.mult)
                    v.drain()
                    if tt >= 2:
                        v.wait_ge(S["p5st"], 16 * (tt - 1))
                    for eb in range(4):
                        v.tensor_scalar(o5s[:, (tt % 2) * DM + eb * 512:
                                            (tt % 2) * DM + (eb + 1) * 512],
                                        ops[:, (tt % 2) * DM + eb * 512:
                                            (tt % 2) * DM + (eb + 1) * 512],
                                        l5c[:, :], l5d[:, :],
                                        ALU.mult, ALU.add)
                    v.engine_nop().then_inc(S["p5ap"], 1)

            @block.scalar
            def _(s):
                for tt in range(NT5):
                    if tt >= 2:
                        s.wait_ge(S["p5ap"], tt - 1)
                    if tt >= 1:
                        s.wait_ge(S["p5l"], 16 * tt)
                    s.dma_start(ops[:, (tt % 2) * DM:(tt % 2 + 1) * DM],
                                opre[tt * 128:(tt + 1) * 128, :]
                                ).then_inc(S["p5l"], 16)
                    s.wait_ge(S["p5bn"], tt + 1)
                    if tt >= 1:
                        s.wait_ge(S["l5u"], tt)
                    s.sqrt(l5b[:, :], l5a[:, :]).then_inc(S["p5sq"], 1)

    if dbg:
        with nc.Block() as block:
            @block.gpsimd
            def _(g):
                n = 0
                for name, (dt_, srcd) in dbg.items():
                    if n:
                        g.wait_ge(S["dbgs"], n)
                    g.dma_start(dt_[:, :], srcd[:, :]).then_inc(S["dbgs"], 16)
                    n += 16
                g.wait_ge(S["dbgs"], n)

    return nc



def _consts():
    eye = np.eye(128, dtype=np.float32)
    cm = np.asarray(eye - 1.0 / 128.0, np.float32).astype(ml_dtypes.bfloat16)
    idm = eye.astype(ml_dtypes.bfloat16)
    r = np.arange(128)[:, None]
    c = np.arange(QB)[None, :]
    masks = np.concatenate(
        [(c >= r + 128 * j).astype(np.float32) for j in range(4)],
        axis=1).astype(ml_dtypes.bfloat16)
    identE = np.zeros((128, 4 * 512), np.float32)
    for j in range(4):
        identE[np.arange(128), j * 512 + j * 128 + np.arange(128)] = 1.0
    identE = identE.astype(ml_dtypes.bfloat16)
    onescol = np.ones((128, 1), ml_dtypes.bfloat16)
    onesrow = np.ones((1, 128), ml_dtypes.bfloat16)
    return cm, idm, masks, identE, onescol, onesrow


def make_in_maps(inputs):
    x = np.asarray(inputs["L0_attn_hidden_states"], np.float32)
    Wq = np.asarray(inputs["L0_attn_Wq"], np.float32)
    Wk = np.asarray(inputs["L0_attn_Wk"], np.float32)
    Wv = np.asarray(inputs["L0_attn_Wv"], np.float32)
    Wo = np.asarray(inputs["L0_attn_Wo"], np.float32)
    gamma = np.asarray(inputs["L0_attn_gamma"], np.float32)
    beta = np.asarray(inputs["L0_attn_beta"], np.float32)
    wqg = np.asarray(inputs["L0_ffn_wq_gate"], np.int32)
    sgv = np.asarray(inputs["L0_ffn_s_gate"], np.float32)
    wqu = np.asarray(inputs["L0_ffn_wq_up"], np.int32)
    suv = np.asarray(inputs["L0_ffn_s_up"], np.float32)
    wqd = np.asarray(inputs["L0_ffn_wq_down"], np.int32)
    sdv = np.asarray(inputs["L0_ffn_s_down"], np.float32)

    cm, idm, masks, identE, onescol, onesrow = _consts()
    woTh = np.ascontiguousarray(
        Wo.T.reshape(16, 128, DM).transpose(1, 0, 2).reshape(128, 16 * DM))
    wgTh = np.ascontiguousarray(
        wqg.T.reshape(16, 128, 22, 256).transpose(1, 2, 0, 3)
        .reshape(128, 22 * 16 * 256).astype(np.uint8))
    wuTh = np.ascontiguousarray(
        wqu.T.reshape(16, 128, 22, 256).transpose(1, 2, 0, 3)
        .reshape(128, 22 * 16 * 256).astype(np.uint8))
    wdTh = np.ascontiguousarray(
        wqd.T.reshape(NHC, 128, 4, 512).transpose(1, 2, 0, 3)
        .reshape(128, 4 * NHC * 512).astype(np.uint8))
    sg2 = np.ascontiguousarray(sgv.reshape(NHC, 128).T)
    su2 = np.ascontiguousarray(suv.reshape(NHC, 128).T)
    sdr = np.ascontiguousarray(sdv[None, :])

    maps = []
    for c in range(8):
        b, hg = c // 2, c % 2
        p = np.arange(128, dtype=np.uint64)[:, None]
        i = np.arange(16, dtype=np.uint64)[None, :]
        slot = 2 * b + i // 8
        region = hg * 2 + (i % 8) // 4
        oix = (region * 8 * 512 + slot * 512 + ((i % 8) % 4) * 128 + p
               ).astype(np.uint32)
        maps.append({
            "xT": np.ascontiguousarray(
                x[b].T.reshape(16, 128, NTB, TB).transpose(1, 2, 0, 3)
                .reshape(128, NTB * 16 * TB)),
            "xres": np.ascontiguousarray(x[b, hg * TH:(hg + 1) * TH]),
            "wq": np.ascontiguousarray(Wq[hg * EQ:(hg + 1) * EQ]),
            "wk": np.ascontiguousarray(Wk[hg * EQ:(hg + 1) * EQ]),
            "wvT": np.ascontiguousarray(
                Wv[hg * EQ:(hg + 1) * EQ].T.reshape(16, 128, EQ)
                .transpose(1, 0, 2).reshape(128, 16 * EQ)),
            "woT": woTh, "wgT": wgTh, "wuT": wuTh, "wdT": wdTh,
            "sg": sg2, "su": su2, "sdrow": sdr,
            "gam": np.ascontiguousarray(gamma[:, None]),
            "bet": np.ascontiguousarray(beta[:, None]),
            "cmat": cm, "identm": idm, "masks": masks, "identE": identE,
            "onescol": onescol, "onesrow": onesrow,
            "onesrowf": np.ones((1, 128), np.float32), "oidx": oix,
        })
    return maps


def kernel(**inputs):
    maps = make_in_maps(inputs)
    nc = build()
    r = run_bass_kernel_spmd(nc, maps, core_ids=list(range(8)))
    outf = np.empty((4, T, DM), np.float32)
    for c in range(8):
        b, hg = c // 2, c % 2
        outf[b, hg * TH:(hg + 1) * TH] = np.asarray(r.results[c]["out"])
    return outf


# version 21
# speedup vs baseline: 1.0303x; 1.0303x over previous
import contextlib
import numpy as np
import ml_dtypes
import concourse.bass as bass
import concourse.mybir as mybir
from concourse.bass_utils import run_bass_kernel_spmd
from concourse.bass_core_rust import IndirectOffsetOnAxis

F32 = mybir.dt.float32
U8 = mybir.dt.uint8
BF16 = mybir.dt.bfloat16
I32 = mybir.dt.int32
U32 = mybir.dt.uint32
AFT = mybir.ActivationFunctionType
ALU = mybir.AluOpType

T = 4096
DM = 2048
NH = 8
D = 128
EQ = NH * D
HID = 5632
NHC = HID // 128
TH = 2048
TB = 512
NTB = T // TB
QB = 512
NQB = T // QB
FTB = 1024
EPS = 1e-5
DEPS = float(128 * EPS)
C1 = float(EPS / 128.0)
C2 = float(EPS * EPS)


def _r(ap, pat, **kw):
    return ap.rearrange(pat, **kw)


def build(phases=(1, 2, 3, 4, 5), debug_outs=()):
    nc = bass.Bass("TRN2", target_bir_lowering=False)

    xT = nc.dram_tensor("xT", [128, NTB * 16 * TB], F32, kind="ExternalInput")
    xres = nc.dram_tensor("xres", [TH, DM], F32, kind="ExternalInput")
    wqd_ = nc.dram_tensor("wq", [EQ, DM], F32, kind="ExternalInput")
    wkd_ = nc.dram_tensor("wk", [EQ, DM], F32, kind="ExternalInput")
    wvT = nc.dram_tensor("wvT", [128, 16 * EQ], F32, kind="ExternalInput")
    woT = nc.dram_tensor("woT", [128, 16 * DM], F32, kind="ExternalInput")
    wgT = nc.dram_tensor("wgT", [128, 22 * 16 * 256], U8, kind="ExternalInput")
    wuT = nc.dram_tensor("wuT", [128, 22 * 16 * 256], U8, kind="ExternalInput")
    wdT = nc.dram_tensor("wdT", [128, 4 * NHC * 512], U8, kind="ExternalInput")
    sgd = nc.dram_tensor("sg", [128, NHC], F32, kind="ExternalInput")
    sud = nc.dram_tensor("su", [128, NHC], F32, kind="ExternalInput")
    sdrow = nc.dram_tensor("sdrow", [1, DM], F32, kind="ExternalInput")
    gamd = nc.dram_tensor("gam", [128, 1], F32, kind="ExternalInput")
    betd = nc.dram_tensor("bet", [128, 1], F32, kind="ExternalInput")
    cmatd = nc.dram_tensor("cmat", [128, 128], BF16, kind="ExternalInput")
    identd = nc.dram_tensor("identm", [128, 128], BF16, kind="ExternalInput")
    masksd = nc.dram_tensor("masks", [128, 4 * QB], BF16, kind="ExternalInput")
    identEd = nc.dram_tensor("identE", [128, 4 * 512], BF16, kind="ExternalInput")
    onescd = nc.dram_tensor("onescol", [128, 1], BF16, kind="ExternalInput")
    onesrd = nc.dram_tensor("onesrow", [1, 128], BF16, kind="ExternalInput")
    onesrfd = nc.dram_tensor("onesrowf", [1, 128], F32, kind="ExternalInput")
    oidxd = nc.dram_tensor("oidx", [128, 16], U32, kind="ExternalInput")
    out = nc.dram_tensor("out", [TH, DM], F32, kind="ExternalOutput")

    qs = nc.dram_tensor("qs", [EQ, T], BF16)
    ks = nc.dram_tensor("ks", [EQ, T], BF16)
    vs = nc.dram_tensor("vs", [T, EQ], BF16)
    vqh = nc.dram_tensor("vqh", [NH, T], F32)
    agin = nc.dram_tensor("agin", [2 * EQ, TH], BF16)
    agout = nc.dram_tensor("agout", [8 * 2 * EQ, TH], BF16)
    x1f = nc.dram_tensor("x1f", [DM, TH], BF16)
    opre = nc.dram_tensor("opre", [TH, DM], F32)

    dbg = {}
    dbg_specs = {"d_qs": (qs, BF16), "d_ks": (ks, BF16), "d_vs": (vs, BF16),
                 "d_vqh": (vqh, F32), "d_agin": (agin, BF16),
                 "d_agout": (agout, BF16), "d_x1f": (x1f, BF16),
                 "d_opre": (opre, F32)}
    for name in debug_outs:
        src, dt = dbg_specs[name]
        dbg[name] = (nc.dram_tensor(name, list(src.shape), dt,
                                    kind="ExternalOutput"), src)

    keep = []

    def sb(name, shape, dtype):
        h = nc.alloc_sbuf_tensor(name, shape, dtype)
        keep.append(h)
        return h

    SEM_NAMES = [
        "c_ld", "c_mm", "c_ev", "wld", "wsc", "wmm", "wev",
        "xld", "p1mm", "sq2", "vev", "vqm", "vqe", "ksm", "rsa", "rsq", "rsd",
        "spq", "spk", "spv", "spvq",
        "kvld", "qld", "vqld", "p2ms", "sqa", "mka", "p2oz", "zrv", "denv",
        "bcm", "recv", "oev", "ost",
        "agc", "gth", "wol", "xrl", "p3mm", "z1v", "lnv", "p3sq", "apv",
        "trm", "trv", "x1st",
        "x1l", "wldg", "deq", "gumm", "sil", "gml",
        "wdl", "wdq", "dmm", "dev", "dst",
        "p5l", "p5bn", "p5sq", "p5ap", "p5st", "l3u", "l5u", "sqv", "kvv", "wdc", "mld", "ield", "gth1", "gth2", "gth3", "rcs",
        "dbgs",
    ]
    S = {n: nc.alloc_semaphore(name=n) for n in SEM_NAMES}

    cmat = sb("cmat_s", [128, 128], BF16)
    ident = sb("ident_s", [128, 128], BF16)
    onesc = sb("onesc_s", [128, 1], BF16)
    onesr = sb("onesr_s", [1, 128], BF16)
    onesrf = sb("onesrf_s", [1, 128], F32)
    oidx = sb("oidx_s", [128, 16], U32)
    gam = sb("gam_s", [128, 1], F32)
    betbf = sb("betbf_s", [128, 1], BF16)
    cbeta = sb("cbeta_s", [128, 1], F32)
    sg = sb("sg_s", [128, NHC], F32)
    su = sb("su_s", [128, NHC], F32)
    sdbc = sb("sdbc_s", [128, DM], BF16)
    rsk = sb("rsk_s", [128, NH * 32], F32)
    bst5 = sb("bst5_s", [128, 64 * 6], F32)

    with (
        nc.sbuf_tensor("wtq", [128, 16 * EQ], BF16) as wtq,
        nc.sbuf_tensor("wtk", [128, 16 * EQ], BF16) as wtk,
        nc.sbuf_tensor("wvs", [128, 16 * EQ], BF16) as wvs,
    ):
        with (
            nc.sbuf_tensor("sdbf_s", [1, DM], BF16) as sdbf,
            nc.sbuf_tensor("wraw", [128, DM], BF16) as wraw,
            nc.sbuf_tensor("wgam", [128, DM], BF16) as wgam,
            nc.psum_tensor("p0a", [128, 128], F32) as p0a,
            nc.psum_tensor("p0b", [128, 128], F32) as p0b,
            nc.psum_tensor("p0c", [128, 512], F32) as p0c,
            nc.Block() as block,
        ):
            @block.gpsimd
            def _(g):
                nld = 0
                for dst, srcd in [(cmat, cmatd), (ident, identd),
                                  (onesc, onescd),
                                  (onesr, onesrd), (oidx, oidxd), (gam, gamd),
                                  (sg, sgd), (su, sud), (betbf, betd),
                                  (sdbf, sdrow), (onesrf, onesrfd)]:
                    if nld:
                        g.wait_ge(S["c_ld"], nld)
                    g.dma_start(dst[:, :], srcd[:, :]).then_inc(S["c_ld"], 16)
                    nld += 16
                g.wait_ge(S["c_ld"], nld)
                g.dma_start(wvs[:, :], wvT[:, :]).then_inc(S["c_ld"], 16)
                for k in range(16):
                    wsrc = wqd_ if k < 8 else wkd_
                    h = k % 8
                    if k >= 1:
                        g.wait_ge(S["wsc"], k)
                        g.wait_ge(S["wld"], 16 * k)
                    g.dma_start(wraw[:, :], wsrc[h * 128:(h + 1) * 128, :]
                                ).then_inc(S["wld"], 16)

            @block.vector
            def _(v):
                v.wait_ge(S["c_mm"], 1)
                v.tensor_copy(cbeta[:, :], p0a[:, 0:1])
                v.engine_nop().then_inc(S["c_ev"], 1)
                for ebk in range(4):
                    v.wait_ge(S["c_mm"], 2 + ebk)
                    v.tensor_copy(sdbc[:, ebk * 512:(ebk + 1) * 512], p0c[:, :]
                                  ).then_inc(S["c_ev"], 1)
                for k in range(16):
                    v.wait_ge(S["wld"], 16 * (k + 1))
                    if k >= 1:
                        v.wait_ge(S["wmm"], 16 * k)
                    v.tensor_scalar(wgam[:, :], wraw[:, :], gam[:, :], None,
                                    ALU.mult, ALU.bypass).then_inc(S["wsc"], 1)
                    wt = wtq if k < 8 else wtk
                    h = k % 8
                    for fc in range(16):
                        v.wait_ge(S["wmm"], 16 * k + fc + 1)
                        bank = p0a if fc % 2 == 0 else p0b
                        v.tensor_copy(
                            wt[:, fc * EQ + h * 128: fc * EQ + (h + 1) * 128],
                            bank[:, :]).then_inc(S["wev"], 1)

            @block.tensor
            def _(t):
                t.wait_ge(S["c_ld"], 9 * 16)
                t.matmul(p0a[:, 0:1], cmat[:, :], betbf[:, :],
                         start=True, stop=True).then_inc(S["c_mm"], 1)
                t.wait_ge(S["c_ld"], 10 * 16)
                for ebk in range(4):
                    if ebk >= 1:
                        t.wait_ge(S["c_ev"], 1 + ebk)
                    t.matmul(p0c[:, :], onesr[:, :],
                             sdbf[:, ebk * 512:(ebk + 1) * 512],
                             start=True, stop=True).then_inc(S["c_mm"], 1)
                t.wait_ge(S["c_ev"], 5)
                for k in range(16):
                    t.wait_ge(S["wsc"], k + 1)
                    for fc in range(16):
                        gidx = 16 * k + fc
                        if gidx >= 2:
                            t.wait_ge(S["wev"], gidx - 1)
                        bank = p0a if fc % 2 == 0 else p0b
                        t.matmul(bank[:, :], wgam[:, fc * 128:(fc + 1) * 128],
                                 cmat[:, :], start=True, stop=True
                                 ).then_inc(S["wmm"], 1)

        if 1 in phases:
            with (
                nc.sbuf_tensor("xTs", [128, 2 * 16 * TB], BF16) as xTs,
                nc.sbuf_tensor("qcs", [128, 8 * TB], BF16) as qcs,
                nc.sbuf_tensor("kcs", [128, 8 * TB], BF16) as kcs,
                nc.sbuf_tensor("vvs", [128, 8 * TB], BF16) as vvs,
                nc.sbuf_tensor("q2s", [128, 16 * TB], BF16) as q2s,
                nc.sbuf_tensor("k2s", [128, 16 * TB], BF16) as k2s,
                nc.sbuf_tensor("vqr", [1, 2 * TB], F32) as vqr,
                nc.sbuf_tensor("rs1", [128, 32], F32) as rs1,
                nc.sbuf_tensor("rs2", [128, 32], F32) as rs2,
                nc.psum_tensor("pA", [128, TB], F32) as pA,
                nc.psum_tensor("pB", [128, TB], F32) as pB,
                nc.psum_tensor("pC", [128, TB], F32) as pC,
                nc.psum_tensor("pD", [128, TB], F32) as pD,
                nc.psum_tensor("pKS", [128, 32], F32) as pKS,
                nc.psum_tensor("pVQ0", [1, TB], F32) as pVQ0,
                nc.psum_tensor("pVQ1", [1, TB], F32) as pVQ1,
                nc.Block() as block,
            ):
                banks = [pA, pB, pC, pD]
                pVQs = [pVQ0, pVQ1]

                def xv(buf, fc, lo, hi):
                    base = buf * 16 * TB + fc * TB
                    return xTs[:, base + lo: base + hi]

                @block.gpsimd
                def _(g):
                    for tb in range(NTB):
                        if tb >= 2:
                            g.wait_ge(S["p1mm"], 384 * (tb - 1))
                        if tb >= 1:
                            g.wait_ge(S["xld"], 16 * tb)
                        g.dma_start(
                            xTs[:, (tb % 2) * 16 * TB:(tb % 2 + 1) * 16 * TB],
                            xT[:, tb * 16 * TB:(tb + 1) * 16 * TB]
                            ).then_inc(S["xld"], 16)

                @block.tensor
                def _(t):
                    t.wait_ge(S["c_ld"], 12 * 16)
                    for tb in range(NTB):
                        t.wait_ge(S["xld"], 16 * (tb + 1))
                        buf = tb % 2
                        for i in range(24):
                            gidx = tb * 24 + i
                            bank = banks[gidx % 4]
                            if gidx >= 4:
                                t.wait_ge(S["vev"], gidx - 3)
                            if i < 8:
                                for fc in range(16):
                                    t.matmul(bank[:, :],
                                             wtq[:, fc * EQ + i * 128:
                                                 fc * EQ + (i + 1) * 128],
                                             xv(buf, fc, 0, TB),
                                             start=(fc == 0), stop=(fc == 15)
                                             ).then_inc(S["p1mm"], 1)
                            elif i < 16:
                                e = i - 8
                                for fc in range(16):
                                    t.matmul(bank[:, :],
                                             wtk[:, fc * EQ + e * 128:
                                                 fc * EQ + (e + 1) * 128],
                                             xv(buf, fc, 0, TB),
                                             start=(fc == 0), stop=(fc == 15)
                                             ).then_inc(S["p1mm"], 1)
                            else:
                                tt = (i - 16) // 2
                                eb = (i - 16) % 2
                                for fc in range(16):
                                    t.matmul(bank[:, :],
                                             xv(buf, fc, tt * 128, (tt + 1) * 128),
                                             wvs[:, fc * EQ + eb * 512:
                                                 fc * EQ + (eb + 1) * 512],
                                             start=(fc == 0), stop=(fc == 15)
                                             ).then_inc(S["p1mm"], 1)
                        for h in range(NH):
                            t.wait_ge(S["sq2"], 16 * tb + h + 1)
                            if tb * 8 + h >= 2:
                                t.wait_ge(S["vqe"], tb * 8 + h - 1)
                            t.matmul(pVQs[(tb * 8 + h) % 2][:, :], onesc[:, :],
                                     q2s[:, (tb % 2) * 8 * TB + h * TB:
                                         (tb % 2) * 8 * TB + (h + 1) * TB],
                                     start=True, stop=True).then_inc(S["vqm"], 1)
                        if tb >= 1:
                            t.wait_ge(S["rsd"], tb)
                        for h in range(NH):
                            t.wait_ge(S["sq2"], 16 * tb + 8 + h + 1)
                            for tt in range(4):
                                col = h * 4 + tt
                                t.matmul(pKS[:, col:col + 1],
                                         k2s[:, (tb % 2) * 8 * TB + h * TB + tt * 128:
                                             (tb % 2) * 8 * TB + h * TB + (tt + 1) * 128],
                                         onesc[:, :], start=True, stop=True
                                         ).then_inc(S["ksm"], 1)

                @block.scalar
                def _(s):
                    for tb in range(NTB):
                        for i in range(16):
                            gidx = tb * 24 + i
                            h = i % 8
                            s.wait_ge(S["p1mm"], gidx * 16 + 16)
                            if tb >= 2:
                                if i < 8:
                                    s.wait_ge(S["vqm"], 8 * (tb - 2) + h + 1)
                                else:
                                    s.wait_ge(S["ksm"], 32 * (tb - 2) + 4 * (h + 1))
                            bank = banks[gidx % 4]
                            st = q2s if i < 8 else k2s
                            so = (tb % 2) * 8 * TB
                            s.activation(st[:, so + h * TB:so + (h + 1) * TB],
                                         bank[:, :],
                                         AFT.Square, bias=cbeta[:, :], scale=1.0
                                         ).then_inc(S["sq2"], 1)
                        for h in range(NH):
                            gq = tb * 8 + h
                            s.wait_ge(S["vqe"], gq + 1)
                            if gq >= 1:
                                s.wait_ge(S["spvq"], 16 * gq)
                            s.dma_start(vqh[h:h + 1, tb * TB:(tb + 1) * TB],
                                        vqr[:, (gq % 2) * TB:(gq % 2 + 1) * TB]
                                        ).then_inc(S["spvq"], 16)
                        s.wait_ge(S["rsa"], tb + 1)
                        if tb >= 1:
                            s.wait_ge(S["rsd"], tb)
                        s.sqrt(rs2[:, :], rs1[:, :]).then_inc(S["rsq"], 1)

                @block.vector
                def _(v):
                    for tb in range(NTB):
                        for i in range(24):
                            gidx = tb * 24 + i
                            bank = banks[gidx % 4]
                            if i < 16:
                                h = i % 8
                                v.wait_ge(S["sq2"], 16 * tb + i + 1)
                                if tb >= 1:
                                    sp = S["spq"] if i < 8 else S["spk"]
                                    v.wait_ge(sp, 16 * (8 * (tb - 1) + h + 1))
                                st = qcs if i < 8 else kcs
                                v.tensor_scalar(st[:, h * TB:(h + 1) * TB],
                                                bank[:, :], cbeta[:, :], None,
                                                ALU.add, ALU.bypass
                                                ).then_inc(S["vev"], 1)
                            else:
                                j = i - 16
                                v.wait_ge(S["p1mm"], gidx * 16 + 16)
                                if tb >= 1:
                                    v.wait_ge(S["spv"],
                                              16 * (8 * (tb - 1) + j + 1))
                                v.tensor_copy(vvs[:, j * TB:(j + 1) * TB],
                                              bank[:, :]).then_inc(S["vev"], 1)
                        for h in range(NH):
                            gq = tb * 8 + h
                            v.wait_ge(S["vqm"], gq + 1)
                            if gq >= 2:
                                v.wait_ge(S["spvq"], 16 * (gq - 1))
                            v.tensor_copy(vqr[:, (gq % 2) * TB:(gq % 2 + 1) * TB],
                                          pVQs[gq % 2][:, :]).then_inc(S["vqe"], 1)
                        v.wait_ge(S["ksm"], 32 * (tb + 1))
                        v.tensor_scalar(rs1[:, :], pKS[:, :], DEPS, None,
                                        ALU.add, ALU.bypass).then_inc(S["rsa"], 1)
                        v.wait_ge(S["rsq"], tb + 1)
                        v.reciprocal(
                            _r(rsk[:, :], "p (h k) -> p h k", h=NH)
                            [:, :, tb * 4:(tb + 1) * 4],
                            _r(rs2[:, :], "p (h k) -> p h k", h=NH)
                        ).then_inc(S["rsd"], 1)

                @block.sync
                def _(sy):
                    for tb in range(NTB):
                        for i in range(24):
                            sy.wait_ge(S["vev"], tb * 24 + i + 1)
                            if i < 8:
                                if tb * 8 + i >= 1:
                                    sy.wait_ge(S["spq"], 16 * (tb * 8 + i))
                                sy.dma_start(qs[i * 128:(i + 1) * 128,
                                                tb * TB:(tb + 1) * TB],
                                             qcs[:, i * TB:(i + 1) * TB]
                                             ).then_inc(S["spq"], 16)
                            elif i < 16:
                                h = i - 8
                                if tb * 8 + h >= 1:
                                    sy.wait_ge(S["spk"], 16 * (tb * 8 + h))
                                sy.dma_start(ks[h * 128:(h + 1) * 128,
                                                tb * TB:(tb + 1) * TB],
                                             kcs[:, h * TB:(h + 1) * TB]
                                             ).then_inc(S["spk"], 16)
                            else:
                                j = i - 16
                                tt, eb = j // 2, j % 2
                                if tb * 8 + j >= 1:
                                    sy.wait_ge(S["spv"], 16 * (tb * 8 + j))
                                sy.dma_start(vs[tb * TB + tt * 128:
                                                tb * TB + (tt + 1) * 128,
                                                eb * 512:(eb + 1) * 512],
                                             vvs[:, j * TB:(j + 1) * TB]
                                             ).then_inc(S["spv"], 16)
                    sy.wait_ge(S["spq"], 16 * 64)
                    sy.wait_ge(S["spk"], 16 * 64)
                    sy.wait_ge(S["spv"], 16 * 64)
                    sy.wait_ge(S["spvq"], 16 * 64)

    wos_cm = (nc.sbuf_tensor("wos", [128, 16 * DM], BF16)
              if (2 in phases or 3 in phases) else None)
    wos = wos_cm.__enter__() if wos_cm is not None else None
    if 2 in phases:
        seq = [(h, half * 4 + j) for half in range(2)
               for h in range(NH) for j in range(4)]
        nkt = {qb: 4 * (qb + 1) for qb in range(NQB)}
        ga_base = {}
        oz_end = {}
        run = 0
        ozr = 0
        for (h, qb) in seq:
            ga_base[(h, qb)] = run
            run += nkt[qb]
            ozr += 2 * nkt[qb]
            oz_end[(h, qb)] = ozr
        mk_idx = {}
        mrun = 0
        for (h, qb) in seq:
            for kt in range(nkt[qb] - 4, nkt[qb]):
                mk_idx[(h, qb, kt)] = mrun
                mrun += 1

        def is_masked(qb, kt):
            return kt >= nkt[qb] - 4

        def sq_on_vector(kt):
            return kt % 3 == 2 and kt > 5

        cntS, cntV = {}, {}
        _cs = _cv = 0
        for (_h, _qb) in seq:
            for _kt in range(nkt[_qb]):
                if sq_on_vector(_kt):
                    _cv += 1
                else:
                    _cs += 1
                _ga = ga_base[(_h, _qb)] + _kt
                cntS[_ga], cntV[_ga] = _cs, _cv

        sq_owner_vec = {}
        for (_h, _qb) in seq:
            for _kt in range(nkt[_qb]):
                sq_owner_vec[ga_base[(_h, _qb)] + _kt] = sq_on_vector(_kt)

        def wait_sq(eng, ga):
            if sq_owner_vec[ga]:
                eng.wait_ge(S["sqv"], cntV[ga])
            else:
                eng.wait_ge(S["sqa"], cntS[ga])

        with contextlib.ExitStack() as _st:
            _e = _st.enter_context
            masks = _e(nc.sbuf_tensor("masks_s", [128, 4 * QB], BF16))
            khs = _e(nc.sbuf_tensor("khs", [128, 2 * T], BF16))
            vhs = _e(nc.sbuf_tensor("vhs", [128, 2 * 32 * 128], BF16))
            qbs = _e(nc.sbuf_tensor("qbs", [128, 4 * QB], BF16))
            a_s = _e(nc.sbuf_tensor("a_s", [128, 3 * QB], BF16))
            ams = _e(nc.sbuf_tensor("ams", [128, 3 * QB], BF16))
            vqhd = _e(nc.sbuf_tensor("vqhd", [1, T], F32))
            zrow = _e(nc.sbuf_tensor("zrow", [1, QB], F32))
            lnr = _e(nc.sbuf_tensor("lnr", [1, QB], F32))
            rcr = _e(nc.sbuf_tensor("rcr", [1, QB], F32))
            den1 = _e(nc.sbuf_tensor("den1", [1, QB], F32))
            den2 = _e(nc.sbuf_tensor("den2", [1, QB], F32))
            recb = _e(nc.sbuf_tensor("recb", [128, QB], F32))
            vsq = _e(nc.sbuf_tensor("vsq", [128, QB], BF16))
            obs = _e(nc.sbuf_tensor("obs", [128, 2 * QB], BF16))
            pS0 = _e(nc.psum_tensor("pS0", [128, QB], F32))
            pS1 = _e(nc.psum_tensor("pS1", [128, QB], F32))
            pS2 = _e(nc.psum_tensor("pS2", [128, QB], F32))
            pO0 = _e(nc.psum_tensor("pO0", [128, QB], F32))
            pO1 = _e(nc.psum_tensor("pO1", [128, QB], F32))
            pZ0 = _e(nc.psum_tensor("pZ0", [1, QB], F32))
            pZ1 = _e(nc.psum_tensor("pZ1", [1, QB], F32))
            pBC = _e(nc.psum_tensor("pBC", [128, QB], F32))
            block = _e(nc.Block())
            True
            pS = [pS0, pS1, pS2]
            pOs = [pO0, pO1]
            pZs = [pZ0, pZ1]

            @block.gpsimd
            def _(g):
                if 3 not in phases:
                    return
                g.dma_start(wos[:, :], woT[:, :]).then_inc(S["wol"], 16)
                for q4 in range(4):
                    g.wait_ge(S["ost"], 16 * 16 * (q4 + 1))
                    g.collective_compute(
                        "AllGather", ALU.bypass,
                        replica_groups=[[0, 1, 2, 3, 4, 5, 6, 7]],
                        ins=[agin[q4 * 512:(q4 + 1) * 512, :].opt()],
                        outs=[agout[q4 * 8 * 512:(q4 + 1) * 8 * 512, :].opt()],
                    ).then_inc(S["agc"], 1)

            @block.sync
            def _(sy):
                sy.dma_start(masks[:, :], masksd[:, :]).then_inc(S["mld"], 16)
                def qload(gl):
                    hp, qp = seq[gl]
                    if gl >= 4:
                        hx, qx = seq[gl - 4]
                        sy.wait_ge(S["p2ms"], ga_base[(hx, qx)] + nkt[qx])
                    if gl >= 1:
                        sy.wait_ge(S["qld"], 16 * gl)
                    sy.dma_start(qbs[:, (gl % 4) * QB:(gl % 4 + 1) * QB],
                                 qs[hp * 128:(hp + 1) * 128,
                                    qp * QB:(qp + 1) * QB]
                                 ).then_inc(S["qld"], 16)

                qload(0)
                qload(1)
                for gq in range(64):
                    if gq + 2 < 64:
                        qload(gq + 2)
                    if gq >= 1:
                        sy.wait_ge(S["oev"], gq)
                        if gq >= 2:
                            sy.wait_ge(S["ost"], 16 * (gq - 1))
                        hp, qp = seq[gq - 1]
                        hfp, qq = qp // 4, qp % 4
                        sy.dma_start(
                            agin[hfp * EQ + hp * 128:
                                 hfp * EQ + (hp + 1) * 128,
                                 qq * QB:(qq + 1) * QB],
                            obs[:, ((gq - 1) % 2) * QB:((gq - 1) % 2 + 1) * QB]
                            ).then_inc(S["ost"], 16)
                sy.wait_ge(S["oev"], 64)
                sy.wait_ge(S["ost"], 16 * 63)
                sy.dma_start(agin[EQ + 7 * 128: EQ + 8 * 128, 3 * QB:4 * QB],
                             obs[:, (63 % 2) * QB:(63 % 2 + 1) * QB]
                             ).then_inc(S["ost"], 16)
                sy.wait_ge(S["ost"], 16 * 64)

            def kv_load(s, grp):
                h = grp % 8
                if grp >= 2:
                    hp, qp = seq[(grp - 1) * 4 - 1]
                    s.wait_ge(S["p2oz"], oz_end[(hp, qp)])
                if grp >= 1:
                    s.wait_ge(S["kvld"], 16 * grp)
                    s.wait_ge(S["kvv"], 16 * grp)
                kb = (grp % 2) * T
                vb = (grp % 2) * 32 * 128
                s.dma_start(khs[:, kb:kb + T], ks[h * 128:(h + 1) * 128, :]
                            ).then_inc(S["kvld"], 16)
                s.dma_start(_r(vhs[:, vb:vb + 32 * 128],
                               "p (k d) -> p k d", k=32),
                            _r(vs[:, :], "(k p) e -> p k e", p=128)
                            [:, :, h * 128:(h + 1) * 128]
                            ).then_inc(S["kvv"], 16)

            @block.scalar
            def _(s):
                kv_load(s, 0)
                for grp in range(16):
                    half, h = grp // 8, grp % 8
                    if grp >= 1:
                        s.wait_ge(S["denv"], 4 * grp)
                        s.wait_ge(S["vqld"], 16 * grp)
                    s.dma_start(vqhd[:, :], vqh[h:h + 1, :]
                                ).then_inc(S["vqld"], 16)
                    for j in range(4):
                        gq = grp * 4 + j
                        if j == 1 and grp + 1 < 16:
                            kv_load(s, grp + 1)
                        qb = half * 4 + j
                        for kt in range(nkt[qb]):
                            if sq_on_vector(kt):
                                continue
                            ga = ga_base[(h, qb)] + kt
                            s.wait_ge(S["p2ms"], ga + 1)
                            if ga >= 3:
                                s.wait_ge(S["p2oz"], 2 * (ga - 2))
                            s.activation(a_s[:, (ga % 3) * QB:(ga % 3 + 1) * QB],
                                         pS[ga % 3][:, :], AFT.Square,
                                         bias=0.0,
                                         scale=rsk[:, h * 32 + kt:
                                                   h * 32 + kt + 1]
                                         ).then_inc(S["sqa"], 1)
                            if kt == 1 and gq >= 1:
                                s.wait_ge(S["denv"], gq)
                                if gq >= 2:
                                    s.wait_ge(S["bcm"], gq - 1)
                                s.activation(lnr[:, :], den2[:, :], AFT.Ln,
                                             bias=0.0, scale=1.0)
                                s.drain()
                                s.activation(rcr[:, :], lnr[:, :], AFT.Exp,
                                             bias=0.0, scale=-1.0
                                             ).then_inc(S["rcs"], 1)
                s.wait_ge(S["denv"], 64)
                s.wait_ge(S["bcm"], 63)
                s.activation(lnr[:, :], den2[:, :], AFT.Ln, bias=0.0, scale=1.0)
                s.drain()
                s.activation(rcr[:, :], lnr[:, :], AFT.Exp, bias=0.0,
                             scale=-1.0).then_inc(S["rcs"], 1)

            @block.vector
            def _(v):
                v.wait_ge(S["mld"], 16)
                for grp in range(16):
                    half, h = grp // 8, grp % 8
                    for j in range(4):
                        qb = half * 4 + j
                        gq = grp * 4 + j
                        v.drain()
                        for kt in range(nkt[qb]):
                            ga = ga_base[(h, qb)] + kt
                            if sq_on_vector(kt):
                                v.wait_ge(S["p2ms"], ga + 1)
                                if ga >= 3:
                                    v.wait_ge(S["p2oz"], 2 * (ga - 2))
                                v.tensor_scalar(vsq[:, :], pS[ga % 3][:, :],
                                                rsk[:, h * 32 + kt:
                                                    h * 32 + kt + 1],
                                                None, ALU.mult, ALU.bypass)
                                v.drain()
                                v.tensor_tensor(
                                    a_s[:, (ga % 3) * QB:(ga % 3 + 1) * QB],
                                    vsq[:, :], vsq[:, :], ALU.mult
                                    ).then_inc(S["sqv"], 1)
                                v.drain()
                            if not is_masked(qb, kt):
                                continue
                            jm = kt - (nkt[qb] - 4)
                            if not sq_on_vector(kt):
                                wait_sq(v, ga)
                            v.tensor_tensor(
                                ams[:, (ga % 3) * QB:(ga % 3 + 1) * QB],
                                a_s[:, (ga % 3) * QB:(ga % 3 + 1) * QB],
                                masks[:, jm * QB:(jm + 1) * QB],
                                ALU.mult).then_inc(S["mka"], 1)
                        if gq >= 1:
                            gp = gq - 1
                            gpg, gpj = divmod(gp, 4)
                            v.wait_ge(S["p2oz"],
                                      oz_end[(gpg % 8, (gpg // 8) * 4 + gpj)])
                            v.wait_ge(S["bcm"], gp + 1)
                            v.tensor_copy(recb[:, :], pBC[:, :]
                                          ).then_inc(S["recv"], 1)
                            v.drain()
                            if gp >= 2:
                                v.wait_ge(S["ost"], 16 * (gp - 1))
                            v.tensor_tensor(obs[:, (gp % 2) * QB:
                                                (gp % 2 + 1) * QB],
                                            pOs[gp % 2][:, :], recb[:, :],
                                            ALU.mult).then_inc(S["oev"], 1)
                        v.wait_ge(S["p2oz"], oz_end[(h, qb)])
                        v.tensor_copy(zrow[:, :], pZs[gq % 2][:, :]
                                      ).then_inc(S["zrv"], 1)
                        if gq >= 1:
                            v.wait_ge(S["rcs"], gq)
                        v.wait_ge(S["vqld"], 16 * (grp + 1))
                        v.tensor_scalar(den1[:, :],
                                        vqhd[:, qb * QB:(qb + 1) * QB],
                                        C1, C2, ALU.mult, ALU.add)
                        v.drain()
                        v.tensor_tensor(den2[:, :], den1[:, :], zrow[:, :],
                                        ALU.add).then_inc(S["denv"], 1)
                gp = 63
                v.drain()
                v.wait_ge(S["bcm"], gp + 1)
                v.tensor_copy(recb[:, :], pBC[:, :]).then_inc(S["recv"], 1)
                v.drain()
                v.wait_ge(S["ost"], 16 * (gp - 1))
                v.tensor_tensor(obs[:, (gp % 2) * QB:(gp % 2 + 1) * QB],
                                pOs[gp % 2][:, :], recb[:, :],
                                ALU.mult).then_inc(S["oev"], 1)

            @block.tensor
            def _(t):
                pend = []

                def mmoz(grp, h, qb, gq, kt):
                    ga = ga_base[(h, qb)] + kt
                    n = nkt[qb]
                    wait_sq(t, ga)
                    if is_masked(qb, kt):
                        t.wait_ge(S["mka"], mk_idx[(h, qb, kt)] + 1)
                        asrc = ams
                    else:
                        asrc = a_s
                    if kt == 0 and gq >= 2:
                        t.wait_ge(S["oev"], gq - 1)
                        t.wait_ge(S["zrv"], gq - 1)
                    t.matmul(pOs[gq % 2][:, :],
                             vhs[:, (grp % 2) * 32 * 128 + kt * 128:
                                 (grp % 2) * 32 * 128 + (kt + 1) * 128],
                             asrc[:, (ga % 3) * QB:(ga % 3 + 1) * QB],
                             start=(kt == 0), stop=(kt == n - 1)
                             ).then_inc(S["p2oz"], 1)
                    t.matmul(pZs[gq % 2][:, :], onesc[:, :],
                             asrc[:, (ga % 3) * QB:(ga % 3 + 1) * QB],
                             start=(kt == 0), stop=(kt == n - 1)
                             ).then_inc(S["p2oz"], 1)

                for grp in range(16):
                    half, h = grp // 8, grp % 8
                    t.wait_ge(S["kvld"], 16 * (grp + 1))
                    t.wait_ge(S["kvv"], 16 * (grp + 1))
                    for j in range(4):
                        qb = half * 4 + j
                        gq = grp * 4 + j
                        t.wait_ge(S["qld"], 16 * (gq + 1))
                        n = nkt[qb]

                        def bc(gq=gq):
                            t.wait_ge(S["rcs"], gq)
                            if gq >= 2:
                                t.wait_ge(S["recv"], gq - 1)
                            t.matmul(pBC[:, :], onesrf[:, :], rcr[:, :],
                                     start=True, stop=True
                                     ).then_inc(S["bcm"], 1)

                        for step in range(n):
                            if step == 4 and gq >= 1:
                                bc()
                            kt = step
                            ga = ga_base[(h, qb)] + kt
                            if ga >= 3:
                                wait_sq(t, ga - 3)
                            t.matmul(pS[ga % 3][:, :],
                                     khs[:, (grp % 2) * T + kt * 128:
                                         (grp % 2) * T + (kt + 1) * 128],
                                     qbs[:, (gq % 4) * QB:(gq % 4 + 1) * QB],
                                     start=True, stop=True
                                     ).then_inc(S["p2ms"], 1)
                            if step < 2 and pend:
                                pend.pop(0)()
                            if step >= 2:
                                mmoz(grp, h, qb, gq, step - 2)
                        if n <= 4 and gq >= 1:
                            bc()
                        pend.append(
                            (lambda a, b, c, d, e:
                             lambda: mmoz(a, b, c, d, e))(grp, h, qb, gq, n - 2))
                        pend.append(
                            (lambda a, b, c, d, e:
                             lambda: mmoz(a, b, c, d, e))(grp, h, qb, gq, n - 1))
                for f in pend:
                    f()
                gq = 64
                t.wait_ge(S["rcs"], gq)
                t.wait_ge(S["recv"], gq - 1)
                t.matmul(pBC[:, :], onesrf[:, :], rcr[:, :],
                         start=True, stop=True).then_inc(S["bcm"], 1)

    if 3 in phases:
        NT3 = TH // 128
        with (
            nc.sbuf_tensor("oms", [128, 16 * TH], BF16) as oms,
            nc.sbuf_tensor("xrs", [128, 2 * DM], F32) as xrs,
            nc.sbuf_tensor("z1s", [128, DM], F32) as z1s,
            nc.sbuf_tensor("bns", [128, 24], F32) as bns,
            nc.sbuf_tensor("mvs", [128, 2], F32) as mvs,
            nc.sbuf_tensor("l3a", [128, 1], F32) as l3a,
            nc.sbuf_tensor("l3b", [128, 1], F32) as l3b,
            nc.sbuf_tensor("l3c", [128, 1], F32) as l3c,
            nc.sbuf_tensor("l3d", [128, 1], F32) as l3d,
            nc.sbuf_tensor("x1s", [128, 2 * DM], BF16) as x1s,
            nc.sbuf_tensor("x1t", [128, DM], BF16) as x1t,
            nc.psum_tensor("pW0", [128, 512], F32) as pW0,
            nc.psum_tensor("pW1", [128, 512], F32) as pW1,
            nc.psum_tensor("pW2", [128, 512], F32) as pW2,
            nc.psum_tensor("pW3", [128, 512], F32) as pW3,
            nc.psum_tensor("pT0", [128, 128], BF16) as pT0,
            nc.psum_tensor("pT1", [128, 128], BF16) as pT1,
            nc.Block() as block,
        ):
            pW = [pW0, pW1, pW2, pW3]
            pT = [pT0, pT1]

            @block.gpsimd
            def _(g):
                g.wait_ge(S["agc"], 4)
                gsems = [S["gth"], S["gth1"], S["gth2"], S["gth3"]]
                for gi in range(16):
                    if gi >= 4:
                        g.wait_ge(gsems[gi % 4], 16 * (gi // 4))
                    g.indirect_dma_start(
                        oms[:, gi * TH:(gi + 1) * TH], None,
                        agout[:, :],
                        IndirectOffsetOnAxis(ap=oidx[:, gi:gi + 1], axis=0),
                    ).then_inc(gsems[gi % 4], 16)

            @block.sync
            def _(sy):
                for tt in range(NT3):
                    if tt >= 2:
                        sy.wait_ge(S["z1v"], 4 * (tt - 1))
                    if tt >= 1:
                        sy.wait_ge(S["xrl"], 16 * tt)
                    sy.dma_start(xrs[:, (tt % 2) * DM:(tt % 2 + 1) * DM],
                                 xres[tt * 128:(tt + 1) * 128, :]
                                 ).then_inc(S["xrl"], 16)

            @block.tensor
            def _(t):
                t.wait_ge(S["gth"], 16 * 4)
                t.wait_ge(S["gth1"], 16 * 4)
                t.wait_ge(S["gth2"], 16 * 4)
                t.wait_ge(S["gth3"], 16 * 4)
                t.wait_ge(S["wol"], 16)
                for tt in range(NT3):
                    for eb in range(4):
                        if tt >= 1:
                            t.wait_ge(S["z1v"], 4 * (tt - 1) + eb + 1)
                        for fc in range(16):
                            t.matmul(pW[eb][:, :],
                                     oms[:, fc * TH + tt * 128:
                                         fc * TH + (tt + 1) * 128],
                                     wos[:, fc * DM + eb * 512:
                                         fc * DM + (eb + 1) * 512],
                                     start=(fc == 0), stop=(fc == 15)
                                     ).then_inc(S["p3mm"], 1)
                    if tt >= 2:
                        t.wait_ge(S["apv"], tt - 1)
                        for dc in range(16):
                            gtr = 16 * (tt - 2) + dc
                            if gtr >= 2:
                                t.wait_ge(S["trv"], gtr - 1)
                            t.transpose(pT[gtr % 2][:, :],
                                        x1s[:, ((tt - 2) % 2) * DM + dc * 128:
                                            ((tt - 2) % 2) * DM + (dc + 1) * 128],
                                        ident[:, :]).then_inc(S["trm"], 1)
                for tl in (NT3 - 2, NT3 - 1):
                    t.wait_ge(S["apv"], tl + 1)
                    for dc in range(16):
                        gtr = 16 * tl + dc
                        if gtr >= 2:
                            t.wait_ge(S["trv"], gtr - 1)
                        t.transpose(pT[gtr % 2][:, :],
                                    x1s[:, (tl % 2) * DM + dc * 128:
                                        (tl % 2) * DM + (dc + 1) * 128],
                                    ident[:, :]).then_inc(S["trm"], 1)

            @block.vector
            def _(v):
                for tt in range(NT3):
                    v.drain()
                    v.wait_ge(S["xrl"], 16 * (tt + 1))
                    for eb in range(4):
                        v.wait_ge(S["p3mm"], (tt * 4 + eb + 1) * 16)
                        v.tensor_tensor(z1s[:, eb * 512:(eb + 1) * 512],
                                        pW[eb][:, :],
                                        xrs[:, (tt % 2) * DM + eb * 512:
                                            (tt % 2) * DM + (eb + 1) * 512],
                                        ALU.add).then_inc(S["z1v"], 1)
                    v.drain()
                    for eb in range(4):
                        v.bn_stats(bns[:, eb * 6:(eb + 1) * 6],
                                   z1s[:, eb * 512:(eb + 1) * 512])
                    v.drain()
                    v.bn_aggr(mvs[:, :], bns[:, :])
                    v.drain()
                    v.tensor_scalar(l3a[:, :], mvs[:, 1:2], EPS, None,
                                    ALU.add, ALU.bypass).then_inc(S["lnv"], 1)
                    v.wait_ge(S["p3sq"], tt + 1)
                    v.reciprocal(l3c[:, :], l3b[:, :]).then_inc(S["l3u"], 1)
                    v.drain()
                    v.tensor_scalar(l3d[:, :], mvs[:, 0:1], l3c[:, :], -1.0,
                                    ALU.mult, ALU.mult)
                    v.drain()
                    if tt >= 2:
                        v.wait_ge(S["trm"], 16 * (tt - 1))
                    for eb in range(4):
                        v.tensor_scalar(x1s[:, (tt % 2) * DM + eb * 512:
                                            (tt % 2) * DM + (eb + 1) * 512],
                                        z1s[:, eb * 512:(eb + 1) * 512],
                                        l3c[:, :], l3d[:, :],
                                        ALU.mult, ALU.add)
                    v.engine_nop().then_inc(S["apv"], 1)


            @block.scalar
            def _(s):
                def trevac(s, tl):
                    if tl >= 1:
                        s.wait_ge(S["x1st"], 16 * tl)
                    for dc in range(16):
                        gtr = 16 * tl + dc
                        s.wait_ge(S["trm"], gtr + 1)
                        s.copy(x1t[:, dc * 128:(dc + 1) * 128],
                               pT[gtr % 2][:, :]).then_inc(S["trv"], 1)
                    s.wait_ge(S["trv"], 16 * (tl + 1))
                    s.dma_start(
                        _r(x1f[:, tl * 128:(tl + 1) * 128],
                           "(c p) t -> p c t", p=128),
                        _r(x1t[:, :], "p (c t) -> p c t", c=16)
                    ).then_inc(S["x1st"], 16)

                for tt in range(NT3):
                    s.wait_ge(S["lnv"], tt + 1)
                    if tt >= 1:
                        s.wait_ge(S["l3u"], tt)
                    s.sqrt(l3b[:, :], l3a[:, :]).then_inc(S["p3sq"], 1)
                    if tt >= 2:
                        trevac(s, tt - 2)
                trevac(s, NT3 - 2)
                trevac(s, NT3 - 1)
                s.wait_ge(S["x1st"], 16 * NT3)

    if wos_cm is not None:
        wos_cm.__exit__(None, None, None)

    if 4 in phases:
        NHC2 = 22
        NEB = 4

        def p4_dmm_after_chunk4(gw):
            tbw, rem = divmod(gw, NEB * 11)
            ebw, hgw = divmod(rem, 11)
            return tbw * 1536 + ebw * 384 + (hgw + 1) * 32

        def p4_dmm_group_end(tb, eb, tt):
            return tb * 1536 + eb * 384 + 352 + (tt + 1) * 4

        def gu_done(tb, hc2, t2, hsub, part):
            return tb * 2816 + hc2 * 128 + t2 * 64 + hsub * 32 + 16 * part

        with contextlib.ExitStack() as _st:
            _e = _st.enter_context
            identE = _e(nc.sbuf_tensor("identE_s", [128, 4 * 512], BF16))
            x1b = _e(nc.sbuf_tensor("x1b", [128, 16 * FTB], BF16))
            wgs = _e(nc.sbuf_tensor("wgs", [128, 3 * 16 * 256], BF16))
            wgu8 = _e(nc.sbuf_tensor("wgu8", [128, 16 * 256], U8))
            wus = _e(nc.sbuf_tensor("wus", [128, 3 * 16 * 256], BF16))
            wuu8 = _e(nc.sbuf_tensor("wuu8", [128, 16 * 256], U8))
            gsb = _e(nc.sbuf_tensor("gsb", [128, NHC * FTB], BF16))
            gst = _e(nc.sbuf_tensor("gst", [128, 2 * 512], BF16))
            ust = _e(nc.sbuf_tensor("ust", [128, 2 * 512], BF16))
            wds = _e(nc.sbuf_tensor("wds", [128, 2 * 4 * 512], BF16))
            wdu8 = _e(nc.sbuf_tensor("wdu8", [128, 4 * 512], U8))
            dstg = _e(nc.sbuf_tensor("dstg", [128, 2 * 512], F32))
            pP0 = _e(nc.psum_tensor("pP0", [128, 512], F32))
            pP1 = _e(nc.psum_tensor("pP1", [128, 512], F32))
            pP2 = _e(nc.psum_tensor("pP2", [128, 512], F32))
            pP3 = _e(nc.psum_tensor("pP3", [128, 512], F32))
            pP4 = _e(nc.psum_tensor("pP4", [128, 512], F32))
            pP5 = _e(nc.psum_tensor("pP5", [128, 512], F32))
            pP6 = _e(nc.psum_tensor("pP6", [128, 512], F32))
            pP7 = _e(nc.psum_tensor("pP7", [128, 512], F32))
            block = _e(nc.Block())
            True
            pP = [pP0, pP1, pP2, pP3, pP4, pP5, pP6, pP7]
            pGs = [pP0, pP1, pP2]
            pUs = [pP3, pP4, pP5]

            @block.scalar
            def _(s):
                s.dma_start(identE[:, :], identEd[:, :]).then_inc(S["ield"], 16)
                for tb in range(2):
                    if tb >= 1:
                        s.wait_ge(S["gumm"], 2816)
                        s.wait_ge(S["dmm"], 1536)
                        s.wait_ge(S["x1l"], 16)
                    s.dma_start(
                        _r(x1b[:, :], "p (c t) -> p c t", c=16),
                        _r(x1f[:, tb * FTB:(tb + 1) * FTB],
                           "(c p) t -> p c t", p=128)).then_inc(S["x1l"], 16)
                    for hc2 in range(NHC2):
                        for t2 in range(2):
                            for hsub in range(2):
                                gt = tb * 88 + hc2 * 4 + t2 * 2 + hsub
                                hc = hc2 * 2 + hsub
                                s.wait_ge(S["gumm"],
                                          gu_done(tb, hc2, t2, hsub, 1))
                                if gt >= 2:
                                    s.wait_ge(S["gml"], gt - 1)
                                s.activation(gst[:, (gt % 2) * 512:
                                                 (gt % 2 + 1) * 512],
                                             pGs[gt % 3][:, :], AFT.Silu,
                                             bias=0.0, scale=sg[:, hc:hc + 1]
                                             ).then_inc(S["sil"], 1)

            @block.gpsimd
            def _(g):
                for tb in range(2):
                    for hc2 in range(NHC2):
                        gc = tb * NHC2 + hc2
                        if gc >= 1:
                            g.wait_ge(S["deq"], gc)
                            g.wait_ge(S["wldg"], 32 * gc)
                        sl8 = 0
                        g.dma_start(
                            wgu8[:, sl8:sl8 + 16 * 256],
                            wgT[:, hc2 * 16 * 256:(hc2 + 1) * 16 * 256]
                            ).then_inc(S["wldg"], 16)
                        g.dma_start(
                            wuu8[:, sl8:sl8 + 16 * 256],
                            wuT[:, hc2 * 16 * 256:(hc2 + 1) * 16 * 256]
                            ).then_inc(S["wldg"], 16)
                    for eb in range(NEB):
                        for hg4 in range(11):
                            gw = tb * NEB * 11 + eb * 11 + hg4
                            if gw >= 1:
                                g.wait_ge(S["wdc"], gw)
                                g.wait_ge(S["wdl"], 16 * gw)
                            g.dma_start(
                                wdu8[:, 0:2048],
                                wdT[:, (eb * NHC + hg4 * 4) * 512:
                                    (eb * NHC + hg4 * 4 + 4) * 512]
                                ).then_inc(S["wdl"], 16)

            @block.vector
            def _(v):
                def deq_chunk(gc):
                    v.wait_ge(S["wldg"], 32 * (gc + 1))
                    if gc >= 3:
                        v.wait_ge(S["gumm"], (gc - 2) * 128)
                    sl8 = 0
                    slb = (gc % 3) * 16 * 256
                    v.tensor_scalar(wgs[:, slb:slb + 16 * 256],
                                    wgu8[:, sl8:sl8 + 16 * 256],
                                    -128.0, None, ALU.add, ALU.bypass)
                    v.drain()
                    v.tensor_scalar(wus[:, slb:slb + 16 * 256],
                                    wuu8[:, sl8:sl8 + 16 * 256],
                                    -128.0, None, ALU.add, ALU.bypass
                                    ).then_inc(S["deq"], 1)

                for tb in range(2):
                    deq_chunk(tb * NHC2 + 0)
                    deq_chunk(tb * NHC2 + 1)
                    deq_chunk(tb * NHC2 + 2)
                    for hc2 in range(NHC2):
                        gc = tb * NHC2 + hc2
                        for t2 in range(2):
                            for hsub in range(2):
                                gt = tb * 88 + hc2 * 4 + t2 * 2 + hsub
                                hc = hc2 * 2 + hsub
                                v.wait_ge(S["gumm"],
                                          gu_done(tb, hc2, t2, hsub, 2))
                                if gt >= 2:
                                    v.wait_ge(S["gml"], gt - 1)
                                v.tensor_scalar(ust[:, (gt % 2) * 512:
                                                    (gt % 2 + 1) * 512],
                                                pUs[gt % 3][:, :],
                                                su[:, hc:hc + 1], None,
                                                ALU.mult, ALU.bypass)
                                v.wait_ge(S["sil"], gt + 1)
                                v.drain()
                                v.tensor_tensor(
                                    gsb[:, hc * FTB + t2 * 512:
                                        hc * FTB + (t2 + 1) * 512],
                                    gst[:, (gt % 2) * 512:(gt % 2 + 1) * 512],
                                    ust[:, (gt % 2) * 512:(gt % 2 + 1) * 512],
                                    ALU.mult).then_inc(S["gml"], 1)
                        if hc2 + 3 < NHC2:
                            deq_chunk(gc + 3)
                    for eb in range(NEB):
                        for hg4 in range(11):
                            gw = tb * NEB * 11 + eb * 11 + hg4
                            v.wait_ge(S["wdl"], 16 * (gw + 1))
                            if gw >= 2:
                                v.wait_ge(S["dmm"],
                                          p4_dmm_after_chunk4(gw - 2))
                            sw = (gw % 2) * 2048
                            v.tensor_scalar(wds[:, sw:sw + 2048],
                                            wdu8[:, 0:2048],
                                            -128.0, None, ALU.add, ALU.bypass
                                            ).then_inc(S["wdc"], 1)
                            v.drain()
                            for q4 in range(4):
                                v.tensor_tensor(
                                    wds[:, sw + q4 * 512:sw + (q4 + 1) * 512],
                                    wds[:, sw + q4 * 512:sw + (q4 + 1) * 512],
                                    sdbc[:, eb * 512:(eb + 1) * 512],
                                    ALU.mult).then_inc(S["wdq"], 1)
                        for tt in range(8):
                            ge = tb * 32 + eb * 8 + tt
                            v.wait_ge(S["dmm"], p4_dmm_group_end(tb, eb, tt))
                            if ge >= 2:
                                v.wait_ge(S["dst"], 16 * (ge - 1))
                            sidx = ((tb * 8 + tt) * 4 + eb) * 6
                            v.bn_stats(bst5[:, sidx:sidx + 6], pP[tt][:, :])
                            v.drain()
                            v.tensor_copy(dstg[:, (ge % 2) * 512:
                                              (ge % 2 + 1) * 512],
                                          pP[tt][:, :]).then_inc(S["dev"], 1)

            @block.tensor
            def _(t):
                t.wait_ge(S["ield"], 16)
                for tb in range(2):
                    t.wait_ge(S["x1l"], 16 * (tb + 1))
                    if tb >= 1:
                        t.wait_ge(S["dev"], 32 * tb)
                    for hc2 in range(NHC2):
                        gc = tb * NHC2 + hc2
                        t.wait_ge(S["deq"], gc + 1)
                        sl = (gc % 3) * 16 * 256
                        for t2 in range(2):
                            for hsub in range(2):
                                gt = tb * 88 + hc2 * 4 + t2 * 2 + hsub
                                if gt >= 3:
                                    t.wait_ge(S["sil"], gt - 2)
                                    t.wait_ge(S["gml"], gt - 2)
                                for fc in range(16):
                                    t.matmul(pGs[gt % 3][:, :],
                                             wgs[:, sl + fc * 256 + hsub * 128:
                                                 sl + fc * 256 + (hsub + 1) * 128],
                                             x1b[:, fc * FTB + t2 * 512:
                                                 fc * FTB + (t2 + 1) * 512],
                                             start=(fc == 0), stop=(fc == 15)
                                             ).then_inc(S["gumm"], 1)
                                for fc in range(16):
                                    t.matmul(pUs[gt % 3][:, :],
                                             wus[:, sl + fc * 256 + hsub * 128:
                                                 sl + fc * 256 + (hsub + 1) * 128],
                                             x1b[:, fc * FTB + t2 * 512:
                                                 fc * FTB + (t2 + 1) * 512],
                                             start=(fc == 0), stop=(fc == 15)
                                             ).then_inc(S["gumm"], 1)
                    for eb in range(NEB):
                        for hc in range(NHC):
                            hg4, q4 = divmod(hc, 4)
                            gw = tb * NEB * 11 + eb * 11 + hg4
                            t.wait_ge(S["wdq"], 4 * gw + q4 + 1)
                            if hc == 0:
                                geb = tb * NEB + eb
                                if geb >= 1:
                                    t.wait_ge(S["dev"], (geb - 1) * 8 + 8)
                                t.wait_ge(S["gml"], tb * 88 + 88)
                            sw = (gw % 2) * 2048 + q4 * 512
                            for tt in range(8):
                                t.matmul(pP[tt][:, :],
                                         gsb[:, hc * FTB + tt * 128:
                                             hc * FTB + (tt + 1) * 128],
                                         wds[:, sw:sw + 512],
                                         start=(hc == 0), stop=False
                                         ).then_inc(S["dmm"], 1)
                        for tt in range(8):
                            for j in range(4):
                                dc = eb * 4 + j
                                t.matmul(pP[tt][:, :],
                                         x1b[:, dc * FTB + tt * 128:
                                             dc * FTB + (tt + 1) * 128],
                                         identE[:, j * 512:(j + 1) * 512],
                                         start=False, stop=(j == 3)
                                         ).then_inc(S["dmm"], 1)

            @block.sync
            def _(sy):
                for tb in range(2):
                    for eb in range(NEB):
                        for tt in range(8):
                            ge = tb * 32 + eb * 8 + tt
                            sy.wait_ge(S["dev"], ge + 1)
                            if ge >= 1:
                                sy.wait_ge(S["dst"], 16 * ge)
                            sy.dma_start(
                                opre[tb * FTB + tt * 128:
                                     tb * FTB + (tt + 1) * 128,
                                     eb * 512:(eb + 1) * 512],
                                dstg[:, (ge % 2) * 512:(ge % 2 + 1) * 512]
                                ).then_inc(S["dst"], 16)
                sy.wait_ge(S["dst"], 16 * 64)

    if 5 in phases:
        NT5 = TH // 128
        with (
            nc.sbuf_tensor("ops", [128, 2 * DM], F32) as ops,
            nc.sbuf_tensor("bn5", [128, 24], F32) as bn5,
            nc.sbuf_tensor("mv5", [128, 2], F32) as mv5,
            nc.sbuf_tensor("l5a", [128, 1], F32) as l5a,
            nc.sbuf_tensor("l5b", [128, 1], F32) as l5b,
            nc.sbuf_tensor("l5c", [128, 1], F32) as l5c,
            nc.sbuf_tensor("l5d", [128, 1], F32) as l5d,
            nc.sbuf_tensor("o5s", [128, 2 * DM], F32) as o5s,
            nc.Block() as block,
        ):
            @block.sync
            def _(sy):
                for tt in range(NT5):
                    if tt >= 1:
                        sy.wait_ge(S["p5ap"], tt)
                        if tt >= 2:
                            sy.wait_ge(S["p5st"], 16 * (tt - 1))
                        sy.dma_start(out[(tt - 1) * 128: tt * 128, :],
                                     o5s[:, ((tt - 1) % 2) * DM:
                                         ((tt - 1) % 2 + 1) * DM]
                                     ).then_inc(S["p5st"], 16)
                sy.wait_ge(S["p5ap"], NT5)
                sy.wait_ge(S["p5st"], 16 * (NT5 - 1))
                sy.dma_start(out[(NT5 - 1) * 128: NT5 * 128, :],
                             o5s[:, ((NT5 - 1) % 2) * DM:
                                 ((NT5 - 1) % 2 + 1) * DM]
                             ).then_inc(S["p5st"], 16)
                sy.wait_ge(S["p5st"], 16 * NT5)

            @block.vector
            def _(v):
                for tt in range(NT5):
                    v.drain()
                    v.wait_ge(S["p5l"], 16 * (tt + 1))
                    v.bn_aggr(mv5[:, :], bst5[:, tt * 24:(tt + 1) * 24])
                    v.drain()
                    v.tensor_scalar(l5a[:, :], mv5[:, 1:2], EPS, None,
                                    ALU.add, ALU.bypass).then_inc(S["p5bn"], 1)
                    v.wait_ge(S["p5sq"], tt + 1)
                    v.reciprocal(l5c[:, :], l5b[:, :]).then_inc(S["l5u"], 1)
                    v.drain()
                    v.tensor_scalar(l5d[:, :], mv5[:, 0:1], l5c[:, :], -1.0,
                                    ALU.mult, ALU.mult)
                    v.drain()
                    if tt >= 2:
                        v.wait_ge(S["p5st"], 16 * (tt - 1))
                    for eb in range(4):
                        v.tensor_scalar(o5s[:, (tt % 2) * DM + eb * 512:
                                            (tt % 2) * DM + (eb + 1) * 512],
                                        ops[:, (tt % 2) * DM + eb * 512:
                                            (tt % 2) * DM + (eb + 1) * 512],
                                        l5c[:, :], l5d[:, :],
                                        ALU.mult, ALU.add)
                    v.engine_nop().then_inc(S["p5ap"], 1)

            @block.scalar
            def _(s):
                for tt in range(NT5):
                    if tt >= 2:
                        s.wait_ge(S["p5ap"], tt - 1)
                    if tt >= 1:
                        s.wait_ge(S["p5l"], 16 * tt)
                    s.dma_start(ops[:, (tt % 2) * DM:(tt % 2 + 1) * DM],
                                opre[tt * 128:(tt + 1) * 128, :]
                                ).then_inc(S["p5l"], 16)
                    s.wait_ge(S["p5bn"], tt + 1)
                    if tt >= 1:
                        s.wait_ge(S["l5u"], tt)
                    s.sqrt(l5b[:, :], l5a[:, :]).then_inc(S["p5sq"], 1)

    if dbg:
        with nc.Block() as block:
            @block.gpsimd
            def _(g):
                n = 0
                for name, (dt_, srcd) in dbg.items():
                    if n:
                        g.wait_ge(S["dbgs"], n)
                    g.dma_start(dt_[:, :], srcd[:, :]).then_inc(S["dbgs"], 16)
                    n += 16
                g.wait_ge(S["dbgs"], n)

    return nc



def _consts():
    eye = np.eye(128, dtype=np.float32)
    cm = np.asarray(eye - 1.0 / 128.0, np.float32).astype(ml_dtypes.bfloat16)
    idm = eye.astype(ml_dtypes.bfloat16)
    r = np.arange(128)[:, None]
    c = np.arange(QB)[None, :]
    masks = np.concatenate(
        [(c >= r + 128 * j).astype(np.float32) for j in range(4)],
        axis=1).astype(ml_dtypes.bfloat16)
    identE = np.zeros((128, 4 * 512), np.float32)
    for j in range(4):
        identE[np.arange(128), j * 512 + j * 128 + np.arange(128)] = 1.0
    identE = identE.astype(ml_dtypes.bfloat16)
    onescol = np.ones((128, 1), ml_dtypes.bfloat16)
    onesrow = np.ones((1, 128), ml_dtypes.bfloat16)
    return cm, idm, masks, identE, onescol, onesrow


def make_in_maps(inputs):
    x = np.asarray(inputs["L0_attn_hidden_states"], np.float32)
    Wq = np.asarray(inputs["L0_attn_Wq"], np.float32)
    Wk = np.asarray(inputs["L0_attn_Wk"], np.float32)
    Wv = np.asarray(inputs["L0_attn_Wv"], np.float32)
    Wo = np.asarray(inputs["L0_attn_Wo"], np.float32)
    gamma = np.asarray(inputs["L0_attn_gamma"], np.float32)
    beta = np.asarray(inputs["L0_attn_beta"], np.float32)
    wqg = np.asarray(inputs["L0_ffn_wq_gate"], np.int32)
    sgv = np.asarray(inputs["L0_ffn_s_gate"], np.float32)
    wqu = np.asarray(inputs["L0_ffn_wq_up"], np.int32)
    suv = np.asarray(inputs["L0_ffn_s_up"], np.float32)
    wqd = np.asarray(inputs["L0_ffn_wq_down"], np.int32)
    sdv = np.asarray(inputs["L0_ffn_s_down"], np.float32)

    cm, idm, masks, identE, onescol, onesrow = _consts()
    woTh = np.ascontiguousarray(
        Wo.T.reshape(16, 128, DM).transpose(1, 0, 2).reshape(128, 16 * DM))
    wgTh = np.ascontiguousarray(
        wqg.T.reshape(16, 128, 22, 256).transpose(1, 2, 0, 3)
        .reshape(128, 22 * 16 * 256).astype(np.uint8))
    wuTh = np.ascontiguousarray(
        wqu.T.reshape(16, 128, 22, 256).transpose(1, 2, 0, 3)
        .reshape(128, 22 * 16 * 256).astype(np.uint8))
    wdTh = np.ascontiguousarray(
        wqd.T.reshape(NHC, 128, 4, 512).transpose(1, 2, 0, 3)
        .reshape(128, 4 * NHC * 512).astype(np.uint8))
    sg2 = np.ascontiguousarray(sgv.reshape(NHC, 128).T)
    su2 = np.ascontiguousarray(suv.reshape(NHC, 128).T)
    sdr = np.ascontiguousarray(sdv[None, :])

    maps = []
    for c in range(8):
        b, hg = c // 2, c % 2
        p = np.arange(128, dtype=np.uint64)[:, None]
        i = np.arange(16, dtype=np.uint64)[None, :]
        slot = 2 * b + i // 8
        region = hg * 2 + (i % 8) // 4
        oix = (region * 8 * 512 + slot * 512 + ((i % 8) % 4) * 128 + p
               ).astype(np.uint32)
        maps.append({
            "xT": np.ascontiguousarray(
                x[b].T.reshape(16, 128, NTB, TB).transpose(1, 2, 0, 3)
                .reshape(128, NTB * 16 * TB)),
            "xres": np.ascontiguousarray(x[b, hg * TH:(hg + 1) * TH]),
            "wq": np.ascontiguousarray(Wq[hg * EQ:(hg + 1) * EQ]),
            "wk": np.ascontiguousarray(Wk[hg * EQ:(hg + 1) * EQ]),
            "wvT": np.ascontiguousarray(
                Wv[hg * EQ:(hg + 1) * EQ].T.reshape(16, 128, EQ)
                .transpose(1, 0, 2).reshape(128, 16 * EQ)),
            "woT": woTh, "wgT": wgTh, "wuT": wuTh, "wdT": wdTh,
            "sg": sg2, "su": su2, "sdrow": sdr,
            "gam": np.ascontiguousarray(gamma[:, None]),
            "bet": np.ascontiguousarray(beta[:, None]),
            "cmat": cm, "identm": idm, "masks": masks, "identE": identE,
            "onescol": onescol, "onesrow": onesrow,
            "onesrowf": np.ones((1, 128), np.float32), "oidx": oix,
        })
    return maps


def kernel(**inputs):
    maps = make_in_maps(inputs)
    nc = build()
    r = run_bass_kernel_spmd(nc, maps, core_ids=list(range(8)))
    outf = np.empty((4, T, DM), np.float32)
    for c in range(8):
        b, hg = c // 2, c % 2
        outf[b, hg * TH:(hg + 1) * TH] = np.asarray(r.results[c]["out"])
    return outf


# version 22
# speedup vs baseline: 1.0397x; 1.0092x over previous
import contextlib
import numpy as np
import ml_dtypes
import concourse.bass as bass
import concourse.mybir as mybir
from concourse.bass_utils import run_bass_kernel_spmd
from concourse.bass_core_rust import IndirectOffsetOnAxis

F32 = mybir.dt.float32
U8 = mybir.dt.uint8
BF16 = mybir.dt.bfloat16
I32 = mybir.dt.int32
U32 = mybir.dt.uint32
AFT = mybir.ActivationFunctionType
ALU = mybir.AluOpType

T = 4096
DM = 2048
NH = 8
D = 128
EQ = NH * D
HID = 5632
NHC = HID // 128
TH = 2048
TB = 512
NTB = T // TB
QB = 512
NQB = T // QB
FTB = 1024
EPS = 1e-5
DEPS = float(128 * EPS)
AG_CUM = [0, 4, 8, 12, 14, 15, 16]
P3_A = [0, 1, 2, 3, 8, 9, 10, 11]
P3_B = [4, 5, 12, 13]
P3_C = [6, 14]
P3_D = [7, 15]
C1 = float(EPS / 128.0)
C2 = float(EPS * EPS)


def _r(ap, pat, **kw):
    return ap.rearrange(pat, **kw)


def build(phases=(1, 2, 3, 4, 5), debug_outs=()):
    nc = bass.Bass("TRN2", target_bir_lowering=False)

    xT = nc.dram_tensor("xT", [128, NTB * 16 * TB], F32, kind="ExternalInput")
    xres = nc.dram_tensor("xres", [TH, DM], F32, kind="ExternalInput")
    wqd_ = nc.dram_tensor("wq", [EQ, DM], F32, kind="ExternalInput")
    wkd_ = nc.dram_tensor("wk", [EQ, DM], F32, kind="ExternalInput")
    wvT = nc.dram_tensor("wvT", [128, 16 * EQ], F32, kind="ExternalInput")
    woT = nc.dram_tensor("woT", [128, 16 * DM], F32, kind="ExternalInput")
    wgT = nc.dram_tensor("wgT", [128, 22 * 16 * 256], U8, kind="ExternalInput")
    wuT = nc.dram_tensor("wuT", [128, 22 * 16 * 256], U8, kind="ExternalInput")
    wdT = nc.dram_tensor("wdT", [128, 4 * NHC * 512], U8, kind="ExternalInput")
    sgd = nc.dram_tensor("sg", [128, NHC], F32, kind="ExternalInput")
    sud = nc.dram_tensor("su", [128, NHC], F32, kind="ExternalInput")
    sdrow = nc.dram_tensor("sdrow", [1, DM], F32, kind="ExternalInput")
    gamd = nc.dram_tensor("gam", [128, 1], F32, kind="ExternalInput")
    betd = nc.dram_tensor("bet", [128, 1], F32, kind="ExternalInput")
    cmatd = nc.dram_tensor("cmat", [128, 128], BF16, kind="ExternalInput")
    identd = nc.dram_tensor("identm", [128, 128], BF16, kind="ExternalInput")
    masksd = nc.dram_tensor("masks", [128, 4 * QB], BF16, kind="ExternalInput")
    identEd = nc.dram_tensor("identE", [128, 4 * 512], BF16, kind="ExternalInput")
    onescd = nc.dram_tensor("onescol", [128, 1], BF16, kind="ExternalInput")
    onesrd = nc.dram_tensor("onesrow", [1, 128], BF16, kind="ExternalInput")
    onesrfd = nc.dram_tensor("onesrowf", [1, 128], F32, kind="ExternalInput")
    oidxd = nc.dram_tensor("oidx", [128, 16], U32, kind="ExternalInput")
    out = nc.dram_tensor("out", [TH, DM], F32, kind="ExternalOutput")

    qs = nc.dram_tensor("qs", [EQ, T], BF16)
    ks = nc.dram_tensor("ks", [EQ, T], BF16)
    vs = nc.dram_tensor("vs", [T, EQ], BF16)
    vqh = nc.dram_tensor("vqh", [NH, T], F32)
    agin = nc.dram_tensor("agin", [2 * EQ, TH], BF16)
    agout = nc.dram_tensor("agout", [8 * 2 * EQ, TH], BF16)
    x1f = nc.dram_tensor("x1f", [DM, TH], BF16)
    opre = nc.dram_tensor("opre", [TH, DM], F32)

    dbg = {}
    dbg_specs = {"d_qs": (qs, BF16), "d_ks": (ks, BF16), "d_vs": (vs, BF16),
                 "d_vqh": (vqh, F32), "d_agin": (agin, BF16),
                 "d_agout": (agout, BF16), "d_x1f": (x1f, BF16),
                 "d_opre": (opre, F32)}
    for name in debug_outs:
        src, dt = dbg_specs[name]
        dbg[name] = (nc.dram_tensor(name, list(src.shape), dt,
                                    kind="ExternalOutput"), src)

    keep = []

    def sb(name, shape, dtype):
        h = nc.alloc_sbuf_tensor(name, shape, dtype)
        keep.append(h)
        return h

    SEM_NAMES = [
        "c_ld", "c_mm", "c_ev", "wld", "wsc", "wmm", "wev",
        "xld", "p1mm", "sq2", "vev", "vqm", "vqe", "ksm", "rsa", "rsq", "rsd",
        "spq", "spk", "spv", "spvq",
        "kvld", "qld", "vqld", "p2ms", "sqa", "mka", "p2oz", "zrv", "denv",
        "bcm", "recv", "oev", "ost",
        "agc", "gth", "wol", "xrl", "p3mm", "z1v", "lnv", "p3sq", "apv",
        "trm", "trv", "x1st",
        "x1l", "wldg", "deq", "gumm", "sil", "gml",
        "wdl", "wdq", "dmm", "dev", "dst",
        "p5l", "p5bn", "p5sq", "p5ap", "p5st", "l3u", "l5u", "sqv", "kvv", "wdc", "mld", "ield", "gth1", "gth2", "gth3", "rcs",
        "dbgs",
    ]
    S = {n: nc.alloc_semaphore(name=n) for n in SEM_NAMES}

    cmat = sb("cmat_s", [128, 128], BF16)
    ident = sb("ident_s", [128, 128], BF16)
    onesc = sb("onesc_s", [128, 1], BF16)
    onesr = sb("onesr_s", [1, 128], BF16)
    onesrf = sb("onesrf_s", [1, 128], F32)
    oidx = sb("oidx_s", [128, 16], U32)
    gam = sb("gam_s", [128, 1], F32)
    betbf = sb("betbf_s", [128, 1], BF16)
    cbeta = sb("cbeta_s", [128, 1], F32)
    sg = sb("sg_s", [128, NHC], F32)
    su = sb("su_s", [128, NHC], F32)
    sdbc = sb("sdbc_s", [128, DM], BF16)
    rsk = sb("rsk_s", [128, NH * 32], F32)
    bst5 = sb("bst5_s", [128, 64 * 6], F32)

    with (
        nc.sbuf_tensor("wtq", [128, 16 * EQ], BF16) as wtq,
        nc.sbuf_tensor("wtk", [128, 16 * EQ], BF16) as wtk,
        nc.sbuf_tensor("wvs", [128, 16 * EQ], BF16) as wvs,
    ):
        with (
            nc.sbuf_tensor("sdbf_s", [1, DM], BF16) as sdbf,
            nc.sbuf_tensor("wraw", [128, DM], BF16) as wraw,
            nc.sbuf_tensor("wgam", [128, DM], BF16) as wgam,
            nc.psum_tensor("p0a", [128, 128], F32) as p0a,
            nc.psum_tensor("p0b", [128, 128], F32) as p0b,
            nc.psum_tensor("p0c", [128, 512], F32) as p0c,
            nc.Block() as block,
        ):
            @block.gpsimd
            def _(g):
                nld = 0
                for dst, srcd in [(cmat, cmatd), (ident, identd),
                                  (onesc, onescd),
                                  (onesr, onesrd), (oidx, oidxd), (gam, gamd),
                                  (sg, sgd), (su, sud), (betbf, betd),
                                  (sdbf, sdrow), (onesrf, onesrfd)]:
                    if nld:
                        g.wait_ge(S["c_ld"], nld)
                    g.dma_start(dst[:, :], srcd[:, :]).then_inc(S["c_ld"], 16)
                    nld += 16
                g.wait_ge(S["c_ld"], nld)
                g.dma_start(wvs[:, :], wvT[:, :]).then_inc(S["c_ld"], 16)
                for k in range(16):
                    wsrc = wqd_ if k < 8 else wkd_
                    h = k % 8
                    if k >= 1:
                        g.wait_ge(S["wsc"], k)
                        g.wait_ge(S["wld"], 16 * k)
                    g.dma_start(wraw[:, :], wsrc[h * 128:(h + 1) * 128, :]
                                ).then_inc(S["wld"], 16)

            @block.vector
            def _(v):
                v.wait_ge(S["c_mm"], 1)
                v.tensor_copy(cbeta[:, :], p0a[:, 0:1])
                v.engine_nop().then_inc(S["c_ev"], 1)
                for ebk in range(4):
                    v.wait_ge(S["c_mm"], 2 + ebk)
                    v.tensor_copy(sdbc[:, ebk * 512:(ebk + 1) * 512], p0c[:, :]
                                  ).then_inc(S["c_ev"], 1)
                for k in range(16):
                    v.wait_ge(S["wld"], 16 * (k + 1))
                    if k >= 1:
                        v.wait_ge(S["wmm"], 16 * k)
                    v.tensor_scalar(wgam[:, :], wraw[:, :], gam[:, :], None,
                                    ALU.mult, ALU.bypass).then_inc(S["wsc"], 1)
                    wt = wtq if k < 8 else wtk
                    h = k % 8
                    for fc in range(16):
                        v.wait_ge(S["wmm"], 16 * k + fc + 1)
                        bank = p0a if fc % 2 == 0 else p0b
                        v.tensor_copy(
                            wt[:, fc * EQ + h * 128: fc * EQ + (h + 1) * 128],
                            bank[:, :]).then_inc(S["wev"], 1)

            @block.tensor
            def _(t):
                t.wait_ge(S["c_ld"], 9 * 16)
                t.matmul(p0a[:, 0:1], cmat[:, :], betbf[:, :],
                         start=True, stop=True).then_inc(S["c_mm"], 1)
                t.wait_ge(S["c_ld"], 10 * 16)
                for ebk in range(4):
                    if ebk >= 1:
                        t.wait_ge(S["c_ev"], 1 + ebk)
                    t.matmul(p0c[:, :], onesr[:, :],
                             sdbf[:, ebk * 512:(ebk + 1) * 512],
                             start=True, stop=True).then_inc(S["c_mm"], 1)
                t.wait_ge(S["c_ev"], 5)
                for k in range(16):
                    t.wait_ge(S["wsc"], k + 1)
                    for fc in range(16):
                        gidx = 16 * k + fc
                        if gidx >= 2:
                            t.wait_ge(S["wev"], gidx - 1)
                        bank = p0a if fc % 2 == 0 else p0b
                        t.matmul(bank[:, :], wgam[:, fc * 128:(fc + 1) * 128],
                                 cmat[:, :], start=True, stop=True
                                 ).then_inc(S["wmm"], 1)

        if 1 in phases:
            with (
                nc.sbuf_tensor("xTs", [128, 2 * 16 * TB], BF16) as xTs,
                nc.sbuf_tensor("qcs", [128, 8 * TB], BF16) as qcs,
                nc.sbuf_tensor("kcs", [128, 8 * TB], BF16) as kcs,
                nc.sbuf_tensor("vvs", [128, 8 * TB], BF16) as vvs,
                nc.sbuf_tensor("q2s", [128, 16 * TB], BF16) as q2s,
                nc.sbuf_tensor("k2s", [128, 16 * TB], BF16) as k2s,
                nc.sbuf_tensor("vqr", [1, 2 * TB], F32) as vqr,
                nc.sbuf_tensor("rs1", [128, 32], F32) as rs1,
                nc.sbuf_tensor("rs2", [128, 32], F32) as rs2,
                nc.psum_tensor("pA", [128, TB], F32) as pA,
                nc.psum_tensor("pB", [128, TB], F32) as pB,
                nc.psum_tensor("pC", [128, TB], F32) as pC,
                nc.psum_tensor("pD", [128, TB], F32) as pD,
                nc.psum_tensor("pKS", [128, 32], F32) as pKS,
                nc.psum_tensor("pVQ0", [1, TB], F32) as pVQ0,
                nc.psum_tensor("pVQ1", [1, TB], F32) as pVQ1,
                nc.Block() as block,
            ):
                banks = [pA, pB, pC, pD]
                pVQs = [pVQ0, pVQ1]

                def xv(buf, fc, lo, hi):
                    base = buf * 16 * TB + fc * TB
                    return xTs[:, base + lo: base + hi]

                @block.gpsimd
                def _(g):
                    for tb in range(NTB):
                        if tb >= 2:
                            g.wait_ge(S["p1mm"], 384 * (tb - 1))
                        if tb >= 1:
                            g.wait_ge(S["xld"], 16 * tb)
                        g.dma_start(
                            xTs[:, (tb % 2) * 16 * TB:(tb % 2 + 1) * 16 * TB],
                            xT[:, tb * 16 * TB:(tb + 1) * 16 * TB]
                            ).then_inc(S["xld"], 16)

                @block.tensor
                def _(t):
                    t.wait_ge(S["c_ld"], 12 * 16)
                    for tb in range(NTB):
                        t.wait_ge(S["xld"], 16 * (tb + 1))
                        buf = tb % 2
                        for i in range(24):
                            gidx = tb * 24 + i
                            bank = banks[gidx % 4]
                            if gidx >= 4:
                                t.wait_ge(S["vev"], gidx - 3)
                            if i < 8:
                                for fc in range(16):
                                    t.matmul(bank[:, :],
                                             wtq[:, fc * EQ + i * 128:
                                                 fc * EQ + (i + 1) * 128],
                                             xv(buf, fc, 0, TB),
                                             start=(fc == 0), stop=(fc == 15)
                                             ).then_inc(S["p1mm"], 1)
                            elif i < 16:
                                e = i - 8
                                for fc in range(16):
                                    t.matmul(bank[:, :],
                                             wtk[:, fc * EQ + e * 128:
                                                 fc * EQ + (e + 1) * 128],
                                             xv(buf, fc, 0, TB),
                                             start=(fc == 0), stop=(fc == 15)
                                             ).then_inc(S["p1mm"], 1)
                            else:
                                tt = (i - 16) // 2
                                eb = (i - 16) % 2
                                for fc in range(16):
                                    t.matmul(bank[:, :],
                                             xv(buf, fc, tt * 128, (tt + 1) * 128),
                                             wvs[:, fc * EQ + eb * 512:
                                                 fc * EQ + (eb + 1) * 512],
                                             start=(fc == 0), stop=(fc == 15)
                                             ).then_inc(S["p1mm"], 1)
                        for h in range(NH):
                            t.wait_ge(S["sq2"], 16 * tb + h + 1)
                            if tb * 8 + h >= 2:
                                t.wait_ge(S["vqe"], tb * 8 + h - 1)
                            t.matmul(pVQs[(tb * 8 + h) % 2][:, :], onesc[:, :],
                                     q2s[:, (tb % 2) * 8 * TB + h * TB:
                                         (tb % 2) * 8 * TB + (h + 1) * TB],
                                     start=True, stop=True).then_inc(S["vqm"], 1)
                        if tb >= 1:
                            t.wait_ge(S["rsd"], tb)
                        for h in range(NH):
                            t.wait_ge(S["sq2"], 16 * tb + 8 + h + 1)
                            for tt in range(4):
                                col = h * 4 + tt
                                t.matmul(pKS[:, col:col + 1],
                                         k2s[:, (tb % 2) * 8 * TB + h * TB + tt * 128:
                                             (tb % 2) * 8 * TB + h * TB + (tt + 1) * 128],
                                         onesc[:, :], start=True, stop=True
                                         ).then_inc(S["ksm"], 1)

                @block.scalar
                def _(s):
                    for tb in range(NTB):
                        for i in range(16):
                            gidx = tb * 24 + i
                            h = i % 8
                            s.wait_ge(S["p1mm"], gidx * 16 + 16)
                            if tb >= 2:
                                if i < 8:
                                    s.wait_ge(S["vqm"], 8 * (tb - 2) + h + 1)
                                else:
                                    s.wait_ge(S["ksm"], 32 * (tb - 2) + 4 * (h + 1))
                            bank = banks[gidx % 4]
                            st = q2s if i < 8 else k2s
                            so = (tb % 2) * 8 * TB
                            s.activation(st[:, so + h * TB:so + (h + 1) * TB],
                                         bank[:, :],
                                         AFT.Square, bias=cbeta[:, :], scale=1.0
                                         ).then_inc(S["sq2"], 1)
                        for h in range(NH):
                            gq = tb * 8 + h
                            s.wait_ge(S["vqe"], gq + 1)
                            if gq >= 1:
                                s.wait_ge(S["spvq"], 16 * gq)
                            s.dma_start(vqh[h:h + 1, tb * TB:(tb + 1) * TB],
                                        vqr[:, (gq % 2) * TB:(gq % 2 + 1) * TB]
                                        ).then_inc(S["spvq"], 16)
                        s.wait_ge(S["rsa"], tb + 1)
                        if tb >= 1:
                            s.wait_ge(S["rsd"], tb)
                        s.sqrt(rs2[:, :], rs1[:, :]).then_inc(S["rsq"], 1)

                @block.vector
                def _(v):
                    for tb in range(NTB):
                        for i in range(24):
                            gidx = tb * 24 + i
                            bank = banks[gidx % 4]
                            if i < 16:
                                h = i % 8
                                v.wait_ge(S["sq2"], 16 * tb + i + 1)
                                if tb >= 1:
                                    sp = S["spq"] if i < 8 else S["spk"]
                                    v.wait_ge(sp, 16 * (8 * (tb - 1) + h + 1))
                                st = qcs if i < 8 else kcs
                                v.tensor_scalar(st[:, h * TB:(h + 1) * TB],
                                                bank[:, :], cbeta[:, :], None,
                                                ALU.add, ALU.bypass
                                                ).then_inc(S["vev"], 1)
                            else:
                                j = i - 16
                                v.wait_ge(S["p1mm"], gidx * 16 + 16)
                                if tb >= 1:
                                    v.wait_ge(S["spv"],
                                              16 * (8 * (tb - 1) + j + 1))
                                v.tensor_copy(vvs[:, j * TB:(j + 1) * TB],
                                              bank[:, :]).then_inc(S["vev"], 1)
                        for h in range(NH):
                            gq = tb * 8 + h
                            v.wait_ge(S["vqm"], gq + 1)
                            if gq >= 2:
                                v.wait_ge(S["spvq"], 16 * (gq - 1))
                            v.tensor_copy(vqr[:, (gq % 2) * TB:(gq % 2 + 1) * TB],
                                          pVQs[gq % 2][:, :]).then_inc(S["vqe"], 1)
                        v.wait_ge(S["ksm"], 32 * (tb + 1))
                        v.tensor_scalar(rs1[:, :], pKS[:, :], DEPS, None,
                                        ALU.add, ALU.bypass).then_inc(S["rsa"], 1)
                        v.wait_ge(S["rsq"], tb + 1)
                        v.reciprocal(
                            _r(rsk[:, :], "p (h k) -> p h k", h=NH)
                            [:, :, tb * 4:(tb + 1) * 4],
                            _r(rs2[:, :], "p (h k) -> p h k", h=NH)
                        ).then_inc(S["rsd"], 1)

                @block.sync
                def _(sy):
                    for tb in range(NTB):
                        for i in range(24):
                            sy.wait_ge(S["vev"], tb * 24 + i + 1)
                            if i < 8:
                                if tb * 8 + i >= 1:
                                    sy.wait_ge(S["spq"], 16 * (tb * 8 + i))
                                sy.dma_start(qs[i * 128:(i + 1) * 128,
                                                tb * TB:(tb + 1) * TB],
                                             qcs[:, i * TB:(i + 1) * TB]
                                             ).then_inc(S["spq"], 16)
                            elif i < 16:
                                h = i - 8
                                if tb * 8 + h >= 1:
                                    sy.wait_ge(S["spk"], 16 * (tb * 8 + h))
                                sy.dma_start(ks[h * 128:(h + 1) * 128,
                                                tb * TB:(tb + 1) * TB],
                                             kcs[:, h * TB:(h + 1) * TB]
                                             ).then_inc(S["spk"], 16)
                            else:
                                j = i - 16
                                tt, eb = j // 2, j % 2
                                if tb * 8 + j >= 1:
                                    sy.wait_ge(S["spv"], 16 * (tb * 8 + j))
                                sy.dma_start(vs[tb * TB + tt * 128:
                                                tb * TB + (tt + 1) * 128,
                                                eb * 512:(eb + 1) * 512],
                                             vvs[:, j * TB:(j + 1) * TB]
                                             ).then_inc(S["spv"], 16)
                    sy.wait_ge(S["spq"], 16 * 64)
                    sy.wait_ge(S["spk"], 16 * 64)
                    sy.wait_ge(S["spv"], 16 * 64)
                    sy.wait_ge(S["spvq"], 16 * 64)

    wos_cm = (nc.sbuf_tensor("wos", [128, 16 * DM], BF16)
              if (2 in phases or 3 in phases) else None)
    wos = wos_cm.__enter__() if wos_cm is not None else None
    if 2 in phases:
        seq = [(h, half * 4 + j) for half in range(2)
               for h in range(NH) for j in range(4)]
        nkt = {qb: 4 * (qb + 1) for qb in range(NQB)}
        ga_base = {}
        oz_end = {}
        run = 0
        ozr = 0
        for (h, qb) in seq:
            ga_base[(h, qb)] = run
            run += nkt[qb]
            ozr += 2 * nkt[qb]
            oz_end[(h, qb)] = ozr
        mk_idx = {}
        mrun = 0
        for (h, qb) in seq:
            for kt in range(nkt[qb] - 4, nkt[qb]):
                mk_idx[(h, qb, kt)] = mrun
                mrun += 1

        def is_masked(qb, kt):
            return kt >= nkt[qb] - 4

        def sq_on_vector(kt):
            return kt % 3 == 2 and kt > 5

        cntS, cntV = {}, {}
        _cs = _cv = 0
        for (_h, _qb) in seq:
            for _kt in range(nkt[_qb]):
                if sq_on_vector(_kt):
                    _cv += 1
                else:
                    _cs += 1
                _ga = ga_base[(_h, _qb)] + _kt
                cntS[_ga], cntV[_ga] = _cs, _cv

        sq_owner_vec = {}
        for (_h, _qb) in seq:
            for _kt in range(nkt[_qb]):
                sq_owner_vec[ga_base[(_h, _qb)] + _kt] = sq_on_vector(_kt)

        def wait_sq(eng, ga):
            if sq_owner_vec[ga]:
                eng.wait_ge(S["sqv"], cntV[ga])
            else:
                eng.wait_ge(S["sqa"], cntS[ga])

        with contextlib.ExitStack() as _st:
            _e = _st.enter_context
            masks = _e(nc.sbuf_tensor("masks_s", [128, 4 * QB], BF16))
            khs = _e(nc.sbuf_tensor("khs", [128, 2 * T], BF16))
            vhs = _e(nc.sbuf_tensor("vhs", [128, 2 * 32 * 128], BF16))
            qbs = _e(nc.sbuf_tensor("qbs", [128, 4 * QB], BF16))
            a_s = _e(nc.sbuf_tensor("a_s", [128, 3 * QB], BF16))
            ams = _e(nc.sbuf_tensor("ams", [128, 3 * QB], BF16))
            vqhd = _e(nc.sbuf_tensor("vqhd", [1, T], F32))
            zrow = _e(nc.sbuf_tensor("zrow", [1, QB], F32))
            lnr = _e(nc.sbuf_tensor("lnr", [1, QB], F32))
            rcr = _e(nc.sbuf_tensor("rcr", [1, QB], F32))
            den1 = _e(nc.sbuf_tensor("den1", [1, 4 * QB], F32))
            den2 = _e(nc.sbuf_tensor("den2", [1, QB], F32))
            recb = _e(nc.sbuf_tensor("recb", [128, QB], F32))
            vsq = _e(nc.sbuf_tensor("vsq", [128, QB], BF16))
            obs = _e(nc.sbuf_tensor("obs", [128, 2 * QB], BF16))
            pS0 = _e(nc.psum_tensor("pS0", [128, QB], F32))
            pS1 = _e(nc.psum_tensor("pS1", [128, QB], F32))
            pS2 = _e(nc.psum_tensor("pS2", [128, QB], F32))
            pO0 = _e(nc.psum_tensor("pO0", [128, QB], F32))
            pO1 = _e(nc.psum_tensor("pO1", [128, QB], F32))
            pZ0 = _e(nc.psum_tensor("pZ0", [1, QB], F32))
            pZ1 = _e(nc.psum_tensor("pZ1", [1, QB], F32))
            pBC = _e(nc.psum_tensor("pBC", [128, QB], F32))
            block = _e(nc.Block())
            True
            pS = [pS0, pS1, pS2]
            pOs = [pO0, pO1]
            pZs = [pZ0, pZ1]

            @block.gpsimd
            def _(g):
                if 3 not in phases:
                    return
                g.dma_start(wos[:, :], woT[:, :]).then_inc(S["wol"], 16)
                for c4 in range(6):
                    g.wait_ge(S["ost"], 16 * 4 * AG_CUM[c4 + 1])
                    g.collective_compute(
                        "AllGather", ALU.bypass,
                        replica_groups=[[0, 1, 2, 3, 4, 5, 6, 7]],
                        ins=[agin[AG_CUM[c4] * 128:
                                  AG_CUM[c4 + 1] * 128, :].opt()],
                        outs=[agout[AG_CUM[c4] * 128 * 8:
                                    AG_CUM[c4 + 1] * 128 * 8, :].opt()],
                    ).then_inc(S["agc"], 1)

            @block.sync
            def _(sy):
                sy.dma_start(masks[:, :], masksd[:, :]).then_inc(S["mld"], 16)
                def qload(gl):
                    hp, qp = seq[gl]
                    if gl >= 4:
                        hx, qx = seq[gl - 4]
                        sy.wait_ge(S["p2ms"], ga_base[(hx, qx)] + nkt[qx])
                    if gl >= 1:
                        sy.wait_ge(S["qld"], 16 * gl)
                    sy.dma_start(qbs[:, (gl % 4) * QB:(gl % 4 + 1) * QB],
                                 qs[hp * 128:(hp + 1) * 128,
                                    qp * QB:(qp + 1) * QB]
                                 ).then_inc(S["qld"], 16)

                qload(0)
                qload(1)
                for gq in range(64):
                    if gq + 2 < 64:
                        qload(gq + 2)
                    if gq >= 1:
                        sy.wait_ge(S["oev"], gq)
                        if gq >= 2:
                            sy.wait_ge(S["ost"], 16 * (gq - 1))
                        hp, qp = seq[gq - 1]
                        hfp, qq = qp // 4, qp % 4
                        sy.dma_start(
                            agin[hfp * EQ + hp * 128:
                                 hfp * EQ + (hp + 1) * 128,
                                 qq * QB:(qq + 1) * QB],
                            obs[:, ((gq - 1) % 2) * QB:((gq - 1) % 2 + 1) * QB]
                            ).then_inc(S["ost"], 16)
                sy.wait_ge(S["oev"], 64)
                sy.wait_ge(S["ost"], 16 * 63)
                sy.dma_start(agin[EQ + 7 * 128: EQ + 8 * 128, 3 * QB:4 * QB],
                             obs[:, (63 % 2) * QB:(63 % 2 + 1) * QB]
                             ).then_inc(S["ost"], 16)
                sy.wait_ge(S["ost"], 16 * 64)

            def kv_load(s, grp):
                h = grp % 8
                if grp >= 2:
                    hp, qp = seq[(grp - 1) * 4 - 1]
                    s.wait_ge(S["p2oz"], oz_end[(hp, qp)])
                if grp >= 1:
                    s.wait_ge(S["kvld"], 16 * grp)
                    s.wait_ge(S["kvv"], 16 * grp)
                kb = (grp % 2) * T
                vb = (grp % 2) * 32 * 128
                s.dma_start(khs[:, kb:kb + T], ks[h * 128:(h + 1) * 128, :]
                            ).then_inc(S["kvld"], 16)
                s.dma_start(_r(vhs[:, vb:vb + 32 * 128],
                               "p (k d) -> p k d", k=32),
                            _r(vs[:, :], "(k p) e -> p k e", p=128)
                            [:, :, h * 128:(h + 1) * 128]
                            ).then_inc(S["kvv"], 16)

            @block.scalar
            def _(s):
                kv_load(s, 0)
                for grp in range(16):
                    half, h = grp // 8, grp % 8
                    if grp >= 1:
                        s.wait_ge(S["denv"], 4 * grp)
                        s.wait_ge(S["vqld"], 16 * grp)
                    s.dma_start(vqhd[:, :], vqh[h:h + 1, :]
                                ).then_inc(S["vqld"], 16)
                    for j in range(4):
                        gq = grp * 4 + j
                        if j == 1 and grp + 1 < 16:
                            kv_load(s, grp + 1)
                        qb = half * 4 + j
                        for kt in range(nkt[qb]):
                            if sq_on_vector(kt):
                                continue
                            ga = ga_base[(h, qb)] + kt
                            s.wait_ge(S["p2ms"], ga + 1)
                            if ga >= 3:
                                s.wait_ge(S["p2oz"], 2 * (ga - 2))
                            s.activation(a_s[:, (ga % 3) * QB:(ga % 3 + 1) * QB],
                                         pS[ga % 3][:, :], AFT.Square,
                                         bias=0.0,
                                         scale=rsk[:, h * 32 + kt:
                                                   h * 32 + kt + 1]
                                         ).then_inc(S["sqa"], 1)
                            if kt == 1 and gq >= 1:
                                s.wait_ge(S["denv"], gq)
                                if gq >= 2:
                                    s.wait_ge(S["bcm"], gq - 1)
                                s.activation(lnr[:, :], den2[:, :], AFT.Ln,
                                             bias=0.0, scale=1.0)
                                s.drain()
                                s.activation(rcr[:, :], lnr[:, :], AFT.Exp,
                                             bias=0.0, scale=-1.0
                                             ).then_inc(S["rcs"], 1)
                s.wait_ge(S["denv"], 64)
                s.wait_ge(S["bcm"], 63)
                s.activation(lnr[:, :], den2[:, :], AFT.Ln, bias=0.0, scale=1.0)
                s.drain()
                s.activation(rcr[:, :], lnr[:, :], AFT.Exp, bias=0.0,
                             scale=-1.0).then_inc(S["rcs"], 1)

            @block.vector
            def _(v):
                v.wait_ge(S["mld"], 16)
                for grp in range(16):
                    half, h = grp // 8, grp % 8
                    for j in range(4):
                        qb = half * 4 + j
                        gq = grp * 4 + j
                        v.drain()
                        if j == 0:
                            v.wait_ge(S["vqld"], 16 * (grp + 1))
                            v.tensor_scalar(den1[:, :],
                                            vqhd[:, half * 4 * QB:
                                                 (half * 4 + 4) * QB],
                                            C1, C2, ALU.mult, ALU.add)
                            v.drain()
                        for kt in range(nkt[qb]):
                            ga = ga_base[(h, qb)] + kt
                            if sq_on_vector(kt):
                                v.wait_ge(S["p2ms"], ga + 1)
                                if ga >= 3:
                                    v.wait_ge(S["p2oz"], 2 * (ga - 2))
                                v.tensor_scalar(vsq[:, :], pS[ga % 3][:, :],
                                                rsk[:, h * 32 + kt:
                                                    h * 32 + kt + 1],
                                                None, ALU.mult, ALU.bypass)
                                v.drain()
                                v.tensor_tensor(
                                    a_s[:, (ga % 3) * QB:(ga % 3 + 1) * QB],
                                    vsq[:, :], vsq[:, :], ALU.mult
                                    ).then_inc(S["sqv"], 1)
                                v.drain()
                            if not is_masked(qb, kt):
                                continue
                            jm = kt - (nkt[qb] - 4)
                            if not sq_on_vector(kt):
                                wait_sq(v, ga)
                            v.tensor_tensor(
                                ams[:, (ga % 3) * QB:(ga % 3 + 1) * QB],
                                a_s[:, (ga % 3) * QB:(ga % 3 + 1) * QB],
                                masks[:, jm * QB:(jm + 1) * QB],
                                ALU.mult).then_inc(S["mka"], 1)
                        if gq >= 1:
                            gp = gq - 1
                            gpg, gpj = divmod(gp, 4)
                            v.wait_ge(S["p2oz"],
                                      oz_end[(gpg % 8, (gpg // 8) * 4 + gpj)])
                            v.wait_ge(S["bcm"], gp + 1)
                            v.tensor_copy(recb[:, :], pBC[:, :]
                                          ).then_inc(S["recv"], 1)
                            v.drain()
                            if gp >= 2:
                                v.wait_ge(S["ost"], 16 * (gp - 1))
                            v.tensor_tensor(obs[:, (gp % 2) * QB:
                                                (gp % 2 + 1) * QB],
                                            pOs[gp % 2][:, :], recb[:, :],
                                            ALU.mult).then_inc(S["oev"], 1)
                        v.wait_ge(S["p2oz"], oz_end[(h, qb)])
                        v.tensor_copy(zrow[:, :], pZs[gq % 2][:, :]
                                      ).then_inc(S["zrv"], 1)
                        v.drain()
                        if gq >= 1:
                            v.wait_ge(S["rcs"], gq)
                        v.tensor_tensor(den2[:, :],
                                        den1[:, j * QB:(j + 1) * QB],
                                        zrow[:, :],
                                        ALU.add).then_inc(S["denv"], 1)
                gp = 63
                v.drain()
                v.wait_ge(S["bcm"], gp + 1)
                v.tensor_copy(recb[:, :], pBC[:, :]).then_inc(S["recv"], 1)
                v.drain()
                v.wait_ge(S["ost"], 16 * (gp - 1))
                v.tensor_tensor(obs[:, (gp % 2) * QB:(gp % 2 + 1) * QB],
                                pOs[gp % 2][:, :], recb[:, :],
                                ALU.mult).then_inc(S["oev"], 1)

            @block.tensor
            def _(t):
                pend = []

                def mmoz(grp, h, qb, gq, kt):
                    ga = ga_base[(h, qb)] + kt
                    n = nkt[qb]
                    wait_sq(t, ga)
                    if is_masked(qb, kt):
                        t.wait_ge(S["mka"], mk_idx[(h, qb, kt)] + 1)
                        asrc = ams
                    else:
                        asrc = a_s
                    if kt == 0 and gq >= 2:
                        t.wait_ge(S["oev"], gq - 1)
                        t.wait_ge(S["zrv"], gq - 1)
                    t.matmul(pOs[gq % 2][:, :],
                             vhs[:, (grp % 2) * 32 * 128 + kt * 128:
                                 (grp % 2) * 32 * 128 + (kt + 1) * 128],
                             asrc[:, (ga % 3) * QB:(ga % 3 + 1) * QB],
                             start=(kt == 0), stop=(kt == n - 1)
                             ).then_inc(S["p2oz"], 1)
                    t.matmul(pZs[gq % 2][:, :], onesc[:, :],
                             asrc[:, (ga % 3) * QB:(ga % 3 + 1) * QB],
                             start=(kt == 0), stop=(kt == n - 1)
                             ).then_inc(S["p2oz"], 1)

                for grp in range(16):
                    half, h = grp // 8, grp % 8
                    t.wait_ge(S["kvld"], 16 * (grp + 1))
                    t.wait_ge(S["kvv"], 16 * (grp + 1))
                    for j in range(4):
                        qb = half * 4 + j
                        gq = grp * 4 + j
                        t.wait_ge(S["qld"], 16 * (gq + 1))
                        n = nkt[qb]

                        def bc(gq=gq):
                            t.wait_ge(S["rcs"], gq)
                            if gq >= 2:
                                t.wait_ge(S["recv"], gq - 1)
                            t.matmul(pBC[:, :], onesrf[:, :], rcr[:, :],
                                     start=True, stop=True
                                     ).then_inc(S["bcm"], 1)

                        for step in range(n):
                            if step == 4 and gq >= 1:
                                bc()
                            kt = step
                            ga = ga_base[(h, qb)] + kt
                            if ga >= 3:
                                wait_sq(t, ga - 3)
                            t.matmul(pS[ga % 3][:, :],
                                     khs[:, (grp % 2) * T + kt * 128:
                                         (grp % 2) * T + (kt + 1) * 128],
                                     qbs[:, (gq % 4) * QB:(gq % 4 + 1) * QB],
                                     start=True, stop=True
                                     ).then_inc(S["p2ms"], 1)
                            if step < 2 and pend:
                                pend.pop(0)()
                            if step >= 2:
                                mmoz(grp, h, qb, gq, step - 2)
                        if n <= 4 and gq >= 1:
                            bc()
                        pend.append(
                            (lambda a, b, c, d, e:
                             lambda: mmoz(a, b, c, d, e))(grp, h, qb, gq, n - 2))
                        pend.append(
                            (lambda a, b, c, d, e:
                             lambda: mmoz(a, b, c, d, e))(grp, h, qb, gq, n - 1))
                for f in pend:
                    f()
                gq = 64
                t.wait_ge(S["rcs"], gq)
                t.wait_ge(S["recv"], gq - 1)
                t.matmul(pBC[:, :], onesrf[:, :], rcr[:, :],
                         start=True, stop=True).then_inc(S["bcm"], 1)

    if 3 in phases:
        NT3 = TH // 128
        with (
            nc.sbuf_tensor("oms", [128, 16 * TH], BF16) as oms,
            nc.sbuf_tensor("xrs", [128, 2 * DM], F32) as xrs,
            nc.sbuf_tensor("z1s", [128, DM], F32) as z1s,
            nc.sbuf_tensor("bns", [128, 24], F32) as bns,
            nc.sbuf_tensor("mvs", [128, 2], F32) as mvs,
            nc.sbuf_tensor("l3a", [128, 1], F32) as l3a,
            nc.sbuf_tensor("l3b", [128, 1], F32) as l3b,
            nc.sbuf_tensor("l3c", [128, 1], F32) as l3c,
            nc.sbuf_tensor("l3d", [128, 1], F32) as l3d,
            nc.sbuf_tensor("x1s", [128, 2 * DM], BF16) as x1s,
            nc.sbuf_tensor("x1t", [128, DM], BF16) as x1t,
            nc.psum_tensor("pW0", [128, 512], F32) as pW0,
            nc.psum_tensor("pW1", [128, 512], F32) as pW1,
            nc.psum_tensor("pW2", [128, 512], F32) as pW2,
            nc.psum_tensor("pW3", [128, 512], F32) as pW3,
            nc.psum_tensor("pT0", [128, 128], BF16) as pT0,
            nc.psum_tensor("pT1", [128, 128], BF16) as pT1,
            nc.Block() as block,
        ):
            pW = [pW0, pW1, pW2, pW3]
            pT = [pT0, pT1]

            @block.gpsimd
            def _(g):
                gsems = [S["gth"], S["gth1"], S["gth2"], S["gth3"]]
                cnt = [0, 0, 0, 0]
                prev_req = 0
                for idx, gi in enumerate(P3_A + P3_B + P3_C + P3_D):
                    req = 3 if idx < 8 else 4 if idx < 12 else \
                        5 if idx < 14 else 6
                    if req != prev_req:
                        g.wait_ge(S["agc"], req)
                        prev_req = req
                    sm = idx % 4 if idx < 12 else \
                        (idx - 12) if idx < 14 else (idx - 12)
                    if cnt[sm] >= 1:
                        g.wait_ge(gsems[sm], 16 * cnt[sm])
                    g.indirect_dma_start(
                        oms[:, gi * TH:(gi + 1) * TH], None,
                        agout[:, :],
                        IndirectOffsetOnAxis(ap=oidx[:, gi:gi + 1], axis=0),
                    ).then_inc(gsems[sm], 16)
                    cnt[sm] += 1

            @block.sync
            def _(sy):
                for tt in range(NT3):
                    if tt >= 2:
                        sy.wait_ge(S["z1v"], 4 * (tt - 1))
                    if tt >= 1:
                        sy.wait_ge(S["xrl"], 16 * tt)
                    sy.dma_start(xrs[:, (tt % 2) * DM:(tt % 2 + 1) * DM],
                                 xres[tt * 128:(tt + 1) * 128, :]
                                 ).then_inc(S["xrl"], 16)

            @block.tensor
            def _(t):
                for s4 in ("gth", "gth1", "gth2", "gth3"):
                    t.wait_ge(S[s4], 32)
                t.wait_ge(S["wol"], 16)
                for tt in range(NT3):
                    def wo_mm(eb, fc, start, stop):
                        t.matmul(pW[eb][:, :],
                                 oms[:, fc * TH + tt * 128:
                                     fc * TH + (tt + 1) * 128],
                                 wos[:, fc * DM + eb * 512:
                                     fc * DM + (eb + 1) * 512],
                                 start=start, stop=stop
                                 ).then_inc(S["p3mm"], 1)

                    for eb in range(4):
                        if tt >= 1:
                            t.wait_ge(S["z1v"], 4 * (tt - 1) + eb + 1)
                        for k, fc in enumerate(P3_A):
                            wo_mm(eb, fc, k == 0, False)
                    if tt == 0:
                        for s4 in ("gth", "gth1", "gth2", "gth3"):
                            t.wait_ge(S[s4], 48)
                    for eb in range(4):
                        for fc in P3_B:
                            wo_mm(eb, fc, False, False)
                    if tt == 0:
                        t.wait_ge(S["gth"], 64)
                        t.wait_ge(S["gth1"], 64)
                    for eb in range(4):
                        for fc in P3_C:
                            wo_mm(eb, fc, False, False)
                    if tt == 0:
                        t.wait_ge(S["gth2"], 64)
                        t.wait_ge(S["gth3"], 64)
                    for eb in range(4):
                        for k, fc in enumerate(P3_D):
                            wo_mm(eb, fc, False, k == len(P3_D) - 1)
                    if tt >= 2:
                        t.wait_ge(S["apv"], tt - 1)
                        for dc in range(16):
                            gtr = 16 * (tt - 2) + dc
                            if gtr >= 2:
                                t.wait_ge(S["trv"], gtr - 1)
                            t.transpose(pT[gtr % 2][:, :],
                                        x1s[:, ((tt - 2) % 2) * DM + dc * 128:
                                            ((tt - 2) % 2) * DM + (dc + 1) * 128],
                                        ident[:, :]).then_inc(S["trm"], 1)
                for tl in (NT3 - 2, NT3 - 1):
                    t.wait_ge(S["apv"], tl + 1)
                    for dc in range(16):
                        gtr = 16 * tl + dc
                        if gtr >= 2:
                            t.wait_ge(S["trv"], gtr - 1)
                        t.transpose(pT[gtr % 2][:, :],
                                    x1s[:, (tl % 2) * DM + dc * 128:
                                        (tl % 2) * DM + (dc + 1) * 128],
                                    ident[:, :]).then_inc(S["trm"], 1)

            @block.vector
            def _(v):
                for tt in range(NT3):
                    v.drain()
                    v.wait_ge(S["xrl"], 16 * (tt + 1))
                    for eb in range(4):
                        v.wait_ge(S["p3mm"], tt * 64 + 56 + 2 * (eb + 1))
                        v.tensor_tensor(z1s[:, eb * 512:(eb + 1) * 512],
                                        pW[eb][:, :],
                                        xrs[:, (tt % 2) * DM + eb * 512:
                                            (tt % 2) * DM + (eb + 1) * 512],
                                        ALU.add).then_inc(S["z1v"], 1)
                    v.drain()
                    for eb in range(4):
                        v.bn_stats(bns[:, eb * 6:(eb + 1) * 6],
                                   z1s[:, eb * 512:(eb + 1) * 512])
                    v.drain()
                    v.bn_aggr(mvs[:, :], bns[:, :])
                    v.drain()
                    v.tensor_scalar(l3a[:, :], mvs[:, 1:2], EPS, None,
                                    ALU.add, ALU.bypass).then_inc(S["lnv"], 1)
                    v.wait_ge(S["p3sq"], tt + 1)
                    v.reciprocal(l3c[:, :], l3b[:, :]).then_inc(S["l3u"], 1)
                    v.drain()
                    v.tensor_scalar(l3d[:, :], mvs[:, 0:1], l3c[:, :], -1.0,
                                    ALU.mult, ALU.mult)
                    v.drain()
                    if tt >= 2:
                        v.wait_ge(S["trm"], 16 * (tt - 1))
                    for eb in range(4):
                        v.tensor_scalar(x1s[:, (tt % 2) * DM + eb * 512:
                                            (tt % 2) * DM + (eb + 1) * 512],
                                        z1s[:, eb * 512:(eb + 1) * 512],
                                        l3c[:, :], l3d[:, :],
                                        ALU.mult, ALU.add)
                    v.engine_nop().then_inc(S["apv"], 1)


            @block.scalar
            def _(s):
                def trevac(s, tl):
                    if tl >= 1:
                        s.wait_ge(S["x1st"], 16 * tl)
                    for dc in range(16):
                        gtr = 16 * tl + dc
                        s.wait_ge(S["trm"], gtr + 1)
                        s.copy(x1t[:, dc * 128:(dc + 1) * 128],
                               pT[gtr % 2][:, :]).then_inc(S["trv"], 1)
                    s.wait_ge(S["trv"], 16 * (tl + 1))
                    s.dma_start(
                        _r(x1f[:, tl * 128:(tl + 1) * 128],
                           "(c p) t -> p c t", p=128),
                        _r(x1t[:, :], "p (c t) -> p c t", c=16)
                    ).then_inc(S["x1st"], 16)

                for tt in range(NT3):
                    s.wait_ge(S["lnv"], tt + 1)
                    if tt >= 1:
                        s.wait_ge(S["l3u"], tt)
                    s.sqrt(l3b[:, :], l3a[:, :]).then_inc(S["p3sq"], 1)
                    if tt >= 2:
                        trevac(s, tt - 2)
                trevac(s, NT3 - 2)
                trevac(s, NT3 - 1)
                s.wait_ge(S["x1st"], 16 * NT3)

    if wos_cm is not None:
        wos_cm.__exit__(None, None, None)

    if 4 in phases:
        NHC2 = 22
        NEB = 4

        def p4_dmm_after_chunk4(gw):
            tbw, rem = divmod(gw, NEB * 11)
            ebw, hgw = divmod(rem, 11)
            return tbw * 1536 + ebw * 384 + (hgw + 1) * 32

        def p4_dmm_group_end(tb, eb, tt):
            return tb * 1536 + eb * 384 + 352 + (tt + 1) * 4

        def gu_done(tb, hc2, t2, hsub, part):
            return tb * 2816 + hc2 * 128 + t2 * 64 + hsub * 32 + 16 * part

        with contextlib.ExitStack() as _st:
            _e = _st.enter_context
            identE = _e(nc.sbuf_tensor("identE_s", [128, 4 * 512], BF16))
            x1b = _e(nc.sbuf_tensor("x1b", [128, 16 * FTB], BF16))
            wgs = _e(nc.sbuf_tensor("wgs", [128, 3 * 16 * 256], BF16))
            wgu8 = _e(nc.sbuf_tensor("wgu8", [128, 16 * 256], U8))
            wus = _e(nc.sbuf_tensor("wus", [128, 3 * 16 * 256], BF16))
            wuu8 = _e(nc.sbuf_tensor("wuu8", [128, 16 * 256], U8))
            gsb = _e(nc.sbuf_tensor("gsb", [128, NHC * FTB], BF16))
            gst = _e(nc.sbuf_tensor("gst", [128, 2 * 512], BF16))
            ust = _e(nc.sbuf_tensor("ust", [128, 2 * 512], BF16))
            wds = _e(nc.sbuf_tensor("wds", [128, 2 * 4 * 512], BF16))
            wdu8 = _e(nc.sbuf_tensor("wdu8", [128, 4 * 512], U8))
            dstg = _e(nc.sbuf_tensor("dstg", [128, 2 * 512], F32))
            pP0 = _e(nc.psum_tensor("pP0", [128, 512], F32))
            pP1 = _e(nc.psum_tensor("pP1", [128, 512], F32))
            pP2 = _e(nc.psum_tensor("pP2", [128, 512], F32))
            pP3 = _e(nc.psum_tensor("pP3", [128, 512], F32))
            pP4 = _e(nc.psum_tensor("pP4", [128, 512], F32))
            pP5 = _e(nc.psum_tensor("pP5", [128, 512], F32))
            pP6 = _e(nc.psum_tensor("pP6", [128, 512], F32))
            pP7 = _e(nc.psum_tensor("pP7", [128, 512], F32))
            block = _e(nc.Block())
            True
            pP = [pP0, pP1, pP2, pP3, pP4, pP5, pP6, pP7]
            pGs = [pP0, pP1, pP2]
            pUs = [pP3, pP4, pP5]

            @block.scalar
            def _(s):
                s.dma_start(identE[:, :], identEd[:, :]).then_inc(S["ield"], 16)
                for tb in range(2):
                    if tb >= 1:
                        s.wait_ge(S["gumm"], 2816)
                        s.wait_ge(S["dmm"], 1536)
                        s.wait_ge(S["x1l"], 16)
                    s.dma_start(
                        _r(x1b[:, :], "p (c t) -> p c t", c=16),
                        _r(x1f[:, tb * FTB:(tb + 1) * FTB],
                           "(c p) t -> p c t", p=128)).then_inc(S["x1l"], 16)
                    for hc2 in range(NHC2):
                        for t2 in range(2):
                            for hsub in range(2):
                                gt = tb * 88 + hc2 * 4 + t2 * 2 + hsub
                                hc = hc2 * 2 + hsub
                                s.wait_ge(S["gumm"],
                                          gu_done(tb, hc2, t2, hsub, 1))
                                if gt >= 2:
                                    s.wait_ge(S["gml"], gt - 1)
                                s.activation(gst[:, (gt % 2) * 512:
                                                 (gt % 2 + 1) * 512],
                                             pGs[gt % 3][:, :], AFT.Silu,
                                             bias=0.0, scale=sg[:, hc:hc + 1]
                                             ).then_inc(S["sil"], 1)

            @block.gpsimd
            def _(g):
                for tb in range(2):
                    for hc2 in range(NHC2):
                        gc = tb * NHC2 + hc2
                        if gc >= 1:
                            g.wait_ge(S["deq"], gc)
                            g.wait_ge(S["wldg"], 32 * gc)
                        sl8 = 0
                        g.dma_start(
                            wgu8[:, sl8:sl8 + 16 * 256],
                            wgT[:, hc2 * 16 * 256:(hc2 + 1) * 16 * 256]
                            ).then_inc(S["wldg"], 16)
                        g.dma_start(
                            wuu8[:, sl8:sl8 + 16 * 256],
                            wuT[:, hc2 * 16 * 256:(hc2 + 1) * 16 * 256]
                            ).then_inc(S["wldg"], 16)
                    for eb in range(NEB):
                        for hg4 in range(11):
                            gw = tb * NEB * 11 + eb * 11 + hg4
                            if gw >= 1:
                                g.wait_ge(S["wdc"], gw)
                                g.wait_ge(S["wdl"], 16 * gw)
                            g.dma_start(
                                wdu8[:, 0:2048],
                                wdT[:, (eb * NHC + hg4 * 4) * 512:
                                    (eb * NHC + hg4 * 4 + 4) * 512]
                                ).then_inc(S["wdl"], 16)

            @block.vector
            def _(v):
                def deq_chunk(gc):
                    v.wait_ge(S["wldg"], 32 * (gc + 1))
                    if gc >= 3:
                        v.wait_ge(S["gumm"], (gc - 2) * 128)
                    sl8 = 0
                    slb = (gc % 3) * 16 * 256
                    v.tensor_scalar(wgs[:, slb:slb + 16 * 256],
                                    wgu8[:, sl8:sl8 + 16 * 256],
                                    -128.0, None, ALU.add, ALU.bypass)
                    v.drain()
                    v.tensor_scalar(wus[:, slb:slb + 16 * 256],
                                    wuu8[:, sl8:sl8 + 16 * 256],
                                    -128.0, None, ALU.add, ALU.bypass
                                    ).then_inc(S["deq"], 1)

                for tb in range(2):
                    deq_chunk(tb * NHC2 + 0)
                    deq_chunk(tb * NHC2 + 1)
                    deq_chunk(tb * NHC2 + 2)
                    for hc2 in range(NHC2):
                        gc = tb * NHC2 + hc2
                        for t2 in range(2):
                            for hsub in range(2):
                                gt = tb * 88 + hc2 * 4 + t2 * 2 + hsub
                                hc = hc2 * 2 + hsub
                                v.wait_ge(S["gumm"],
                                          gu_done(tb, hc2, t2, hsub, 2))
                                if gt >= 2:
                                    v.wait_ge(S["gml"], gt - 1)
                                v.tensor_scalar(ust[:, (gt % 2) * 512:
                                                    (gt % 2 + 1) * 512],
                                                pUs[gt % 3][:, :],
                                                su[:, hc:hc + 1], None,
                                                ALU.mult, ALU.bypass)
                                v.wait_ge(S["sil"], gt + 1)
                                v.drain()
                                v.tensor_tensor(
                                    gsb[:, hc * FTB + t2 * 512:
                                        hc * FTB + (t2 + 1) * 512],
                                    gst[:, (gt % 2) * 512:(gt % 2 + 1) * 512],
                                    ust[:, (gt % 2) * 512:(gt % 2 + 1) * 512],
                                    ALU.mult).then_inc(S["gml"], 1)
                        if hc2 + 3 < NHC2:
                            deq_chunk(gc + 3)
                    for eb in range(NEB):
                        for hg4 in range(11):
                            gw = tb * NEB * 11 + eb * 11 + hg4
                            v.wait_ge(S["wdl"], 16 * (gw + 1))
                            if gw >= 2:
                                v.wait_ge(S["dmm"],
                                          p4_dmm_after_chunk4(gw - 2))
                            sw = (gw % 2) * 2048
                            v.tensor_scalar(wds[:, sw:sw + 2048],
                                            wdu8[:, 0:2048],
                                            -128.0, None, ALU.add, ALU.bypass
                                            ).then_inc(S["wdc"], 1)
                            v.drain()
                            for q4 in range(4):
                                v.tensor_tensor(
                                    wds[:, sw + q4 * 512:sw + (q4 + 1) * 512],
                                    wds[:, sw + q4 * 512:sw + (q4 + 1) * 512],
                                    sdbc[:, eb * 512:(eb + 1) * 512],
                                    ALU.mult).then_inc(S["wdq"], 1)
                        for tt in range(8):
                            ge = tb * 32 + eb * 8 + tt
                            v.wait_ge(S["dmm"], p4_dmm_group_end(tb, eb, tt))
                            if ge >= 2:
                                v.wait_ge(S["dst"], 16 * (ge - 1))
                            sidx = ((tb * 8 + tt) * 4 + eb) * 6
                            v.bn_stats(bst5[:, sidx:sidx + 6], pP[tt][:, :])
                            v.drain()
                            v.tensor_copy(dstg[:, (ge % 2) * 512:
                                              (ge % 2 + 1) * 512],
                                          pP[tt][:, :]).then_inc(S["dev"], 1)

            @block.tensor
            def _(t):
                t.wait_ge(S["ield"], 16)
                for tb in range(2):
                    t.wait_ge(S["x1l"], 16 * (tb + 1))
                    if tb >= 1:
                        t.wait_ge(S["dev"], 32 * tb)
                    for hc2 in range(NHC2):
                        gc = tb * NHC2 + hc2
                        t.wait_ge(S["deq"], gc + 1)
                        sl = (gc % 3) * 16 * 256
                        for t2 in range(2):
                            for hsub in range(2):
                                gt = tb * 88 + hc2 * 4 + t2 * 2 + hsub
                                if gt >= 3:
                                    t.wait_ge(S["sil"], gt - 2)
                                    t.wait_ge(S["gml"], gt - 2)
                                for fc in range(16):
                                    t.matmul(pGs[gt % 3][:, :],
                                             wgs[:, sl + fc * 256 + hsub * 128:
                                                 sl + fc * 256 + (hsub + 1) * 128],
                                             x1b[:, fc * FTB + t2 * 512:
                                                 fc * FTB + (t2 + 1) * 512],
                                             start=(fc == 0), stop=(fc == 15)
                                             ).then_inc(S["gumm"], 1)
                                for fc in range(16):
                                    t.matmul(pUs[gt % 3][:, :],
                                             wus[:, sl + fc * 256 + hsub * 128:
                                                 sl + fc * 256 + (hsub + 1) * 128],
                                             x1b[:, fc * FTB + t2 * 512:
                                                 fc * FTB + (t2 + 1) * 512],
                                             start=(fc == 0), stop=(fc == 15)
                                             ).then_inc(S["gumm"], 1)
                    for eb in range(NEB):
                        for hc in range(NHC):
                            hg4, q4 = divmod(hc, 4)
                            gw = tb * NEB * 11 + eb * 11 + hg4
                            t.wait_ge(S["wdq"], 4 * gw + q4 + 1)
                            if hc == 0:
                                geb = tb * NEB + eb
                                if geb >= 1:
                                    t.wait_ge(S["dev"], (geb - 1) * 8 + 8)
                                t.wait_ge(S["gml"], tb * 88 + 88)
                            sw = (gw % 2) * 2048 + q4 * 512
                            for tt in range(8):
                                t.matmul(pP[tt][:, :],
                                         gsb[:, hc * FTB + tt * 128:
                                             hc * FTB + (tt + 1) * 128],
                                         wds[:, sw:sw + 512],
                                         start=(hc == 0), stop=False
                                         ).then_inc(S["dmm"], 1)
                        for tt in range(8):
                            for j in range(4):
                                dc = eb * 4 + j
                                t.matmul(pP[tt][:, :],
                                         x1b[:, dc * FTB + tt * 128:
                                             dc * FTB + (tt + 1) * 128],
                                         identE[:, j * 512:(j + 1) * 512],
                                         start=False, stop=(j == 3)
                                         ).then_inc(S["dmm"], 1)

            @block.sync
            def _(sy):
                for tb in range(2):
                    for eb in range(NEB):
                        for tt in range(8):
                            ge = tb * 32 + eb * 8 + tt
                            sy.wait_ge(S["dev"], ge + 1)
                            if ge >= 1:
                                sy.wait_ge(S["dst"], 16 * ge)
                            sy.dma_start(
                                opre[tb * FTB + tt * 128:
                                     tb * FTB + (tt + 1) * 128,
                                     eb * 512:(eb + 1) * 512],
                                dstg[:, (ge % 2) * 512:(ge % 2 + 1) * 512]
                                ).then_inc(S["dst"], 16)
                sy.wait_ge(S["dst"], 16 * 64)

    if 5 in phases:
        NT5 = TH // 128
        with (
            nc.sbuf_tensor("ops", [128, 2 * DM], F32) as ops,
            nc.sbuf_tensor("bn5", [128, 24], F32) as bn5,
            nc.sbuf_tensor("mv5", [128, 2], F32) as mv5,
            nc.sbuf_tensor("l5a", [128, 1], F32) as l5a,
            nc.sbuf_tensor("l5b", [128, 1], F32) as l5b,
            nc.sbuf_tensor("l5c", [128, 1], F32) as l5c,
            nc.sbuf_tensor("l5d", [128, 1], F32) as l5d,
            nc.sbuf_tensor("o5s", [128, 2 * DM], F32) as o5s,
            nc.Block() as block,
        ):
            @block.sync
            def _(sy):
                for tt in range(NT5):
                    if tt >= 1:
                        sy.wait_ge(S["p5ap"], tt)
                        if tt >= 2:
                            sy.wait_ge(S["p5st"], 16 * (tt - 1))
                        sy.dma_start(out[(tt - 1) * 128: tt * 128, :],
                                     o5s[:, ((tt - 1) % 2) * DM:
                                         ((tt - 1) % 2 + 1) * DM]
                                     ).then_inc(S["p5st"], 16)
                sy.wait_ge(S["p5ap"], NT5)
                sy.wait_ge(S["p5st"], 16 * (NT5 - 1))
                sy.dma_start(out[(NT5 - 1) * 128: NT5 * 128, :],
                             o5s[:, ((NT5 - 1) % 2) * DM:
                                 ((NT5 - 1) % 2 + 1) * DM]
                             ).then_inc(S["p5st"], 16)
                sy.wait_ge(S["p5st"], 16 * NT5)

            @block.vector
            def _(v):
                for tt in range(NT5):
                    v.drain()
                    v.wait_ge(S["p5l"], 16 * (tt + 1))
                    v.bn_aggr(mv5[:, :], bst5[:, tt * 24:(tt + 1) * 24])
                    v.drain()
                    v.tensor_scalar(l5a[:, :], mv5[:, 1:2], EPS, None,
                                    ALU.add, ALU.bypass).then_inc(S["p5bn"], 1)
                    v.wait_ge(S["p5sq"], tt + 1)
                    v.reciprocal(l5c[:, :], l5b[:, :]).then_inc(S["l5u"], 1)
                    v.drain()
                    v.tensor_scalar(l5d[:, :], mv5[:, 0:1], l5c[:, :], -1.0,
                                    ALU.mult, ALU.mult)
                    v.drain()
                    if tt >= 2:
                        v.wait_ge(S["p5st"], 16 * (tt - 1))
                    for eb in range(4):
                        v.tensor_scalar(o5s[:, (tt % 2) * DM + eb * 512:
                                            (tt % 2) * DM + (eb + 1) * 512],
                                        ops[:, (tt % 2) * DM + eb * 512:
                                            (tt % 2) * DM + (eb + 1) * 512],
                                        l5c[:, :], l5d[:, :],
                                        ALU.mult, ALU.add)
                    v.engine_nop().then_inc(S["p5ap"], 1)

            @block.scalar
            def _(s):
                for tt in range(NT5):
                    if tt >= 2:
                        s.wait_ge(S["p5ap"], tt - 1)
                    if tt >= 1:
                        s.wait_ge(S["p5l"], 16 * tt)
                    s.dma_start(ops[:, (tt % 2) * DM:(tt % 2 + 1) * DM],
                                opre[tt * 128:(tt + 1) * 128, :]
                                ).then_inc(S["p5l"], 16)
                    s.wait_ge(S["p5bn"], tt + 1)
                    if tt >= 1:
                        s.wait_ge(S["l5u"], tt)
                    s.sqrt(l5b[:, :], l5a[:, :]).then_inc(S["p5sq"], 1)

    if dbg:
        with nc.Block() as block:
            @block.gpsimd
            def _(g):
                n = 0
                for name, (dt_, srcd) in dbg.items():
                    if n:
                        g.wait_ge(S["dbgs"], n)
                    g.dma_start(dt_[:, :], srcd[:, :]).then_inc(S["dbgs"], 16)
                    n += 16
                g.wait_ge(S["dbgs"], n)

    return nc



def _consts():
    eye = np.eye(128, dtype=np.float32)
    cm = np.asarray(eye - 1.0 / 128.0, np.float32).astype(ml_dtypes.bfloat16)
    idm = eye.astype(ml_dtypes.bfloat16)
    r = np.arange(128)[:, None]
    c = np.arange(QB)[None, :]
    masks = np.concatenate(
        [(c >= r + 128 * j).astype(np.float32) for j in range(4)],
        axis=1).astype(ml_dtypes.bfloat16)
    identE = np.zeros((128, 4 * 512), np.float32)
    for j in range(4):
        identE[np.arange(128), j * 512 + j * 128 + np.arange(128)] = 1.0
    identE = identE.astype(ml_dtypes.bfloat16)
    onescol = np.ones((128, 1), ml_dtypes.bfloat16)
    onesrow = np.ones((1, 128), ml_dtypes.bfloat16)
    return cm, idm, masks, identE, onescol, onesrow


def make_in_maps(inputs):
    x = np.asarray(inputs["L0_attn_hidden_states"], np.float32)
    Wq = np.asarray(inputs["L0_attn_Wq"], np.float32)
    Wk = np.asarray(inputs["L0_attn_Wk"], np.float32)
    Wv = np.asarray(inputs["L0_attn_Wv"], np.float32)
    Wo = np.asarray(inputs["L0_attn_Wo"], np.float32)
    gamma = np.asarray(inputs["L0_attn_gamma"], np.float32)
    beta = np.asarray(inputs["L0_attn_beta"], np.float32)
    wqg = np.asarray(inputs["L0_ffn_wq_gate"], np.int32)
    sgv = np.asarray(inputs["L0_ffn_s_gate"], np.float32)
    wqu = np.asarray(inputs["L0_ffn_wq_up"], np.int32)
    suv = np.asarray(inputs["L0_ffn_s_up"], np.float32)
    wqd = np.asarray(inputs["L0_ffn_wq_down"], np.int32)
    sdv = np.asarray(inputs["L0_ffn_s_down"], np.float32)

    cm, idm, masks, identE, onescol, onesrow = _consts()
    woTh = np.ascontiguousarray(
        Wo.T.reshape(16, 128, DM).transpose(1, 0, 2).reshape(128, 16 * DM))
    wgTh = np.ascontiguousarray(
        wqg.T.reshape(16, 128, 22, 256).transpose(1, 2, 0, 3)
        .reshape(128, 22 * 16 * 256).astype(np.uint8))
    wuTh = np.ascontiguousarray(
        wqu.T.reshape(16, 128, 22, 256).transpose(1, 2, 0, 3)
        .reshape(128, 22 * 16 * 256).astype(np.uint8))
    wdTh = np.ascontiguousarray(
        wqd.T.reshape(NHC, 128, 4, 512).transpose(1, 2, 0, 3)
        .reshape(128, 4 * NHC * 512).astype(np.uint8))
    sg2 = np.ascontiguousarray(sgv.reshape(NHC, 128).T)
    su2 = np.ascontiguousarray(suv.reshape(NHC, 128).T)
    sdr = np.ascontiguousarray(sdv[None, :])

    maps = []
    for c in range(8):
        b, hg = c // 2, c % 2
        p = np.arange(128, dtype=np.int64)[:, None]
        i = np.arange(16, dtype=np.int64)[None, :]
        slot = 2 * b + i // 8
        cum = np.asarray(AG_CUM, dtype=np.int64)
        grp_row = hg * 8 + (i % 8)
        ci = np.searchsorted(cum, grp_row, side="right") - 1
        rows = (cum[ci + 1] - cum[ci]) * 128
        oix = (cum[ci] * 128 * 8 + slot * rows
               + (grp_row - cum[ci]) * 128 + p).astype(np.uint32)
        maps.append({
            "xT": np.ascontiguousarray(
                x[b].T.reshape(16, 128, NTB, TB).transpose(1, 2, 0, 3)
                .reshape(128, NTB * 16 * TB)),
            "xres": np.ascontiguousarray(x[b, hg * TH:(hg + 1) * TH]),
            "wq": np.ascontiguousarray(Wq[hg * EQ:(hg + 1) * EQ]),
            "wk": np.ascontiguousarray(Wk[hg * EQ:(hg + 1) * EQ]),
            "wvT": np.ascontiguousarray(
                Wv[hg * EQ:(hg + 1) * EQ].T.reshape(16, 128, EQ)
                .transpose(1, 0, 2).reshape(128, 16 * EQ)),
            "woT": woTh, "wgT": wgTh, "wuT": wuTh, "wdT": wdTh,
            "sg": sg2, "su": su2, "sdrow": sdr,
            "gam": np.ascontiguousarray(gamma[:, None]),
            "bet": np.ascontiguousarray(beta[:, None]),
            "cmat": cm, "identm": idm, "masks": masks, "identE": identE,
            "onescol": onescol, "onesrow": onesrow,
            "onesrowf": np.ones((1, 128), np.float32), "oidx": oix,
        })
    return maps


def kernel(**inputs):
    maps = make_in_maps(inputs)
    nc = build()
    r = run_bass_kernel_spmd(nc, maps, core_ids=list(range(8)))
    outf = np.empty((4, T, DM), np.float32)
    for c in range(8):
        b, hg = c // 2, c % 2
        outf[b, hg * TH:(hg + 1) * TH] = np.asarray(r.results[c]["out"])
    return outf


# version 23
# speedup vs baseline: 1.0417x; 1.0019x over previous
import contextlib
import numpy as np
import ml_dtypes
import concourse.bass as bass
import concourse.mybir as mybir
from concourse.bass_utils import run_bass_kernel_spmd
from concourse.bass_core_rust import IndirectOffsetOnAxis

F32 = mybir.dt.float32
U8 = mybir.dt.uint8
BF16 = mybir.dt.bfloat16
I32 = mybir.dt.int32
U32 = mybir.dt.uint32
AFT = mybir.ActivationFunctionType
ALU = mybir.AluOpType

T = 4096
DM = 2048
NH = 8
D = 128
EQ = NH * D
HID = 5632
NHC = HID // 128
TH = 2048
TB = 512
NTB = T // TB
QB = 512
NQB = T // QB
FTB = 1024
EPS = 1e-5
DEPS = float(128 * EPS)
AG_CUM = [0, 4, 8, 12, 14, 15, 16]
P3_A = [0, 1, 2, 3, 8, 9, 10, 11]
P3_B = [4, 5, 12, 13]
P3_C = [6, 14]
P3_D = [7, 15]
C1 = float(EPS / 128.0)
C2 = float(EPS * EPS)


def _r(ap, pat, **kw):
    return ap.rearrange(pat, **kw)


def build(phases=(1, 2, 3, 4, 5), debug_outs=()):
    nc = bass.Bass("TRN2", target_bir_lowering=False)

    xT = nc.dram_tensor("xT", [128, NTB * 16 * TB], F32, kind="ExternalInput")
    xres = nc.dram_tensor("xres", [TH, DM], F32, kind="ExternalInput")
    wqd_ = nc.dram_tensor("wq", [EQ, DM], F32, kind="ExternalInput")
    wkd_ = nc.dram_tensor("wk", [EQ, DM], F32, kind="ExternalInput")
    wvT = nc.dram_tensor("wvT", [128, 16 * EQ], F32, kind="ExternalInput")
    woT = nc.dram_tensor("woT", [128, 16 * DM], F32, kind="ExternalInput")
    wgT = nc.dram_tensor("wgT", [128, 22 * 16 * 256], U8, kind="ExternalInput")
    wuT = nc.dram_tensor("wuT", [128, 22 * 16 * 256], U8, kind="ExternalInput")
    wdT = nc.dram_tensor("wdT", [128, 4 * NHC * 512], U8, kind="ExternalInput")
    sgd = nc.dram_tensor("sg", [128, NHC], F32, kind="ExternalInput")
    sud = nc.dram_tensor("su", [128, NHC], F32, kind="ExternalInput")
    sdrow = nc.dram_tensor("sdrow", [1, DM], F32, kind="ExternalInput")
    gamd = nc.dram_tensor("gam", [128, 1], F32, kind="ExternalInput")
    betd = nc.dram_tensor("bet", [128, 1], F32, kind="ExternalInput")
    cmatd = nc.dram_tensor("cmat", [128, 128], BF16, kind="ExternalInput")
    identd = nc.dram_tensor("identm", [128, 128], BF16, kind="ExternalInput")
    masksd = nc.dram_tensor("masks", [128, 4 * QB], BF16, kind="ExternalInput")
    identEd = nc.dram_tensor("identE", [128, 4 * 512], BF16, kind="ExternalInput")
    onescd = nc.dram_tensor("onescol", [128, 1], BF16, kind="ExternalInput")
    onesrd = nc.dram_tensor("onesrow", [1, 128], BF16, kind="ExternalInput")
    onesrfd = nc.dram_tensor("onesrowf", [1, 128], F32, kind="ExternalInput")
    oidxd = nc.dram_tensor("oidx", [128, 16], U32, kind="ExternalInput")
    out = nc.dram_tensor("out", [TH, DM], F32, kind="ExternalOutput")

    qs = nc.dram_tensor("qs", [EQ, T], BF16)
    ks = nc.dram_tensor("ks", [EQ, T], BF16)
    vs = nc.dram_tensor("vs", [128, NH * T], BF16)
    vqh = nc.dram_tensor("vqh", [NH, T], F32)
    agin = nc.dram_tensor("agin", [2 * EQ, TH], BF16)
    agout = nc.dram_tensor("agout", [8 * 2 * EQ, TH], BF16)
    x1f = nc.dram_tensor("x1f", [DM, TH], BF16)
    opre = nc.dram_tensor("opre", [TH, DM], F32)

    dbg = {}
    dbg_specs = {"d_qs": (qs, BF16), "d_ks": (ks, BF16), "d_vs": (vs, BF16),
                 "d_vqh": (vqh, F32), "d_agin": (agin, BF16),
                 "d_agout": (agout, BF16), "d_x1f": (x1f, BF16),
                 "d_opre": (opre, F32)}
    for name in debug_outs:
        src, dt = dbg_specs[name]
        dbg[name] = (nc.dram_tensor(name, list(src.shape), dt,
                                    kind="ExternalOutput"), src)

    keep = []

    def sb(name, shape, dtype):
        h = nc.alloc_sbuf_tensor(name, shape, dtype)
        keep.append(h)
        return h

    SEM_NAMES = [
        "c_ld", "c_mm", "c_ev", "wld", "wsc", "wmm", "wev",
        "xld", "p1mm", "sq2", "vev", "vqm", "vqe", "ksm", "rsa", "rsq", "rsd",
        "spq", "spk", "spv", "spvq",
        "kvld", "qld", "vqld", "p2ms", "sqa", "mka", "p2oz", "zrv", "denv",
        "bcm", "recv", "oev", "ost",
        "agc", "gth", "wol", "xrl", "p3mm", "z1v", "lnv", "p3sq", "apv",
        "trm", "trv", "x1st",
        "x1l", "wldg", "deq", "gumm", "sil", "gml",
        "wdl", "wdq", "dmm", "dev", "dst",
        "p5l", "p5bn", "p5sq", "p5ap", "p5st", "l3u", "l5u", "sqv", "kvv", "wdc", "mld", "ield", "gth1", "gth2", "gth3", "rcs",
        "dbgs",
    ]
    S = {n: nc.alloc_semaphore(name=n) for n in SEM_NAMES}

    cmat = sb("cmat_s", [128, 128], BF16)
    ident = sb("ident_s", [128, 128], BF16)
    onesc = sb("onesc_s", [128, 1], BF16)
    onesr = sb("onesr_s", [1, 128], BF16)
    onesrf = sb("onesrf_s", [1, 128], F32)
    oidx = sb("oidx_s", [128, 16], U32)
    gam = sb("gam_s", [128, 1], F32)
    betbf = sb("betbf_s", [128, 1], BF16)
    cbeta = sb("cbeta_s", [128, 1], F32)
    sg = sb("sg_s", [128, NHC], F32)
    su = sb("su_s", [128, NHC], F32)
    sdbc = sb("sdbc_s", [128, DM], BF16)
    rsk = sb("rsk_s", [128, NH * 32], F32)
    bst5 = sb("bst5_s", [128, 64 * 6], F32)

    with (
        nc.sbuf_tensor("wtq", [128, 16 * EQ], BF16) as wtq,
        nc.sbuf_tensor("wtk", [128, 16 * EQ], BF16) as wtk,
        nc.sbuf_tensor("wvs", [128, 16 * EQ], BF16) as wvs,
    ):
        with (
            nc.sbuf_tensor("sdbf_s", [1, DM], BF16) as sdbf,
            nc.sbuf_tensor("wraw", [128, DM], BF16) as wraw,
            nc.sbuf_tensor("wgam", [128, DM], BF16) as wgam,
            nc.psum_tensor("p0a", [128, 128], F32) as p0a,
            nc.psum_tensor("p0b", [128, 128], F32) as p0b,
            nc.psum_tensor("p0c", [128, 512], F32) as p0c,
            nc.Block() as block,
        ):
            @block.gpsimd
            def _(g):
                nld = 0
                for dst, srcd in [(cmat, cmatd), (ident, identd),
                                  (onesc, onescd),
                                  (onesr, onesrd), (oidx, oidxd), (gam, gamd),
                                  (sg, sgd), (su, sud), (betbf, betd),
                                  (sdbf, sdrow), (onesrf, onesrfd)]:
                    if nld:
                        g.wait_ge(S["c_ld"], nld)
                    g.dma_start(dst[:, :], srcd[:, :]).then_inc(S["c_ld"], 16)
                    nld += 16
                g.wait_ge(S["c_ld"], nld)
                g.dma_start(wvs[:, :], wvT[:, :]).then_inc(S["c_ld"], 16)
                for k in range(16):
                    wsrc = wqd_ if k < 8 else wkd_
                    h = k % 8
                    if k >= 1:
                        g.wait_ge(S["wsc"], k)
                        g.wait_ge(S["wld"], 16 * k)
                    g.dma_start(wraw[:, :], wsrc[h * 128:(h + 1) * 128, :]
                                ).then_inc(S["wld"], 16)

            @block.vector
            def _(v):
                v.wait_ge(S["c_mm"], 1)
                v.tensor_copy(cbeta[:, :], p0a[:, 0:1])
                v.engine_nop().then_inc(S["c_ev"], 1)
                for ebk in range(4):
                    v.wait_ge(S["c_mm"], 2 + ebk)
                    v.tensor_copy(sdbc[:, ebk * 512:(ebk + 1) * 512], p0c[:, :]
                                  ).then_inc(S["c_ev"], 1)
                for k in range(16):
                    v.wait_ge(S["wld"], 16 * (k + 1))
                    if k >= 1:
                        v.wait_ge(S["wmm"], 16 * k)
                    v.tensor_scalar(wgam[:, :], wraw[:, :], gam[:, :], None,
                                    ALU.mult, ALU.bypass).then_inc(S["wsc"], 1)
                    wt = wtq if k < 8 else wtk
                    h = k % 8
                    for fc in range(16):
                        v.wait_ge(S["wmm"], 16 * k + fc + 1)
                        bank = p0a if fc % 2 == 0 else p0b
                        v.tensor_copy(
                            wt[:, fc * EQ + h * 128: fc * EQ + (h + 1) * 128],
                            bank[:, :]).then_inc(S["wev"], 1)

            @block.tensor
            def _(t):
                t.wait_ge(S["c_ld"], 9 * 16)
                t.matmul(p0a[:, 0:1], cmat[:, :], betbf[:, :],
                         start=True, stop=True).then_inc(S["c_mm"], 1)
                t.wait_ge(S["c_ld"], 10 * 16)
                for ebk in range(4):
                    if ebk >= 1:
                        t.wait_ge(S["c_ev"], 1 + ebk)
                    t.matmul(p0c[:, :], onesr[:, :],
                             sdbf[:, ebk * 512:(ebk + 1) * 512],
                             start=True, stop=True).then_inc(S["c_mm"], 1)
                t.wait_ge(S["c_ev"], 5)
                for k in range(16):
                    t.wait_ge(S["wsc"], k + 1)
                    for fc in range(16):
                        gidx = 16 * k + fc
                        if gidx >= 2:
                            t.wait_ge(S["wev"], gidx - 1)
                        bank = p0a if fc % 2 == 0 else p0b
                        t.matmul(bank[:, :], wgam[:, fc * 128:(fc + 1) * 128],
                                 cmat[:, :], start=True, stop=True
                                 ).then_inc(S["wmm"], 1)

        if 1 in phases:
            with (
                nc.sbuf_tensor("xTs", [128, 2 * 16 * TB], BF16) as xTs,
                nc.sbuf_tensor("qcs", [128, 8 * TB], BF16) as qcs,
                nc.sbuf_tensor("kcs", [128, 8 * TB], BF16) as kcs,
                nc.sbuf_tensor("vvs", [128, 8 * TB], BF16) as vvs,
                nc.sbuf_tensor("q2s", [128, 16 * TB], BF16) as q2s,
                nc.sbuf_tensor("k2s", [128, 16 * TB], BF16) as k2s,
                nc.sbuf_tensor("vqr", [1, 2 * TB], F32) as vqr,
                nc.sbuf_tensor("rs1", [128, 32], F32) as rs1,
                nc.sbuf_tensor("rs2", [128, 32], F32) as rs2,
                nc.psum_tensor("pA", [128, TB], F32) as pA,
                nc.psum_tensor("pB", [128, TB], F32) as pB,
                nc.psum_tensor("pC", [128, TB], F32) as pC,
                nc.psum_tensor("pD", [128, TB], F32) as pD,
                nc.psum_tensor("pKS", [128, 32], F32) as pKS,
                nc.psum_tensor("pVQ0", [1, TB], F32) as pVQ0,
                nc.psum_tensor("pVQ1", [1, TB], F32) as pVQ1,
                nc.Block() as block,
            ):
                banks = [pA, pB, pC, pD]
                pVQs = [pVQ0, pVQ1]

                def xv(buf, fc, lo, hi):
                    base = buf * 16 * TB + fc * TB
                    return xTs[:, base + lo: base + hi]

                @block.gpsimd
                def _(g):
                    for tb in range(NTB):
                        if tb >= 2:
                            g.wait_ge(S["p1mm"], 384 * (tb - 1))
                        if tb >= 1:
                            g.wait_ge(S["xld"], 16 * tb)
                        g.dma_start(
                            xTs[:, (tb % 2) * 16 * TB:(tb % 2 + 1) * 16 * TB],
                            xT[:, tb * 16 * TB:(tb + 1) * 16 * TB]
                            ).then_inc(S["xld"], 16)

                @block.tensor
                def _(t):
                    t.wait_ge(S["c_ld"], 12 * 16)
                    for tb in range(NTB):
                        t.wait_ge(S["xld"], 16 * (tb + 1))
                        buf = tb % 2
                        for i in range(24):
                            gidx = tb * 24 + i
                            bank = banks[gidx % 4]
                            if gidx >= 4:
                                t.wait_ge(S["vev"], gidx - 3)
                            if i < 8:
                                for fc in range(16):
                                    t.matmul(bank[:, :],
                                             wtq[:, fc * EQ + i * 128:
                                                 fc * EQ + (i + 1) * 128],
                                             xv(buf, fc, 0, TB),
                                             start=(fc == 0), stop=(fc == 15)
                                             ).then_inc(S["p1mm"], 1)
                            elif i < 16:
                                e = i - 8
                                for fc in range(16):
                                    t.matmul(bank[:, :],
                                             wtk[:, fc * EQ + e * 128:
                                                 fc * EQ + (e + 1) * 128],
                                             xv(buf, fc, 0, TB),
                                             start=(fc == 0), stop=(fc == 15)
                                             ).then_inc(S["p1mm"], 1)
                            else:
                                tt = (i - 16) // 2
                                eb = (i - 16) % 2
                                for fc in range(16):
                                    t.matmul(bank[:, :],
                                             xv(buf, fc, tt * 128, (tt + 1) * 128),
                                             wvs[:, fc * EQ + eb * 512:
                                                 fc * EQ + (eb + 1) * 512],
                                             start=(fc == 0), stop=(fc == 15)
                                             ).then_inc(S["p1mm"], 1)
                        for h in range(NH):
                            t.wait_ge(S["sq2"], 16 * tb + h + 1)
                            if tb * 8 + h >= 2:
                                t.wait_ge(S["vqe"], tb * 8 + h - 1)
                            t.matmul(pVQs[(tb * 8 + h) % 2][:, :], onesc[:, :],
                                     q2s[:, (tb % 2) * 8 * TB + h * TB:
                                         (tb % 2) * 8 * TB + (h + 1) * TB],
                                     start=True, stop=True).then_inc(S["vqm"], 1)
                        if tb >= 1:
                            t.wait_ge(S["rsd"], tb)
                        for h in range(NH):
                            t.wait_ge(S["sq2"], 16 * tb + 8 + h + 1)
                            for tt in range(4):
                                col = h * 4 + tt
                                t.matmul(pKS[:, col:col + 1],
                                         k2s[:, (tb % 2) * 8 * TB + h * TB + tt * 128:
                                             (tb % 2) * 8 * TB + h * TB + (tt + 1) * 128],
                                         onesc[:, :], start=True, stop=True
                                         ).then_inc(S["ksm"], 1)

                @block.scalar
                def _(s):
                    for tb in range(NTB):
                        for i in range(16):
                            gidx = tb * 24 + i
                            h = i % 8
                            s.wait_ge(S["p1mm"], gidx * 16 + 16)
                            if tb >= 2:
                                if i < 8:
                                    s.wait_ge(S["vqm"], 8 * (tb - 2) + h + 1)
                                else:
                                    s.wait_ge(S["ksm"], 32 * (tb - 2) + 4 * (h + 1))
                            bank = banks[gidx % 4]
                            st = q2s if i < 8 else k2s
                            so = (tb % 2) * 8 * TB
                            s.activation(st[:, so + h * TB:so + (h + 1) * TB],
                                         bank[:, :],
                                         AFT.Square, bias=cbeta[:, :], scale=1.0
                                         ).then_inc(S["sq2"], 1)
                        for h in range(NH):
                            gq = tb * 8 + h
                            s.wait_ge(S["vqe"], gq + 1)
                            if gq >= 1:
                                s.wait_ge(S["spvq"], 16 * gq)
                            s.dma_start(vqh[h:h + 1, tb * TB:(tb + 1) * TB],
                                        vqr[:, (gq % 2) * TB:(gq % 2 + 1) * TB]
                                        ).then_inc(S["spvq"], 16)
                        s.wait_ge(S["rsa"], tb + 1)
                        if tb >= 1:
                            s.wait_ge(S["rsd"], tb)
                        s.sqrt(rs2[:, :], rs1[:, :]).then_inc(S["rsq"], 1)

                @block.vector
                def _(v):
                    for tb in range(NTB):
                        for i in range(24):
                            gidx = tb * 24 + i
                            bank = banks[gidx % 4]
                            if i < 16:
                                h = i % 8
                                v.wait_ge(S["sq2"], 16 * tb + i + 1)
                                if tb >= 1:
                                    sp = S["spq"] if i < 8 else S["spk"]
                                    v.wait_ge(sp, 16 * (8 * (tb - 1) + h + 1))
                                st = qcs if i < 8 else kcs
                                v.tensor_scalar(st[:, h * TB:(h + 1) * TB],
                                                bank[:, :], cbeta[:, :], None,
                                                ALU.add, ALU.bypass
                                                ).then_inc(S["vev"], 1)
                            else:
                                j = i - 16
                                v.wait_ge(S["p1mm"], gidx * 16 + 16)
                                if tb >= 1:
                                    v.wait_ge(S["spv"],
                                              16 * (8 * (tb - 1) + j + 1))
                                v.tensor_copy(vvs[:, j * TB:(j + 1) * TB],
                                              bank[:, :]).then_inc(S["vev"], 1)
                        for h in range(NH):
                            gq = tb * 8 + h
                            v.wait_ge(S["vqm"], gq + 1)
                            if gq >= 2:
                                v.wait_ge(S["spvq"], 16 * (gq - 1))
                            v.tensor_copy(vqr[:, (gq % 2) * TB:(gq % 2 + 1) * TB],
                                          pVQs[gq % 2][:, :]).then_inc(S["vqe"], 1)
                        v.wait_ge(S["ksm"], 32 * (tb + 1))
                        v.tensor_scalar(rs1[:, :], pKS[:, :], DEPS, None,
                                        ALU.add, ALU.bypass).then_inc(S["rsa"], 1)
                        v.wait_ge(S["rsq"], tb + 1)
                        v.reciprocal(
                            _r(rsk[:, :], "p (h k) -> p h k", h=NH)
                            [:, :, tb * 4:(tb + 1) * 4],
                            _r(rs2[:, :], "p (h k) -> p h k", h=NH)
                        ).then_inc(S["rsd"], 1)

                @block.sync
                def _(sy):
                    for tb in range(NTB):
                        for i in range(24):
                            sy.wait_ge(S["vev"], tb * 24 + i + 1)
                            if i < 8:
                                if tb * 8 + i >= 1:
                                    sy.wait_ge(S["spq"], 16 * (tb * 8 + i))
                                sy.dma_start(qs[i * 128:(i + 1) * 128,
                                                tb * TB:(tb + 1) * TB],
                                             qcs[:, i * TB:(i + 1) * TB]
                                             ).then_inc(S["spq"], 16)
                            elif i < 16:
                                h = i - 8
                                if tb * 8 + h >= 1:
                                    sy.wait_ge(S["spk"], 16 * (tb * 8 + h))
                                sy.dma_start(ks[h * 128:(h + 1) * 128,
                                                tb * TB:(tb + 1) * TB],
                                             kcs[:, h * TB:(h + 1) * TB]
                                             ).then_inc(S["spk"], 16)
                            else:
                                j = i - 16
                                tt, eb = j // 2, j % 2
                                if tb * 8 + j >= 1:
                                    sy.wait_ge(S["spv"], 16 * (tb * 8 + j))
                                sy.dma_start(
                                    _r(vs[:, :], "p (h k d) -> p h k d",
                                       h=NH, k=32)
                                    [:, 4 * eb:4 * eb + 4, tb * 4 + tt, :],
                                    _r(vvs[:, j * TB:(j + 1) * TB],
                                       "p (hh d) -> p hh d", hh=4)
                                    ).then_inc(S["spv"], 16)
                    sy.wait_ge(S["spq"], 16 * 64)
                    sy.wait_ge(S["spk"], 16 * 64)
                    sy.wait_ge(S["spv"], 16 * 64)
                    sy.wait_ge(S["spvq"], 16 * 64)

    wos_cm = (nc.sbuf_tensor("wos", [128, 16 * DM], BF16)
              if (2 in phases or 3 in phases) else None)
    wos = wos_cm.__enter__() if wos_cm is not None else None
    if 2 in phases:
        seq = [(h, half * 4 + j) for half in range(2)
               for h in range(NH) for j in range(4)]
        nkt = {qb: 4 * (qb + 1) for qb in range(NQB)}
        ga_base = {}
        oz_end = {}
        run = 0
        ozr = 0
        for (h, qb) in seq:
            ga_base[(h, qb)] = run
            run += nkt[qb]
            ozr += 2 * nkt[qb]
            oz_end[(h, qb)] = ozr
        mk_idx = {}
        mrun = 0
        for (h, qb) in seq:
            for kt in range(nkt[qb] - 4, nkt[qb]):
                mk_idx[(h, qb, kt)] = mrun
                mrun += 1

        def is_masked(qb, kt):
            return kt >= nkt[qb] - 4

        def sq_on_vector(kt):
            return kt % 3 == 2 and kt > 5

        cntS, cntV = {}, {}
        _cs = _cv = 0
        for (_h, _qb) in seq:
            for _kt in range(nkt[_qb]):
                if sq_on_vector(_kt):
                    _cv += 1
                else:
                    _cs += 1
                _ga = ga_base[(_h, _qb)] + _kt
                cntS[_ga], cntV[_ga] = _cs, _cv

        sq_owner_vec = {}
        for (_h, _qb) in seq:
            for _kt in range(nkt[_qb]):
                sq_owner_vec[ga_base[(_h, _qb)] + _kt] = sq_on_vector(_kt)

        def wait_sq(eng, ga):
            if sq_owner_vec[ga]:
                eng.wait_ge(S["sqv"], cntV[ga])
            else:
                eng.wait_ge(S["sqa"], cntS[ga])

        with contextlib.ExitStack() as _st:
            _e = _st.enter_context
            masks = _e(nc.sbuf_tensor("masks_s", [128, 4 * QB], BF16))
            khs = _e(nc.sbuf_tensor("khs", [128, 2 * T], BF16))
            vhs = _e(nc.sbuf_tensor("vhs", [128, 2 * 32 * 128], BF16))
            qbs = _e(nc.sbuf_tensor("qbs", [128, 4 * QB], BF16))
            a_s = _e(nc.sbuf_tensor("a_s", [128, 3 * QB], BF16))
            ams = _e(nc.sbuf_tensor("ams", [128, 3 * QB], BF16))
            vqhd = _e(nc.sbuf_tensor("vqhd", [1, T], F32))
            zrow = _e(nc.sbuf_tensor("zrow", [1, QB], F32))
            lnr = _e(nc.sbuf_tensor("lnr", [1, QB], F32))
            rcr = _e(nc.sbuf_tensor("rcr", [1, QB], F32))
            den1 = _e(nc.sbuf_tensor("den1", [1, 4 * QB], F32))
            den2 = _e(nc.sbuf_tensor("den2", [1, QB], F32))
            recb = _e(nc.sbuf_tensor("recb", [128, QB], F32))
            vsq = _e(nc.sbuf_tensor("vsq", [128, QB], BF16))
            obs = _e(nc.sbuf_tensor("obs", [128, 2 * QB], BF16))
            pS0 = _e(nc.psum_tensor("pS0", [128, QB], F32))
            pS1 = _e(nc.psum_tensor("pS1", [128, QB], F32))
            pS2 = _e(nc.psum_tensor("pS2", [128, QB], F32))
            pO0 = _e(nc.psum_tensor("pO0", [128, QB], F32))
            pO1 = _e(nc.psum_tensor("pO1", [128, QB], F32))
            pZ0 = _e(nc.psum_tensor("pZ0", [1, QB], F32))
            pZ1 = _e(nc.psum_tensor("pZ1", [1, QB], F32))
            pBC = _e(nc.psum_tensor("pBC", [128, QB], F32))
            block = _e(nc.Block())
            True
            pS = [pS0, pS1, pS2]
            pOs = [pO0, pO1]
            pZs = [pZ0, pZ1]

            @block.gpsimd
            def _(g):
                if 3 not in phases:
                    return
                g.dma_start(wos[:, :], woT[:, :]).then_inc(S["wol"], 16)
                for c4 in range(6):
                    g.wait_ge(S["ost"], 16 * 4 * AG_CUM[c4 + 1])
                    g.collective_compute(
                        "AllGather", ALU.bypass,
                        replica_groups=[[0, 1, 2, 3, 4, 5, 6, 7]],
                        ins=[agin[AG_CUM[c4] * 128:
                                  AG_CUM[c4 + 1] * 128, :].opt()],
                        outs=[agout[AG_CUM[c4] * 128 * 8:
                                    AG_CUM[c4 + 1] * 128 * 8, :].opt()],
                    ).then_inc(S["agc"], 1)

            @block.sync
            def _(sy):
                sy.dma_start(masks[:, :], masksd[:, :]).then_inc(S["mld"], 16)
                def qload(gl):
                    hp, qp = seq[gl]
                    if gl >= 4:
                        hx, qx = seq[gl - 4]
                        sy.wait_ge(S["p2ms"], ga_base[(hx, qx)] + nkt[qx])
                    if gl >= 1:
                        sy.wait_ge(S["qld"], 16 * gl)
                    sy.dma_start(qbs[:, (gl % 4) * QB:(gl % 4 + 1) * QB],
                                 qs[hp * 128:(hp + 1) * 128,
                                    qp * QB:(qp + 1) * QB]
                                 ).then_inc(S["qld"], 16)

                qload(0)
                qload(1)
                for gq in range(64):
                    if gq + 2 < 64:
                        qload(gq + 2)
                    if gq >= 1:
                        sy.wait_ge(S["oev"], gq)
                        if gq >= 2:
                            sy.wait_ge(S["ost"], 16 * (gq - 1))
                        hp, qp = seq[gq - 1]
                        hfp, qq = qp // 4, qp % 4
                        sy.dma_start(
                            agin[hfp * EQ + hp * 128:
                                 hfp * EQ + (hp + 1) * 128,
                                 qq * QB:(qq + 1) * QB],
                            obs[:, ((gq - 1) % 2) * QB:((gq - 1) % 2 + 1) * QB]
                            ).then_inc(S["ost"], 16)
                sy.wait_ge(S["oev"], 64)
                sy.wait_ge(S["ost"], 16 * 63)
                sy.dma_start(agin[EQ + 7 * 128: EQ + 8 * 128, 3 * QB:4 * QB],
                             obs[:, (63 % 2) * QB:(63 % 2 + 1) * QB]
                             ).then_inc(S["ost"], 16)
                sy.wait_ge(S["ost"], 16 * 64)

            def kv_load(s, grp):
                h = grp % 8
                if grp >= 2:
                    hp, qp = seq[(grp - 1) * 4 - 1]
                    s.wait_ge(S["p2oz"], oz_end[(hp, qp)])
                if grp >= 1:
                    s.wait_ge(S["kvld"], 16 * grp)
                    s.wait_ge(S["kvv"], 16 * grp)
                if grp <= 1:
                    s.wait_ge(S["spk"], 16 * (57 + h))
                    s.wait_ge(S["spv"], 16 * 63)
                kb = (grp % 2) * T
                vb = (grp % 2) * 32 * 128
                s.dma_start(khs[:, kb:kb + T], ks[h * 128:(h + 1) * 128, :]
                            ).then_inc(S["kvld"], 16)
                s.dma_start(vhs[:, vb:vb + 32 * 128],
                            vs[:, h * T:(h + 1) * T]
                            ).then_inc(S["kvv"], 16)

            @block.scalar
            def _(s):
                kv_load(s, 0)
                for grp in range(16):
                    half, h = grp // 8, grp % 8
                    if grp >= 1:
                        s.wait_ge(S["denv"], 4 * grp)
                        s.wait_ge(S["vqld"], 16 * grp)
                    s.dma_start(vqhd[:, :], vqh[h:h + 1, :]
                                ).then_inc(S["vqld"], 16)
                    for j in range(4):
                        gq = grp * 4 + j
                        if j == 1 and grp + 1 < 16:
                            kv_load(s, grp + 1)
                        qb = half * 4 + j
                        for kt in range(nkt[qb]):
                            if sq_on_vector(kt):
                                continue
                            ga = ga_base[(h, qb)] + kt
                            s.wait_ge(S["p2ms"], ga + 1)
                            if ga >= 3:
                                s.wait_ge(S["p2oz"], 2 * (ga - 2))
                            s.activation(a_s[:, (ga % 3) * QB:(ga % 3 + 1) * QB],
                                         pS[ga % 3][:, :], AFT.Square,
                                         bias=0.0,
                                         scale=rsk[:, h * 32 + kt:
                                                   h * 32 + kt + 1]
                                         ).then_inc(S["sqa"], 1)
                            if kt == 1 and gq >= 1:
                                s.wait_ge(S["denv"], gq)
                                if gq >= 2:
                                    s.wait_ge(S["bcm"], gq - 1)
                                s.activation(lnr[:, :], den2[:, :], AFT.Ln,
                                             bias=0.0, scale=1.0)
                                s.drain()
                                s.activation(rcr[:, :], lnr[:, :], AFT.Exp,
                                             bias=0.0, scale=-1.0
                                             ).then_inc(S["rcs"], 1)
                s.wait_ge(S["denv"], 64)
                s.wait_ge(S["bcm"], 63)
                s.activation(lnr[:, :], den2[:, :], AFT.Ln, bias=0.0, scale=1.0)
                s.drain()
                s.activation(rcr[:, :], lnr[:, :], AFT.Exp, bias=0.0,
                             scale=-1.0).then_inc(S["rcs"], 1)

            @block.vector
            def _(v):
                v.wait_ge(S["mld"], 16)
                for grp in range(16):
                    half, h = grp // 8, grp % 8
                    for j in range(4):
                        qb = half * 4 + j
                        gq = grp * 4 + j
                        v.drain()
                        if j == 0:
                            v.wait_ge(S["vqld"], 16 * (grp + 1))
                            v.tensor_scalar(den1[:, :],
                                            vqhd[:, half * 4 * QB:
                                                 (half * 4 + 4) * QB],
                                            C1, C2, ALU.mult, ALU.add)
                            v.drain()
                        for kt in range(nkt[qb]):
                            ga = ga_base[(h, qb)] + kt
                            if sq_on_vector(kt):
                                v.wait_ge(S["p2ms"], ga + 1)
                                if ga >= 3:
                                    v.wait_ge(S["p2oz"], 2 * (ga - 2))
                                v.tensor_scalar(vsq[:, :], pS[ga % 3][:, :],
                                                rsk[:, h * 32 + kt:
                                                    h * 32 + kt + 1],
                                                None, ALU.mult, ALU.bypass)
                                v.drain()
                                v.tensor_tensor(
                                    a_s[:, (ga % 3) * QB:(ga % 3 + 1) * QB],
                                    vsq[:, :], vsq[:, :], ALU.mult
                                    ).then_inc(S["sqv"], 1)
                                v.drain()
                            if not is_masked(qb, kt):
                                continue
                            jm = kt - (nkt[qb] - 4)
                            if not sq_on_vector(kt):
                                wait_sq(v, ga)
                            v.tensor_tensor(
                                ams[:, (ga % 3) * QB:(ga % 3 + 1) * QB],
                                a_s[:, (ga % 3) * QB:(ga % 3 + 1) * QB],
                                masks[:, jm * QB:(jm + 1) * QB],
                                ALU.mult).then_inc(S["mka"], 1)
                        if gq >= 1:
                            gp = gq - 1
                            gpg, gpj = divmod(gp, 4)
                            v.wait_ge(S["p2oz"],
                                      oz_end[(gpg % 8, (gpg // 8) * 4 + gpj)])
                            v.wait_ge(S["bcm"], gp + 1)
                            v.tensor_copy(recb[:, :], pBC[:, :]
                                          ).then_inc(S["recv"], 1)
                            v.drain()
                            if gp >= 2:
                                v.wait_ge(S["ost"], 16 * (gp - 1))
                            v.tensor_tensor(obs[:, (gp % 2) * QB:
                                                (gp % 2 + 1) * QB],
                                            pOs[gp % 2][:, :], recb[:, :],
                                            ALU.mult).then_inc(S["oev"], 1)
                        v.wait_ge(S["p2oz"], oz_end[(h, qb)])
                        v.tensor_copy(zrow[:, :], pZs[gq % 2][:, :]
                                      ).then_inc(S["zrv"], 1)
                        v.drain()
                        if gq >= 1:
                            v.wait_ge(S["rcs"], gq)
                        v.tensor_tensor(den2[:, :],
                                        den1[:, j * QB:(j + 1) * QB],
                                        zrow[:, :],
                                        ALU.add).then_inc(S["denv"], 1)
                gp = 63
                v.drain()
                v.wait_ge(S["bcm"], gp + 1)
                v.tensor_copy(recb[:, :], pBC[:, :]).then_inc(S["recv"], 1)
                v.drain()
                v.wait_ge(S["ost"], 16 * (gp - 1))
                v.tensor_tensor(obs[:, (gp % 2) * QB:(gp % 2 + 1) * QB],
                                pOs[gp % 2][:, :], recb[:, :],
                                ALU.mult).then_inc(S["oev"], 1)

            @block.tensor
            def _(t):
                pend = []

                def mmoz(grp, h, qb, gq, kt):
                    ga = ga_base[(h, qb)] + kt
                    n = nkt[qb]
                    wait_sq(t, ga)
                    if is_masked(qb, kt):
                        t.wait_ge(S["mka"], mk_idx[(h, qb, kt)] + 1)
                        asrc = ams
                    else:
                        asrc = a_s
                    if kt == 0 and gq >= 2:
                        t.wait_ge(S["oev"], gq - 1)
                        t.wait_ge(S["zrv"], gq - 1)
                    t.matmul(pOs[gq % 2][:, :],
                             vhs[:, (grp % 2) * 32 * 128 + kt * 128:
                                 (grp % 2) * 32 * 128 + (kt + 1) * 128],
                             asrc[:, (ga % 3) * QB:(ga % 3 + 1) * QB],
                             start=(kt == 0), stop=(kt == n - 1)
                             ).then_inc(S["p2oz"], 1)
                    t.matmul(pZs[gq % 2][:, :], onesc[:, :],
                             asrc[:, (ga % 3) * QB:(ga % 3 + 1) * QB],
                             start=(kt == 0), stop=(kt == n - 1)
                             ).then_inc(S["p2oz"], 1)

                for grp in range(16):
                    half, h = grp // 8, grp % 8
                    t.wait_ge(S["kvld"], 16 * (grp + 1))
                    t.wait_ge(S["kvv"], 16 * (grp + 1))
                    for j in range(4):
                        qb = half * 4 + j
                        gq = grp * 4 + j
                        t.wait_ge(S["qld"], 16 * (gq + 1))
                        n = nkt[qb]

                        def bc(gq=gq):
                            t.wait_ge(S["rcs"], gq)
                            if gq >= 2:
                                t.wait_ge(S["recv"], gq - 1)
                            t.matmul(pBC[:, :], onesrf[:, :], rcr[:, :],
                                     start=True, stop=True
                                     ).then_inc(S["bcm"], 1)

                        for step in range(n):
                            if step == 4 and gq >= 1:
                                bc()
                            kt = step
                            ga = ga_base[(h, qb)] + kt
                            if ga >= 3:
                                wait_sq(t, ga - 3)
                            t.matmul(pS[ga % 3][:, :],
                                     khs[:, (grp % 2) * T + kt * 128:
                                         (grp % 2) * T + (kt + 1) * 128],
                                     qbs[:, (gq % 4) * QB:(gq % 4 + 1) * QB],
                                     start=True, stop=True
                                     ).then_inc(S["p2ms"], 1)
                            if step < 2 and pend:
                                pend.pop(0)()
                            if step >= 2:
                                mmoz(grp, h, qb, gq, step - 2)
                        if n <= 4 and gq >= 1:
                            bc()
                        pend.append(
                            (lambda a, b, c, d, e:
                             lambda: mmoz(a, b, c, d, e))(grp, h, qb, gq, n - 2))
                        pend.append(
                            (lambda a, b, c, d, e:
                             lambda: mmoz(a, b, c, d, e))(grp, h, qb, gq, n - 1))
                for f in pend:
                    f()
                gq = 64
                t.wait_ge(S["rcs"], gq)
                t.wait_ge(S["recv"], gq - 1)
                t.matmul(pBC[:, :], onesrf[:, :], rcr[:, :],
                         start=True, stop=True).then_inc(S["bcm"], 1)

    if 3 in phases:
        NT3 = TH // 128
        with (
            nc.sbuf_tensor("oms", [128, 16 * TH], BF16) as oms,
            nc.sbuf_tensor("xrs", [128, 2 * DM], F32) as xrs,
            nc.sbuf_tensor("z1s", [128, DM], F32) as z1s,
            nc.sbuf_tensor("bns", [128, 24], F32) as bns,
            nc.sbuf_tensor("mvs", [128, 2], F32) as mvs,
            nc.sbuf_tensor("l3a", [128, 1], F32) as l3a,
            nc.sbuf_tensor("l3b", [128, 1], F32) as l3b,
            nc.sbuf_tensor("l3c", [128, 1], F32) as l3c,
            nc.sbuf_tensor("l3d", [128, 1], F32) as l3d,
            nc.sbuf_tensor("x1s", [128, 2 * DM], BF16) as x1s,
            nc.sbuf_tensor("x1t", [128, DM], BF16) as x1t,
            nc.psum_tensor("pW0", [128, 512], F32) as pW0,
            nc.psum_tensor("pW1", [128, 512], F32) as pW1,
            nc.psum_tensor("pW2", [128, 512], F32) as pW2,
            nc.psum_tensor("pW3", [128, 512], F32) as pW3,
            nc.psum_tensor("pT0", [128, 128], BF16) as pT0,
            nc.psum_tensor("pT1", [128, 128], BF16) as pT1,
            nc.Block() as block,
        ):
            pW = [pW0, pW1, pW2, pW3]
            pT = [pT0, pT1]

            @block.gpsimd
            def _(g):
                gsems = [S["gth"], S["gth1"], S["gth2"], S["gth3"]]
                cnt = [0, 0, 0, 0]
                prev_req = 0
                for idx, gi in enumerate(P3_A + P3_B + P3_C + P3_D):
                    req = 3 if idx < 8 else 4 if idx < 12 else \
                        5 if idx < 14 else 6
                    if req != prev_req:
                        g.wait_ge(S["agc"], req)
                        prev_req = req
                    sm = idx % 4 if idx < 12 else \
                        (idx - 12) if idx < 14 else (idx - 12)
                    if cnt[sm] >= 1:
                        g.wait_ge(gsems[sm], 16 * cnt[sm])
                    g.indirect_dma_start(
                        oms[:, gi * TH:(gi + 1) * TH], None,
                        agout[:, :],
                        IndirectOffsetOnAxis(ap=oidx[:, gi:gi + 1], axis=0),
                    ).then_inc(gsems[sm], 16)
                    cnt[sm] += 1

            @block.sync
            def _(sy):
                for tt in range(NT3):
                    if tt >= 2:
                        sy.wait_ge(S["z1v"], 4 * (tt - 1))
                    if tt >= 1:
                        sy.wait_ge(S["xrl"], 16 * tt)
                    sy.dma_start(xrs[:, (tt % 2) * DM:(tt % 2 + 1) * DM],
                                 xres[tt * 128:(tt + 1) * 128, :]
                                 ).then_inc(S["xrl"], 16)

            @block.tensor
            def _(t):
                for s4 in ("gth", "gth1", "gth2", "gth3"):
                    t.wait_ge(S[s4], 32)
                t.wait_ge(S["wol"], 16)
                for tt in range(NT3):
                    def wo_mm(eb, fc, start, stop):
                        t.matmul(pW[eb][:, :],
                                 oms[:, fc * TH + tt * 128:
                                     fc * TH + (tt + 1) * 128],
                                 wos[:, fc * DM + eb * 512:
                                     fc * DM + (eb + 1) * 512],
                                 start=start, stop=stop
                                 ).then_inc(S["p3mm"], 1)

                    for eb in range(4):
                        if tt >= 1:
                            t.wait_ge(S["z1v"], 4 * (tt - 1) + eb + 1)
                        for k, fc in enumerate(P3_A):
                            wo_mm(eb, fc, k == 0, False)
                    if tt == 0:
                        for s4 in ("gth", "gth1", "gth2", "gth3"):
                            t.wait_ge(S[s4], 48)
                    for eb in range(4):
                        for fc in P3_B:
                            wo_mm(eb, fc, False, False)
                    if tt == 0:
                        t.wait_ge(S["gth"], 64)
                        t.wait_ge(S["gth1"], 64)
                    for eb in range(4):
                        for fc in P3_C:
                            wo_mm(eb, fc, False, False)
                    if tt == 0:
                        t.wait_ge(S["gth2"], 64)
                        t.wait_ge(S["gth3"], 64)
                    for eb in range(4):
                        for k, fc in enumerate(P3_D):
                            wo_mm(eb, fc, False, k == len(P3_D) - 1)
                    if tt >= 2:
                        t.wait_ge(S["apv"], tt - 1)
                        for dc in range(16):
                            gtr = 16 * (tt - 2) + dc
                            if gtr >= 2:
                                t.wait_ge(S["trv"], gtr - 1)
                            t.transpose(pT[gtr % 2][:, :],
                                        x1s[:, ((tt - 2) % 2) * DM + dc * 128:
                                            ((tt - 2) % 2) * DM + (dc + 1) * 128],
                                        ident[:, :]).then_inc(S["trm"], 1)
                for tl in (NT3 - 2, NT3 - 1):
                    t.wait_ge(S["apv"], tl + 1)
                    for dc in range(16):
                        gtr = 16 * tl + dc
                        if gtr >= 2:
                            t.wait_ge(S["trv"], gtr - 1)
                        t.transpose(pT[gtr % 2][:, :],
                                    x1s[:, (tl % 2) * DM + dc * 128:
                                        (tl % 2) * DM + (dc + 1) * 128],
                                    ident[:, :]).then_inc(S["trm"], 1)

            @block.vector
            def _(v):
                for tt in range(NT3):
                    v.drain()
                    v.wait_ge(S["xrl"], 16 * (tt + 1))
                    for eb in range(4):
                        v.wait_ge(S["p3mm"], tt * 64 + 56 + 2 * (eb + 1))
                        v.tensor_tensor(z1s[:, eb * 512:(eb + 1) * 512],
                                        pW[eb][:, :],
                                        xrs[:, (tt % 2) * DM + eb * 512:
                                            (tt % 2) * DM + (eb + 1) * 512],
                                        ALU.add).then_inc(S["z1v"], 1)
                    v.drain()
                    for eb in range(4):
                        v.bn_stats(bns[:, eb * 6:(eb + 1) * 6],
                                   z1s[:, eb * 512:(eb + 1) * 512])
                    v.drain()
                    v.bn_aggr(mvs[:, :], bns[:, :])
                    v.drain()
                    v.tensor_scalar(l3a[:, :], mvs[:, 1:2], EPS, None,
                                    ALU.add, ALU.bypass).then_inc(S["lnv"], 1)
                    v.wait_ge(S["p3sq"], tt + 1)
                    v.reciprocal(l3c[:, :], l3b[:, :]).then_inc(S["l3u"], 1)
                    v.drain()
                    v.tensor_scalar(l3d[:, :], mvs[:, 0:1], l3c[:, :], -1.0,
                                    ALU.mult, ALU.mult)
                    v.drain()
                    if tt >= 2:
                        v.wait_ge(S["trm"], 16 * (tt - 1))
                    for eb in range(4):
                        v.tensor_scalar(x1s[:, (tt % 2) * DM + eb * 512:
                                            (tt % 2) * DM + (eb + 1) * 512],
                                        z1s[:, eb * 512:(eb + 1) * 512],
                                        l3c[:, :], l3d[:, :],
                                        ALU.mult, ALU.add)
                    v.engine_nop().then_inc(S["apv"], 1)


            @block.scalar
            def _(s):
                def trevac(s, tl):
                    if tl >= 1:
                        s.wait_ge(S["x1st"], 16 * tl)
                    for dc in range(16):
                        gtr = 16 * tl + dc
                        s.wait_ge(S["trm"], gtr + 1)
                        s.copy(x1t[:, dc * 128:(dc + 1) * 128],
                               pT[gtr % 2][:, :]).then_inc(S["trv"], 1)
                    s.wait_ge(S["trv"], 16 * (tl + 1))
                    s.dma_start(
                        _r(x1f[:, tl * 128:(tl + 1) * 128],
                           "(c p) t -> p c t", p=128),
                        _r(x1t[:, :], "p (c t) -> p c t", c=16)
                    ).then_inc(S["x1st"], 16)

                for tt in range(NT3):
                    s.wait_ge(S["lnv"], tt + 1)
                    if tt >= 1:
                        s.wait_ge(S["l3u"], tt)
                    s.sqrt(l3b[:, :], l3a[:, :]).then_inc(S["p3sq"], 1)
                    if tt >= 2:
                        trevac(s, tt - 2)
                trevac(s, NT3 - 2)
                trevac(s, NT3 - 1)
                s.wait_ge(S["x1st"], 16 * NT3)

    if wos_cm is not None:
        wos_cm.__exit__(None, None, None)

    if 4 in phases:
        NHC2 = 22
        NEB = 4

        def p4_dmm_after_chunk4(gw):
            tbw, rem = divmod(gw, NEB * 11)
            ebw, hgw = divmod(rem, 11)
            return tbw * 1536 + ebw * 384 + (hgw + 1) * 32

        def p4_dmm_group_end(tb, eb, tt):
            return tb * 1536 + eb * 384 + 352 + (tt + 1) * 4

        def gu_done(tb, hc2, t2, hsub, part):
            return tb * 2816 + hc2 * 128 + t2 * 64 + hsub * 32 + 16 * part

        with contextlib.ExitStack() as _st:
            _e = _st.enter_context
            identE = _e(nc.sbuf_tensor("identE_s", [128, 4 * 512], BF16))
            x1b = _e(nc.sbuf_tensor("x1b", [128, 16 * FTB], BF16))
            wgs = _e(nc.sbuf_tensor("wgs", [128, 3 * 16 * 256], BF16))
            wgu8 = _e(nc.sbuf_tensor("wgu8", [128, 16 * 256], U8))
            wus = _e(nc.sbuf_tensor("wus", [128, 3 * 16 * 256], BF16))
            wuu8 = _e(nc.sbuf_tensor("wuu8", [128, 16 * 256], U8))
            gsb = _e(nc.sbuf_tensor("gsb", [128, NHC * FTB], BF16))
            gst = _e(nc.sbuf_tensor("gst", [128, 2 * 512], BF16))
            ust = _e(nc.sbuf_tensor("ust", [128, 2 * 512], BF16))
            wds = _e(nc.sbuf_tensor("wds", [128, 2 * 4 * 512], BF16))
            wdu8 = _e(nc.sbuf_tensor("wdu8", [128, 4 * 512], U8))
            dstg = _e(nc.sbuf_tensor("dstg", [128, 2 * 512], F32))
            pP0 = _e(nc.psum_tensor("pP0", [128, 512], F32))
            pP1 = _e(nc.psum_tensor("pP1", [128, 512], F32))
            pP2 = _e(nc.psum_tensor("pP2", [128, 512], F32))
            pP3 = _e(nc.psum_tensor("pP3", [128, 512], F32))
            pP4 = _e(nc.psum_tensor("pP4", [128, 512], F32))
            pP5 = _e(nc.psum_tensor("pP5", [128, 512], F32))
            pP6 = _e(nc.psum_tensor("pP6", [128, 512], F32))
            pP7 = _e(nc.psum_tensor("pP7", [128, 512], F32))
            block = _e(nc.Block())
            True
            pP = [pP0, pP1, pP2, pP3, pP4, pP5, pP6, pP7]
            pGs = [pP0, pP1, pP2]
            pUs = [pP3, pP4, pP5]

            @block.scalar
            def _(s):
                s.dma_start(identE[:, :], identEd[:, :]).then_inc(S["ield"], 16)
                for tb in range(2):
                    if tb >= 1:
                        s.wait_ge(S["gumm"], 2816)
                        s.wait_ge(S["dmm"], 1536)
                        s.wait_ge(S["x1l"], 16)
                    s.dma_start(
                        _r(x1b[:, :], "p (c t) -> p c t", c=16),
                        _r(x1f[:, tb * FTB:(tb + 1) * FTB],
                           "(c p) t -> p c t", p=128)).then_inc(S["x1l"], 16)
                    for hc2 in range(NHC2):
                        for t2 in range(2):
                            for hsub in range(2):
                                gt = tb * 88 + hc2 * 4 + t2 * 2 + hsub
                                hc = hc2 * 2 + hsub
                                s.wait_ge(S["gumm"],
                                          gu_done(tb, hc2, t2, hsub, 1))
                                if gt >= 2:
                                    s.wait_ge(S["gml"], gt - 1)
                                s.activation(gst[:, (gt % 2) * 512:
                                                 (gt % 2 + 1) * 512],
                                             pGs[gt % 3][:, :], AFT.Silu,
                                             bias=0.0, scale=sg[:, hc:hc + 1]
                                             ).then_inc(S["sil"], 1)

            @block.gpsimd
            def _(g):
                for tb in range(2):
                    for hc2 in range(NHC2):
                        gc = tb * NHC2 + hc2
                        if gc >= 1:
                            g.wait_ge(S["deq"], gc)
                            g.wait_ge(S["wldg"], 32 * gc)
                        sl8 = 0
                        g.dma_start(
                            wgu8[:, sl8:sl8 + 16 * 256],
                            wgT[:, hc2 * 16 * 256:(hc2 + 1) * 16 * 256]
                            ).then_inc(S["wldg"], 16)
                        g.dma_start(
                            wuu8[:, sl8:sl8 + 16 * 256],
                            wuT[:, hc2 * 16 * 256:(hc2 + 1) * 16 * 256]
                            ).then_inc(S["wldg"], 16)
                    for eb in range(NEB):
                        for hg4 in range(11):
                            gw = tb * NEB * 11 + eb * 11 + hg4
                            if gw >= 1:
                                g.wait_ge(S["wdc"], gw)
                                g.wait_ge(S["wdl"], 16 * gw)
                            g.dma_start(
                                wdu8[:, 0:2048],
                                wdT[:, (eb * NHC + hg4 * 4) * 512:
                                    (eb * NHC + hg4 * 4 + 4) * 512]
                                ).then_inc(S["wdl"], 16)

            @block.vector
            def _(v):
                def deq_chunk(gc):
                    v.wait_ge(S["wldg"], 32 * (gc + 1))
                    if gc >= 3:
                        v.wait_ge(S["gumm"], (gc - 2) * 128)
                    sl8 = 0
                    slb = (gc % 3) * 16 * 256
                    v.tensor_scalar(wgs[:, slb:slb + 16 * 256],
                                    wgu8[:, sl8:sl8 + 16 * 256],
                                    -128.0, None, ALU.add, ALU.bypass)
                    v.drain()
                    v.tensor_scalar(wus[:, slb:slb + 16 * 256],
                                    wuu8[:, sl8:sl8 + 16 * 256],
                                    -128.0, None, ALU.add, ALU.bypass
                                    ).then_inc(S["deq"], 1)

                for tb in range(2):
                    deq_chunk(tb * NHC2 + 0)
                    deq_chunk(tb * NHC2 + 1)
                    deq_chunk(tb * NHC2 + 2)
                    for hc2 in range(NHC2):
                        gc = tb * NHC2 + hc2
                        for t2 in range(2):
                            for hsub in range(2):
                                gt = tb * 88 + hc2 * 4 + t2 * 2 + hsub
                                hc = hc2 * 2 + hsub
                                v.wait_ge(S["gumm"],
                                          gu_done(tb, hc2, t2, hsub, 2))
                                if gt >= 2:
                                    v.wait_ge(S["gml"], gt - 1)
                                v.tensor_scalar(ust[:, (gt % 2) * 512:
                                                    (gt % 2 + 1) * 512],
                                                pUs[gt % 3][:, :],
                                                su[:, hc:hc + 1], None,
                                                ALU.mult, ALU.bypass)
                                v.wait_ge(S["sil"], gt + 1)
                                v.drain()
                                v.tensor_tensor(
                                    gsb[:, hc * FTB + t2 * 512:
                                        hc * FTB + (t2 + 1) * 512],
                                    gst[:, (gt % 2) * 512:(gt % 2 + 1) * 512],
                                    ust[:, (gt % 2) * 512:(gt % 2 + 1) * 512],
                                    ALU.mult).then_inc(S["gml"], 1)
                        if hc2 + 3 < NHC2:
                            deq_chunk(gc + 3)
                    for eb in range(NEB):
                        for hg4 in range(11):
                            gw = tb * NEB * 11 + eb * 11 + hg4
                            v.wait_ge(S["wdl"], 16 * (gw + 1))
                            if gw >= 2:
                                v.wait_ge(S["dmm"],
                                          p4_dmm_after_chunk4(gw - 2))
                            sw = (gw % 2) * 2048
                            v.tensor_scalar(wds[:, sw:sw + 2048],
                                            wdu8[:, 0:2048],
                                            -128.0, None, ALU.add, ALU.bypass
                                            ).then_inc(S["wdc"], 1)
                            v.drain()
                            for q4 in range(4):
                                v.tensor_tensor(
                                    wds[:, sw + q4 * 512:sw + (q4 + 1) * 512],
                                    wds[:, sw + q4 * 512:sw + (q4 + 1) * 512],
                                    sdbc[:, eb * 512:(eb + 1) * 512],
                                    ALU.mult).then_inc(S["wdq"], 1)
                        for tt in range(8):
                            ge = tb * 32 + eb * 8 + tt
                            v.wait_ge(S["dmm"], p4_dmm_group_end(tb, eb, tt))
                            if ge >= 2:
                                v.wait_ge(S["dst"], 16 * (ge - 1))
                            sidx = ((tb * 8 + tt) * 4 + eb) * 6
                            v.bn_stats(bst5[:, sidx:sidx + 6], pP[tt][:, :])
                            v.drain()
                            v.tensor_copy(dstg[:, (ge % 2) * 512:
                                              (ge % 2 + 1) * 512],
                                          pP[tt][:, :]).then_inc(S["dev"], 1)

            @block.tensor
            def _(t):
                t.wait_ge(S["ield"], 16)
                for tb in range(2):
                    t.wait_ge(S["x1l"], 16 * (tb + 1))
                    if tb >= 1:
                        t.wait_ge(S["dev"], 32 * tb)
                    for hc2 in range(NHC2):
                        gc = tb * NHC2 + hc2
                        t.wait_ge(S["deq"], gc + 1)
                        sl = (gc % 3) * 16 * 256
                        for t2 in range(2):
                            for hsub in range(2):
                                gt = tb * 88 + hc2 * 4 + t2 * 2 + hsub
                                if gt >= 3:
                                    t.wait_ge(S["sil"], gt - 2)
                                    t.wait_ge(S["gml"], gt - 2)
                                for fc in range(16):
                                    t.matmul(pGs[gt % 3][:, :],
                                             wgs[:, sl + fc * 256 + hsub * 128:
                                                 sl + fc * 256 + (hsub + 1) * 128],
                                             x1b[:, fc * FTB + t2 * 512:
                                                 fc * FTB + (t2 + 1) * 512],
                                             start=(fc == 0), stop=(fc == 15)
                                             ).then_inc(S["gumm"], 1)
                                for fc in range(16):
                                    t.matmul(pUs[gt % 3][:, :],
                                             wus[:, sl + fc * 256 + hsub * 128:
                                                 sl + fc * 256 + (hsub + 1) * 128],
                                             x1b[:, fc * FTB + t2 * 512:
                                                 fc * FTB + (t2 + 1) * 512],
                                             start=(fc == 0), stop=(fc == 15)
                                             ).then_inc(S["gumm"], 1)
                    for eb in range(NEB):
                        for hc in range(NHC):
                            hg4, q4 = divmod(hc, 4)
                            gw = tb * NEB * 11 + eb * 11 + hg4
                            t.wait_ge(S["wdq"], 4 * gw + q4 + 1)
                            if hc == 0:
                                geb = tb * NEB + eb
                                if geb >= 1:
                                    t.wait_ge(S["dev"], (geb - 1) * 8 + 8)
                                t.wait_ge(S["gml"], tb * 88 + 88)
                            sw = (gw % 2) * 2048 + q4 * 512
                            for tt in range(8):
                                t.matmul(pP[tt][:, :],
                                         gsb[:, hc * FTB + tt * 128:
                                             hc * FTB + (tt + 1) * 128],
                                         wds[:, sw:sw + 512],
                                         start=(hc == 0), stop=False
                                         ).then_inc(S["dmm"], 1)
                        for tt in range(8):
                            for j in range(4):
                                dc = eb * 4 + j
                                t.matmul(pP[tt][:, :],
                                         x1b[:, dc * FTB + tt * 128:
                                             dc * FTB + (tt + 1) * 128],
                                         identE[:, j * 512:(j + 1) * 512],
                                         start=False, stop=(j == 3)
                                         ).then_inc(S["dmm"], 1)

            @block.sync
            def _(sy):
                for tb in range(2):
                    for eb in range(NEB):
                        for tt in range(8):
                            ge = tb * 32 + eb * 8 + tt
                            sy.wait_ge(S["dev"], ge + 1)
                            if ge >= 1:
                                sy.wait_ge(S["dst"], 16 * ge)
                            sy.dma_start(
                                opre[tb * FTB + tt * 128:
                                     tb * FTB + (tt + 1) * 128,
                                     eb * 512:(eb + 1) * 512],
                                dstg[:, (ge % 2) * 512:(ge % 2 + 1) * 512]
                                ).then_inc(S["dst"], 16)
                sy.wait_ge(S["dst"], 16 * 64)

    if 5 in phases:
        NT5 = TH // 128
        with (
            nc.sbuf_tensor("ops", [128, 2 * DM], F32) as ops,
            nc.sbuf_tensor("bn5", [128, 24], F32) as bn5,
            nc.sbuf_tensor("mv5", [128, 2], F32) as mv5,
            nc.sbuf_tensor("l5a", [128, 1], F32) as l5a,
            nc.sbuf_tensor("l5b", [128, 1], F32) as l5b,
            nc.sbuf_tensor("l5c", [128, 1], F32) as l5c,
            nc.sbuf_tensor("l5d", [128, 1], F32) as l5d,
            nc.sbuf_tensor("o5s", [128, 2 * DM], F32) as o5s,
            nc.Block() as block,
        ):
            @block.sync
            def _(sy):
                for tt in range(NT5):
                    if tt >= 1:
                        sy.wait_ge(S["p5ap"], tt)
                        if tt >= 2:
                            sy.wait_ge(S["p5st"], 16 * (tt - 1))
                        sy.dma_start(out[(tt - 1) * 128: tt * 128, :],
                                     o5s[:, ((tt - 1) % 2) * DM:
                                         ((tt - 1) % 2 + 1) * DM]
                                     ).then_inc(S["p5st"], 16)
                sy.wait_ge(S["p5ap"], NT5)
                sy.wait_ge(S["p5st"], 16 * (NT5 - 1))
                sy.dma_start(out[(NT5 - 1) * 128: NT5 * 128, :],
                             o5s[:, ((NT5 - 1) % 2) * DM:
                                 ((NT5 - 1) % 2 + 1) * DM]
                             ).then_inc(S["p5st"], 16)
                sy.wait_ge(S["p5st"], 16 * NT5)

            @block.vector
            def _(v):
                for tt in range(NT5):
                    v.drain()
                    v.wait_ge(S["p5l"], 16 * (tt + 1))
                    v.bn_aggr(mv5[:, :], bst5[:, tt * 24:(tt + 1) * 24])
                    v.drain()
                    v.tensor_scalar(l5a[:, :], mv5[:, 1:2], EPS, None,
                                    ALU.add, ALU.bypass).then_inc(S["p5bn"], 1)
                    v.wait_ge(S["p5sq"], tt + 1)
                    v.reciprocal(l5c[:, :], l5b[:, :]).then_inc(S["l5u"], 1)
                    v.drain()
                    v.tensor_scalar(l5d[:, :], mv5[:, 0:1], l5c[:, :], -1.0,
                                    ALU.mult, ALU.mult)
                    v.drain()
                    if tt >= 2:
                        v.wait_ge(S["p5st"], 16 * (tt - 1))
                    for eb in range(4):
                        v.tensor_scalar(o5s[:, (tt % 2) * DM + eb * 512:
                                            (tt % 2) * DM + (eb + 1) * 512],
                                        ops[:, (tt % 2) * DM + eb * 512:
                                            (tt % 2) * DM + (eb + 1) * 512],
                                        l5c[:, :], l5d[:, :],
                                        ALU.mult, ALU.add)
                    v.engine_nop().then_inc(S["p5ap"], 1)

            @block.scalar
            def _(s):
                for tt in range(NT5):
                    if tt >= 2:
                        s.wait_ge(S["p5ap"], tt - 1)
                    if tt >= 1:
                        s.wait_ge(S["p5l"], 16 * tt)
                    s.dma_start(ops[:, (tt % 2) * DM:(tt % 2 + 1) * DM],
                                opre[tt * 128:(tt + 1) * 128, :]
                                ).then_inc(S["p5l"], 16)
                    s.wait_ge(S["p5bn"], tt + 1)
                    if tt >= 1:
                        s.wait_ge(S["l5u"], tt)
                    s.sqrt(l5b[:, :], l5a[:, :]).then_inc(S["p5sq"], 1)

    if dbg:
        with nc.Block() as block:
            @block.gpsimd
            def _(g):
                n = 0
                for name, (dt_, srcd) in dbg.items():
                    if n:
                        g.wait_ge(S["dbgs"], n)
                    g.dma_start(dt_[:, :], srcd[:, :]).then_inc(S["dbgs"], 16)
                    n += 16
                g.wait_ge(S["dbgs"], n)

    return nc



def _consts():
    eye = np.eye(128, dtype=np.float32)
    cm = np.asarray(eye - 1.0 / 128.0, np.float32).astype(ml_dtypes.bfloat16)
    idm = eye.astype(ml_dtypes.bfloat16)
    r = np.arange(128)[:, None]
    c = np.arange(QB)[None, :]
    masks = np.concatenate(
        [(c >= r + 128 * j).astype(np.float32) for j in range(4)],
        axis=1).astype(ml_dtypes.bfloat16)
    identE = np.zeros((128, 4 * 512), np.float32)
    for j in range(4):
        identE[np.arange(128), j * 512 + j * 128 + np.arange(128)] = 1.0
    identE = identE.astype(ml_dtypes.bfloat16)
    onescol = np.ones((128, 1), ml_dtypes.bfloat16)
    onesrow = np.ones((1, 128), ml_dtypes.bfloat16)
    return cm, idm, masks, identE, onescol, onesrow


def make_in_maps(inputs):
    x = np.asarray(inputs["L0_attn_hidden_states"], np.float32)
    Wq = np.asarray(inputs["L0_attn_Wq"], np.float32)
    Wk = np.asarray(inputs["L0_attn_Wk"], np.float32)
    Wv = np.asarray(inputs["L0_attn_Wv"], np.float32)
    Wo = np.asarray(inputs["L0_attn_Wo"], np.float32)
    gamma = np.asarray(inputs["L0_attn_gamma"], np.float32)
    beta = np.asarray(inputs["L0_attn_beta"], np.float32)
    wqg = np.asarray(inputs["L0_ffn_wq_gate"], np.int32)
    sgv = np.asarray(inputs["L0_ffn_s_gate"], np.float32)
    wqu = np.asarray(inputs["L0_ffn_wq_up"], np.int32)
    suv = np.asarray(inputs["L0_ffn_s_up"], np.float32)
    wqd = np.asarray(inputs["L0_ffn_wq_down"], np.int32)
    sdv = np.asarray(inputs["L0_ffn_s_down"], np.float32)

    cm, idm, masks, identE, onescol, onesrow = _consts()
    woTh = np.ascontiguousarray(
        Wo.T.reshape(16, 128, DM).transpose(1, 0, 2).reshape(128, 16 * DM))
    wgTh = np.ascontiguousarray(
        wqg.T.reshape(16, 128, 22, 256).transpose(1, 2, 0, 3)
        .reshape(128, 22 * 16 * 256).astype(np.uint8))
    wuTh = np.ascontiguousarray(
        wqu.T.reshape(16, 128, 22, 256).transpose(1, 2, 0, 3)
        .reshape(128, 22 * 16 * 256).astype(np.uint8))
    wdTh = np.ascontiguousarray(
        wqd.T.reshape(NHC, 128, 4, 512).transpose(1, 2, 0, 3)
        .reshape(128, 4 * NHC * 512).astype(np.uint8))
    sg2 = np.ascontiguousarray(sgv.reshape(NHC, 128).T)
    su2 = np.ascontiguousarray(suv.reshape(NHC, 128).T)
    sdr = np.ascontiguousarray(sdv[None, :])

    maps = []
    for c in range(8):
        b, hg = c // 2, c % 2
        p = np.arange(128, dtype=np.int64)[:, None]
        i = np.arange(16, dtype=np.int64)[None, :]
        slot = 2 * b + i // 8
        cum = np.asarray(AG_CUM, dtype=np.int64)
        grp_row = hg * 8 + (i % 8)
        ci = np.searchsorted(cum, grp_row, side="right") - 1
        rows = (cum[ci + 1] - cum[ci]) * 128
        oix = (cum[ci] * 128 * 8 + slot * rows
               + (grp_row - cum[ci]) * 128 + p).astype(np.uint32)
        maps.append({
            "xT": np.ascontiguousarray(
                x[b].T.reshape(16, 128, NTB, TB).transpose(1, 2, 0, 3)
                .reshape(128, NTB * 16 * TB)),
            "xres": np.ascontiguousarray(x[b, hg * TH:(hg + 1) * TH]),
            "wq": np.ascontiguousarray(Wq[hg * EQ:(hg + 1) * EQ]),
            "wk": np.ascontiguousarray(Wk[hg * EQ:(hg + 1) * EQ]),
            "wvT": np.ascontiguousarray(
                Wv[hg * EQ:(hg + 1) * EQ].T.reshape(16, 128, EQ)
                .transpose(1, 0, 2).reshape(128, 16 * EQ)),
            "woT": woTh, "wgT": wgTh, "wuT": wuTh, "wdT": wdTh,
            "sg": sg2, "su": su2, "sdrow": sdr,
            "gam": np.ascontiguousarray(gamma[:, None]),
            "bet": np.ascontiguousarray(beta[:, None]),
            "cmat": cm, "identm": idm, "masks": masks, "identE": identE,
            "onescol": onescol, "onesrow": onesrow,
            "onesrowf": np.ones((1, 128), np.float32), "oidx": oix,
        })
    return maps


def kernel(**inputs):
    maps = make_in_maps(inputs)
    nc = build()
    r = run_bass_kernel_spmd(nc, maps, core_ids=list(range(8)))
    outf = np.empty((4, T, DM), np.float32)
    for c in range(8):
        b, hg = c // 2, c % 2
        outf[b, hg * TH:(hg + 1) * TH] = np.asarray(r.results[c]["out"])
    return outf


# version 24
# speedup vs baseline: 1.0531x; 1.0110x over previous
import contextlib
import numpy as np
import ml_dtypes
import concourse.bass as bass
import concourse.mybir as mybir
from concourse.bass_utils import run_bass_kernel_spmd
from concourse.bass_core_rust import IndirectOffsetOnAxis

F32 = mybir.dt.float32
U8 = mybir.dt.uint8
BF16 = mybir.dt.bfloat16
I32 = mybir.dt.int32
U32 = mybir.dt.uint32
AFT = mybir.ActivationFunctionType
ALU = mybir.AluOpType

T = 4096
DM = 2048
NH = 8
D = 128
EQ = NH * D
HID = 5632
NHC = HID // 128
TH = 2048
TB = 512
NTB = T // TB
QB = 512
NQB = T // QB
FTB = 1024
EPS = 1e-5
DEPS = float(128 * EPS)
AG_CUM = [0, 4, 8, 12, 14, 15, 16]
P3_A = [0, 1, 2, 3, 8, 9, 10, 11]
P3_B = [4, 5, 12, 13]
P3_C = [6, 14]
P3_D = [7, 15]
C1 = float(EPS / 128.0)
C2 = float(EPS * EPS)


def _r(ap, pat, **kw):
    return ap.rearrange(pat, **kw)


def build(phases=(1, 2, 3, 4, 5), debug_outs=()):
    nc = bass.Bass("TRN2", target_bir_lowering=False)

    xT = nc.dram_tensor("xT", [128, NTB * 16 * TB], F32, kind="ExternalInput")
    xres = nc.dram_tensor("xres", [TH, DM], F32, kind="ExternalInput")
    wqd_ = nc.dram_tensor("wq", [EQ, DM], F32, kind="ExternalInput")
    wkd_ = nc.dram_tensor("wk", [EQ, DM], F32, kind="ExternalInput")
    wvT = nc.dram_tensor("wvT", [128, 16 * EQ], F32, kind="ExternalInput")
    woT = nc.dram_tensor("woT", [128, 16 * DM], F32, kind="ExternalInput")
    wgT = nc.dram_tensor("wgT", [128, 22 * 16 * 256], U8, kind="ExternalInput")
    wuT = nc.dram_tensor("wuT", [128, 22 * 16 * 256], U8, kind="ExternalInput")
    wdT = nc.dram_tensor("wdT", [128, 4 * NHC * 512], U8, kind="ExternalInput")
    sgd = nc.dram_tensor("sg", [128, NHC], F32, kind="ExternalInput")
    sud = nc.dram_tensor("su", [128, NHC], F32, kind="ExternalInput")
    sdrow = nc.dram_tensor("sdrow", [1, DM], F32, kind="ExternalInput")
    gamd = nc.dram_tensor("gam", [128, 1], F32, kind="ExternalInput")
    betd = nc.dram_tensor("bet", [128, 1], F32, kind="ExternalInput")
    cmatd = nc.dram_tensor("cmat", [128, 128], BF16, kind="ExternalInput")
    identd = nc.dram_tensor("identm", [128, 128], BF16, kind="ExternalInput")
    masksd = nc.dram_tensor("masks", [128, 4 * QB], BF16, kind="ExternalInput")
    identEd = nc.dram_tensor("identE", [128, 4 * 512], BF16, kind="ExternalInput")
    onescd = nc.dram_tensor("onescol", [128, 1], BF16, kind="ExternalInput")
    onesrd = nc.dram_tensor("onesrow", [1, 128], BF16, kind="ExternalInput")
    onesrfd = nc.dram_tensor("onesrowf", [1, 128], F32, kind="ExternalInput")
    oidxd = nc.dram_tensor("oidx", [128, 16], U32, kind="ExternalInput")
    out = nc.dram_tensor("out", [TH, DM], F32, kind="ExternalOutput")

    qs = nc.dram_tensor("qs", [EQ, T], BF16)
    ks = nc.dram_tensor("ks", [EQ, T], BF16)
    vs = nc.dram_tensor("vs", [128, NH * T], BF16)
    vqh = nc.dram_tensor("vqh", [NH, T], F32)
    agin = nc.dram_tensor("agin", [2 * EQ, TH], BF16)
    agout = nc.dram_tensor("agout", [8 * 2 * EQ, TH], BF16)
    x1f = nc.dram_tensor("x1f", [DM, TH], BF16)
    opre = nc.dram_tensor("opre", [TH, DM], F32)

    dbg = {}
    dbg_specs = {"d_qs": (qs, BF16), "d_ks": (ks, BF16), "d_vs": (vs, BF16),
                 "d_vqh": (vqh, F32), "d_agin": (agin, BF16),
                 "d_agout": (agout, BF16), "d_x1f": (x1f, BF16),
                 "d_opre": (opre, F32)}
    for name in debug_outs:
        src, dt = dbg_specs[name]
        dbg[name] = (nc.dram_tensor(name, list(src.shape), dt,
                                    kind="ExternalOutput"), src)

    keep = []

    def sb(name, shape, dtype):
        h = nc.alloc_sbuf_tensor(name, shape, dtype)
        keep.append(h)
        return h

    SEM_NAMES = [
        "c_ld", "c_mm", "c_ev", "wld", "wsc", "wmm", "wev",
        "xld", "p1mm", "sq2", "vev", "vqm", "vqe", "ksm", "rsa", "rsq", "rsd",
        "spq", "spk", "spv", "spvq",
        "kvld", "qld", "vqld", "p2ms", "sqa", "mka", "p2oz", "zrv", "denv",
        "bcm", "recv", "oev", "ost",
        "agc", "gth", "wol", "xrl", "p3mm", "z1v", "lnv", "p3sq", "apv",
        "trm", "trv", "x1st",
        "x1l", "wldg", "deq", "gumm", "sil", "gml",
        "wdl", "wdq", "dmm", "dev", "dst",
        "p5l", "p5bn", "p5sq", "p5ap", "p5st", "l3u", "l5u", "sqv", "kvv", "wdc", "mld", "ield", "gth1", "gth2", "gth3", "rcs",
        "dbgs",
    ]
    S = {n: nc.alloc_semaphore(name=n) for n in SEM_NAMES}

    cmat = sb("cmat_s", [128, 128], BF16)
    ident = sb("ident_s", [128, 128], BF16)
    onesc = sb("onesc_s", [128, 1], BF16)
    onesr = sb("onesr_s", [1, 128], BF16)
    onesrf = sb("onesrf_s", [1, 128], F32)
    oidx = sb("oidx_s", [128, 16], U32)
    gam = sb("gam_s", [128, 1], F32)
    betbf = sb("betbf_s", [128, 1], BF16)
    cbeta = sb("cbeta_s", [128, 1], F32)
    sg = sb("sg_s", [128, NHC], F32)
    su = sb("su_s", [128, NHC], F32)
    sdbc = sb("sdbc_s", [128, DM], BF16)
    rsk = sb("rsk_s", [128, NH * 32], F32)
    bst5 = sb("bst5_s", [128, 64 * 6], F32)

    with (
        nc.sbuf_tensor("wtq", [128, 16 * EQ], BF16) as wtq,
        nc.sbuf_tensor("wtk", [128, 16 * EQ], BF16) as wtk,
        nc.sbuf_tensor("wvs", [128, 16 * EQ], BF16) as wvs,
    ):
        with (
            nc.sbuf_tensor("sdbf_s", [1, DM], BF16) as sdbf,
            nc.sbuf_tensor("wraw", [128, DM], BF16) as wraw,
            nc.sbuf_tensor("wgam", [128, DM], BF16) as wgam,
            nc.psum_tensor("p0a", [128, 128], F32) as p0a,
            nc.psum_tensor("p0b", [128, 128], F32) as p0b,
            nc.psum_tensor("p0c", [128, 512], F32) as p0c,
            nc.Block() as block,
        ):
            @block.gpsimd
            def _(g):
                nld = 0
                for dst, srcd in [(cmat, cmatd), (ident, identd),
                                  (onesc, onescd),
                                  (onesr, onesrd), (oidx, oidxd), (gam, gamd),
                                  (sg, sgd), (su, sud), (betbf, betd),
                                  (sdbf, sdrow), (onesrf, onesrfd)]:
                    if nld:
                        g.wait_ge(S["c_ld"], nld)
                    g.dma_start(dst[:, :], srcd[:, :]).then_inc(S["c_ld"], 16)
                    nld += 16
                g.wait_ge(S["c_ld"], nld)
                g.dma_start(wvs[:, :], wvT[:, :]).then_inc(S["c_ld"], 16)
                for k in range(16):
                    wsrc = wqd_ if k < 8 else wkd_
                    h = k % 8
                    if k >= 1:
                        g.wait_ge(S["wsc"], k)
                        g.wait_ge(S["wld"], 16 * k)
                    g.dma_start(wraw[:, :], wsrc[h * 128:(h + 1) * 128, :]
                                ).then_inc(S["wld"], 16)

            @block.vector
            def _(v):
                v.wait_ge(S["c_mm"], 1)
                v.tensor_copy(cbeta[:, :], p0a[:, 0:1])
                v.engine_nop().then_inc(S["c_ev"], 1)
                for ebk in range(4):
                    v.wait_ge(S["c_mm"], 2 + ebk)
                    v.tensor_copy(sdbc[:, ebk * 512:(ebk + 1) * 512], p0c[:, :]
                                  ).then_inc(S["c_ev"], 1)
                for k in range(16):
                    v.wait_ge(S["wld"], 16 * (k + 1))
                    if k >= 1:
                        v.wait_ge(S["wmm"], 16 * k)
                    v.tensor_scalar(wgam[:, :], wraw[:, :], gam[:, :], None,
                                    ALU.mult, ALU.bypass).then_inc(S["wsc"], 1)
                    wt = wtq if k < 8 else wtk
                    h = k % 8
                    for fc in range(16):
                        v.wait_ge(S["wmm"], 16 * k + fc + 1)
                        bank = p0a if fc % 2 == 0 else p0b
                        v.tensor_copy(
                            wt[:, fc * EQ + h * 128: fc * EQ + (h + 1) * 128],
                            bank[:, :]).then_inc(S["wev"], 1)

            @block.tensor
            def _(t):
                t.wait_ge(S["c_ld"], 9 * 16)
                t.matmul(p0a[:, 0:1], cmat[:, :], betbf[:, :],
                         start=True, stop=True).then_inc(S["c_mm"], 1)
                t.wait_ge(S["c_ld"], 10 * 16)
                for ebk in range(4):
                    if ebk >= 1:
                        t.wait_ge(S["c_ev"], 1 + ebk)
                    t.matmul(p0c[:, :], onesr[:, :],
                             sdbf[:, ebk * 512:(ebk + 1) * 512],
                             start=True, stop=True).then_inc(S["c_mm"], 1)
                t.wait_ge(S["c_ev"], 5)
                for k in range(16):
                    t.wait_ge(S["wsc"], k + 1)
                    for fc in range(16):
                        gidx = 16 * k + fc
                        if gidx >= 2:
                            t.wait_ge(S["wev"], gidx - 1)
                        bank = p0a if fc % 2 == 0 else p0b
                        t.matmul(bank[:, :], wgam[:, fc * 128:(fc + 1) * 128],
                                 cmat[:, :], start=True, stop=True
                                 ).then_inc(S["wmm"], 1)

        if 1 in phases:
            with (
                nc.sbuf_tensor("xTs", [128, 2 * 16 * TB], BF16) as xTs,
                nc.sbuf_tensor("qcs", [128, 8 * TB], BF16) as qcs,
                nc.sbuf_tensor("kcs", [128, 8 * TB], BF16) as kcs,
                nc.sbuf_tensor("vvs", [128, 8 * TB], BF16) as vvs,
                nc.sbuf_tensor("q2s", [128, 16 * TB], BF16) as q2s,
                nc.sbuf_tensor("k2s", [128, 16 * TB], BF16) as k2s,
                nc.sbuf_tensor("vqr", [1, 2 * TB], F32) as vqr,
                nc.sbuf_tensor("rs1", [128, 32], F32) as rs1,
                nc.sbuf_tensor("rs2", [128, 32], F32) as rs2,
                nc.psum_tensor("pA", [128, TB], F32) as pA,
                nc.psum_tensor("pB", [128, TB], F32) as pB,
                nc.psum_tensor("pC", [128, TB], F32) as pC,
                nc.psum_tensor("pD", [128, TB], F32) as pD,
                nc.psum_tensor("pKS", [128, 32], F32) as pKS,
                nc.psum_tensor("pVQ0", [1, TB], F32) as pVQ0,
                nc.psum_tensor("pVQ1", [1, TB], F32) as pVQ1,
                nc.Block() as block,
            ):
                banks = [pA, pB, pC, pD]
                pVQs = [pVQ0, pVQ1]

                def xv(buf, fc, lo, hi):
                    base = buf * 16 * TB + fc * TB
                    return xTs[:, base + lo: base + hi]

                @block.gpsimd
                def _(g):
                    for tb in range(NTB):
                        if tb >= 2:
                            g.wait_ge(S["p1mm"], 384 * (tb - 1))
                        if tb >= 1:
                            g.wait_ge(S["xld"], 16 * tb)
                        g.dma_start(
                            xTs[:, (tb % 2) * 16 * TB:(tb % 2 + 1) * 16 * TB],
                            xT[:, tb * 16 * TB:(tb + 1) * 16 * TB]
                            ).then_inc(S["xld"], 16)

                @block.tensor
                def _(t):
                    t.wait_ge(S["c_ld"], 12 * 16)
                    for tb in range(NTB):
                        t.wait_ge(S["xld"], 16 * (tb + 1))
                        buf = tb % 2
                        for i in range(24):
                            gidx = tb * 24 + i
                            bank = banks[gidx % 4]
                            if gidx >= 4:
                                t.wait_ge(S["vev"], gidx - 3)
                            if i < 8:
                                for fc in range(16):
                                    t.matmul(bank[:, :],
                                             wtq[:, fc * EQ + i * 128:
                                                 fc * EQ + (i + 1) * 128],
                                             xv(buf, fc, 0, TB),
                                             start=(fc == 0), stop=(fc == 15)
                                             ).then_inc(S["p1mm"], 1)
                            elif i < 16:
                                e = i - 8
                                for fc in range(16):
                                    t.matmul(bank[:, :],
                                             wtk[:, fc * EQ + e * 128:
                                                 fc * EQ + (e + 1) * 128],
                                             xv(buf, fc, 0, TB),
                                             start=(fc == 0), stop=(fc == 15)
                                             ).then_inc(S["p1mm"], 1)
                            else:
                                tt = (i - 16) // 2
                                eb = (i - 16) % 2
                                for fc in range(16):
                                    t.matmul(bank[:, :],
                                             xv(buf, fc, tt * 128, (tt + 1) * 128),
                                             wvs[:, fc * EQ + eb * 512:
                                                 fc * EQ + (eb + 1) * 512],
                                             start=(fc == 0), stop=(fc == 15)
                                             ).then_inc(S["p1mm"], 1)
                        for h in range(NH):
                            t.wait_ge(S["sq2"], 16 * tb + h + 1)
                            if tb * 8 + h >= 2:
                                t.wait_ge(S["vqe"], tb * 8 + h - 1)
                            t.matmul(pVQs[(tb * 8 + h) % 2][:, :], onesc[:, :],
                                     q2s[:, (tb % 2) * 8 * TB + h * TB:
                                         (tb % 2) * 8 * TB + (h + 1) * TB],
                                     start=True, stop=True).then_inc(S["vqm"], 1)
                        if tb >= 1:
                            t.wait_ge(S["rsd"], tb)
                        for h in range(NH):
                            t.wait_ge(S["sq2"], 16 * tb + 8 + h + 1)
                            for tt in range(4):
                                col = h * 4 + tt
                                t.matmul(pKS[:, col:col + 1],
                                         k2s[:, (tb % 2) * 8 * TB + h * TB + tt * 128:
                                             (tb % 2) * 8 * TB + h * TB + (tt + 1) * 128],
                                         onesc[:, :], start=True, stop=True
                                         ).then_inc(S["ksm"], 1)

                @block.scalar
                def _(s):
                    for tb in range(NTB):
                        for i in range(16):
                            gidx = tb * 24 + i
                            h = i % 8
                            s.wait_ge(S["p1mm"], gidx * 16 + 16)
                            if tb >= 2:
                                if i < 8:
                                    s.wait_ge(S["vqm"], 8 * (tb - 2) + h + 1)
                                else:
                                    s.wait_ge(S["ksm"], 32 * (tb - 2) + 4 * (h + 1))
                            bank = banks[gidx % 4]
                            st = q2s if i < 8 else k2s
                            so = (tb % 2) * 8 * TB
                            s.activation(st[:, so + h * TB:so + (h + 1) * TB],
                                         bank[:, :],
                                         AFT.Square, bias=cbeta[:, :], scale=1.0
                                         ).then_inc(S["sq2"], 1)
                        for h in range(NH):
                            gq = tb * 8 + h
                            s.wait_ge(S["vqe"], gq + 1)
                            if gq >= 1:
                                s.wait_ge(S["spvq"], 16 * gq)
                            s.dma_start(vqh[h:h + 1, tb * TB:(tb + 1) * TB],
                                        vqr[:, (gq % 2) * TB:(gq % 2 + 1) * TB]
                                        ).then_inc(S["spvq"], 16)
                        s.wait_ge(S["rsa"], tb + 1)
                        if tb >= 1:
                            s.wait_ge(S["rsd"], tb)
                        s.sqrt(rs2[:, :], rs1[:, :]).then_inc(S["rsq"], 1)

                @block.vector
                def _(v):
                    for tb in range(NTB):
                        for i in range(24):
                            gidx = tb * 24 + i
                            bank = banks[gidx % 4]
                            if i < 16:
                                h = i % 8
                                v.wait_ge(S["sq2"], 16 * tb + i + 1)
                                if tb >= 1:
                                    sp = S["spq"] if i < 8 else S["spk"]
                                    v.wait_ge(sp, 16 * (8 * (tb - 1) + h + 1))
                                st = qcs if i < 8 else kcs
                                v.tensor_scalar(st[:, h * TB:(h + 1) * TB],
                                                bank[:, :], cbeta[:, :], None,
                                                ALU.add, ALU.bypass
                                                ).then_inc(S["vev"], 1)
                            else:
                                j = i - 16
                                v.wait_ge(S["p1mm"], gidx * 16 + 16)
                                if tb >= 1:
                                    v.wait_ge(S["spv"],
                                              16 * (8 * (tb - 1) + j + 1))
                                v.tensor_copy(vvs[:, j * TB:(j + 1) * TB],
                                              bank[:, :]).then_inc(S["vev"], 1)
                        for h in range(NH):
                            gq = tb * 8 + h
                            v.wait_ge(S["vqm"], gq + 1)
                            if gq >= 2:
                                v.wait_ge(S["spvq"], 16 * (gq - 1))
                            v.tensor_copy(vqr[:, (gq % 2) * TB:(gq % 2 + 1) * TB],
                                          pVQs[gq % 2][:, :]).then_inc(S["vqe"], 1)
                        v.wait_ge(S["ksm"], 32 * (tb + 1))
                        v.tensor_scalar(rs1[:, :], pKS[:, :], DEPS, None,
                                        ALU.add, ALU.bypass).then_inc(S["rsa"], 1)
                        v.wait_ge(S["rsq"], tb + 1)
                        v.reciprocal(
                            _r(rsk[:, :], "p (h k) -> p h k", h=NH)
                            [:, :, tb * 4:(tb + 1) * 4],
                            _r(rs2[:, :], "p (h k) -> p h k", h=NH)
                        ).then_inc(S["rsd"], 1)

                @block.sync
                def _(sy):
                    for tb in range(NTB):
                        for i in range(24):
                            sy.wait_ge(S["vev"], tb * 24 + i + 1)
                            if i < 8:
                                if tb * 8 + i >= 1:
                                    sy.wait_ge(S["spq"], 16 * (tb * 8 + i))
                                sy.dma_start(qs[i * 128:(i + 1) * 128,
                                                tb * TB:(tb + 1) * TB],
                                             qcs[:, i * TB:(i + 1) * TB]
                                             ).then_inc(S["spq"], 16)
                            elif i < 16:
                                h = i - 8
                                if tb * 8 + h >= 1:
                                    sy.wait_ge(S["spk"], 16 * (tb * 8 + h))
                                sy.dma_start(ks[h * 128:(h + 1) * 128,
                                                tb * TB:(tb + 1) * TB],
                                             kcs[:, h * TB:(h + 1) * TB]
                                             ).then_inc(S["spk"], 16)
                            else:
                                j = i - 16
                                tt, eb = j // 2, j % 2
                                if tb * 8 + j >= 1:
                                    sy.wait_ge(S["spv"], 16 * (tb * 8 + j))
                                sy.dma_start(
                                    _r(vs[:, :], "p (h k d) -> p h k d",
                                       h=NH, k=32)
                                    [:, 4 * eb:4 * eb + 4, tb * 4 + tt, :],
                                    _r(vvs[:, j * TB:(j + 1) * TB],
                                       "p (hh d) -> p hh d", hh=4)
                                    ).then_inc(S["spv"], 16)
                    sy.wait_ge(S["spq"], 16 * 64)
                    sy.wait_ge(S["spk"], 16 * 64)
                    sy.wait_ge(S["spv"], 16 * 64)
                    sy.wait_ge(S["spvq"], 16 * 64)

    wos_cm = (nc.sbuf_tensor("wos", [128, 16 * DM], BF16)
              if (2 in phases or 3 in phases) else None)
    wos = wos_cm.__enter__() if wos_cm is not None else None
    if 2 in phases:
        seq = [(h, half * 4 + j) for half in range(2)
               for h in range(NH) for j in range(4)]
        nkt = {qb: 4 * (qb + 1) for qb in range(NQB)}
        ga_base = {}
        oz_end = {}
        run = 0
        ozr = 0
        for (h, qb) in seq:
            ga_base[(h, qb)] = run
            run += nkt[qb]
            ozr += 2 * nkt[qb]
            oz_end[(h, qb)] = ozr
        mk_idx = {}
        mrun = 0
        for (h, qb) in seq:
            for kt in range(nkt[qb] - 4, nkt[qb]):
                mk_idx[(h, qb, kt)] = mrun
                mrun += 1

        def is_masked(qb, kt):
            return kt >= nkt[qb] - 4

        def sq_on_vector(kt):
            return kt % 3 == 2 and kt > 5

        cntS, cntV = {}, {}
        _cs = _cv = 0
        for (_h, _qb) in seq:
            for _kt in range(nkt[_qb]):
                if sq_on_vector(_kt):
                    _cv += 1
                else:
                    _cs += 1
                _ga = ga_base[(_h, _qb)] + _kt
                cntS[_ga], cntV[_ga] = _cs, _cv

        sq_owner_vec = {}
        for (_h, _qb) in seq:
            for _kt in range(nkt[_qb]):
                sq_owner_vec[ga_base[(_h, _qb)] + _kt] = sq_on_vector(_kt)

        def wait_sq(eng, ga):
            if sq_owner_vec[ga]:
                eng.wait_ge(S["sqv"], cntV[ga])
            else:
                eng.wait_ge(S["sqa"], cntS[ga])

        with contextlib.ExitStack() as _st:
            _e = _st.enter_context
            masks = _e(nc.sbuf_tensor("masks_s", [128, 4 * QB], BF16))
            khs = _e(nc.sbuf_tensor("khs", [128, 2 * T], BF16))
            vhs = _e(nc.sbuf_tensor("vhs", [128, 2 * 32 * 128], BF16))
            qbs = _e(nc.sbuf_tensor("qbs", [128, 4 * QB], BF16))
            a_s = _e(nc.sbuf_tensor("a_s", [128, 3 * QB], BF16))
            ams = _e(nc.sbuf_tensor("ams", [128, 3 * QB], BF16))
            vqhd = _e(nc.sbuf_tensor("vqhd", [1, T], F32))
            zrow = _e(nc.sbuf_tensor("zrow", [1, QB], F32))
            lnr = _e(nc.sbuf_tensor("lnr", [1, QB], F32))
            rcr = _e(nc.sbuf_tensor("rcr", [1, QB], F32))
            den1 = _e(nc.sbuf_tensor("den1", [1, 4 * QB], F32))
            den2 = _e(nc.sbuf_tensor("den2", [1, QB], F32))
            recb = _e(nc.sbuf_tensor("recb", [128, QB], F32))
            vsq = _e(nc.sbuf_tensor("vsq", [128, QB], BF16))
            obs = _e(nc.sbuf_tensor("obs", [128, 2 * QB], BF16))
            pS0 = _e(nc.psum_tensor("pS0", [128, QB], F32))
            pS1 = _e(nc.psum_tensor("pS1", [128, QB], F32))
            pS2 = _e(nc.psum_tensor("pS2", [128, QB], F32))
            pO0 = _e(nc.psum_tensor("pO0", [128, QB], F32))
            pO1 = _e(nc.psum_tensor("pO1", [128, QB], F32))
            pZ0 = _e(nc.psum_tensor("pZ0", [1, QB], F32))
            pZ1 = _e(nc.psum_tensor("pZ1", [1, QB], F32))
            pBC = _e(nc.psum_tensor("pBC", [128, QB], F32))
            block = _e(nc.Block())
            True
            pS = [pS0, pS1, pS2]
            pOs = [pO0, pO1]
            pZs = [pZ0, pZ1]

            @block.gpsimd
            def _(g):
                if 3 not in phases:
                    return
                g.dma_start(wos[:, :], woT[:, :]).then_inc(S["wol"], 16)
                for c4 in range(6):
                    g.wait_ge(S["ost"], 16 * 4 * AG_CUM[c4 + 1])
                    g.collective_compute(
                        "AllGather", ALU.bypass,
                        replica_groups=[[0, 1, 2, 3, 4, 5, 6, 7]],
                        ins=[agin[AG_CUM[c4] * 128:
                                  AG_CUM[c4 + 1] * 128, :].opt()],
                        outs=[agout[AG_CUM[c4] * 128 * 8:
                                    AG_CUM[c4 + 1] * 128 * 8, :].opt()],
                    ).then_inc(S["agc"], 1)

            @block.sync
            def _(sy):
                sy.dma_start(khs[:, 0:T], ks[0:128, :]
                             ).then_inc(S["kvld"], 16)
                sy.dma_start(vhs[:, 0:32 * 128], vs[:, 0:T]
                             ).then_inc(S["kvv"], 16)
                sy.dma_start(masks[:, :], masksd[:, :]).then_inc(S["mld"], 16)
                def qload(gl):
                    hp, qp = seq[gl]
                    if gl >= 4:
                        hx, qx = seq[gl - 4]
                        sy.wait_ge(S["p2ms"], ga_base[(hx, qx)] + nkt[qx])
                    if gl >= 1:
                        sy.wait_ge(S["qld"], 16 * gl)
                    sy.dma_start(qbs[:, (gl % 4) * QB:(gl % 4 + 1) * QB],
                                 qs[hp * 128:(hp + 1) * 128,
                                    qp * QB:(qp + 1) * QB]
                                 ).then_inc(S["qld"], 16)

                qload(0)
                qload(1)
                for gq in range(64):
                    if gq + 2 < 64:
                        qload(gq + 2)
                    if gq >= 1:
                        sy.wait_ge(S["oev"], gq)
                        if gq >= 2:
                            sy.wait_ge(S["ost"], 16 * (gq - 1))
                        hp, qp = seq[gq - 1]
                        hfp, qq = qp // 4, qp % 4
                        sy.dma_start(
                            agin[hfp * EQ + hp * 128:
                                 hfp * EQ + (hp + 1) * 128,
                                 qq * QB:(qq + 1) * QB],
                            obs[:, ((gq - 1) % 2) * QB:((gq - 1) % 2 + 1) * QB]
                            ).then_inc(S["ost"], 16)
                sy.wait_ge(S["oev"], 64)
                sy.wait_ge(S["ost"], 16 * 63)
                sy.dma_start(agin[EQ + 7 * 128: EQ + 8 * 128, 3 * QB:4 * QB],
                             obs[:, (63 % 2) * QB:(63 % 2 + 1) * QB]
                             ).then_inc(S["ost"], 16)
                sy.wait_ge(S["ost"], 16 * 64)

            def kv_load(s, grp):
                h = grp % 8
                if grp >= 2:
                    hp, qp = seq[(grp - 1) * 4 - 1]
                    s.wait_ge(S["p2oz"], oz_end[(hp, qp)])
                if grp >= 1:
                    s.wait_ge(S["kvld"], 16 * grp)
                    s.wait_ge(S["kvv"], 16 * grp)
                if grp == 1:
                    s.wait_ge(S["spk"], 16 * (57 + h))
                    s.wait_ge(S["spv"], 16 * 63)
                kb = (grp % 2) * T
                vb = (grp % 2) * 32 * 128
                s.dma_start(khs[:, kb:kb + T], ks[h * 128:(h + 1) * 128, :]
                            ).then_inc(S["kvld"], 16)
                s.dma_start(vhs[:, vb:vb + 32 * 128],
                            vs[:, h * T:(h + 1) * T]
                            ).then_inc(S["kvv"], 16)

            @block.scalar
            def _(s):
                for grp in range(16):
                    half, h = grp // 8, grp % 8
                    if grp >= 1:
                        s.wait_ge(S["denv"], 4 * grp)
                        s.wait_ge(S["vqld"], 16 * grp)
                    s.dma_start(vqhd[:, :], vqh[h:h + 1, :]
                                ).then_inc(S["vqld"], 16)
                    for j in range(4):
                        gq = grp * 4 + j
                        if j == 1 and grp + 1 < 16:
                            kv_load(s, grp + 1)
                        qb = half * 4 + j
                        for kt in range(nkt[qb]):
                            if sq_on_vector(kt):
                                continue
                            ga = ga_base[(h, qb)] + kt
                            s.wait_ge(S["p2ms"], ga + 1)
                            if ga >= 3:
                                s.wait_ge(S["p2oz"], 2 * (ga - 2))
                            s.activation(a_s[:, (ga % 3) * QB:(ga % 3 + 1) * QB],
                                         pS[ga % 3][:, :], AFT.Square,
                                         bias=0.0,
                                         scale=rsk[:, h * 32 + kt:
                                                   h * 32 + kt + 1]
                                         ).then_inc(S["sqa"], 1)
                            if kt == 1 and gq >= 1:
                                s.wait_ge(S["denv"], gq)
                                if gq >= 2:
                                    s.wait_ge(S["bcm"], gq - 1)
                                s.activation(lnr[:, :], den2[:, :], AFT.Ln,
                                             bias=0.0, scale=1.0)
                                s.drain()
                                s.activation(rcr[:, :], lnr[:, :], AFT.Exp,
                                             bias=0.0, scale=-1.0
                                             ).then_inc(S["rcs"], 1)
                s.wait_ge(S["denv"], 64)
                s.wait_ge(S["bcm"], 63)
                s.activation(lnr[:, :], den2[:, :], AFT.Ln, bias=0.0, scale=1.0)
                s.drain()
                s.activation(rcr[:, :], lnr[:, :], AFT.Exp, bias=0.0,
                             scale=-1.0).then_inc(S["rcs"], 1)

            @block.vector
            def _(v):
                v.wait_ge(S["mld"], 16)
                for grp in range(16):
                    half, h = grp // 8, grp % 8
                    for j in range(4):
                        qb = half * 4 + j
                        gq = grp * 4 + j
                        v.drain()
                        if j == 0:
                            v.wait_ge(S["vqld"], 16 * (grp + 1))
                            v.tensor_scalar(den1[:, :],
                                            vqhd[:, half * 4 * QB:
                                                 (half * 4 + 4) * QB],
                                            C1, C2, ALU.mult, ALU.add)
                            v.drain()
                        for kt in range(nkt[qb]):
                            ga = ga_base[(h, qb)] + kt
                            if sq_on_vector(kt):
                                v.wait_ge(S["p2ms"], ga + 1)
                                if ga >= 3:
                                    v.wait_ge(S["p2oz"], 2 * (ga - 2))
                                v.tensor_scalar(vsq[:, :], pS[ga % 3][:, :],
                                                rsk[:, h * 32 + kt:
                                                    h * 32 + kt + 1],
                                                None, ALU.mult, ALU.bypass)
                                v.drain()
                                v.tensor_tensor(
                                    a_s[:, (ga % 3) * QB:(ga % 3 + 1) * QB],
                                    vsq[:, :], vsq[:, :], ALU.mult
                                    ).then_inc(S["sqv"], 1)
                                v.drain()
                            if not is_masked(qb, kt):
                                continue
                            jm = kt - (nkt[qb] - 4)
                            if not sq_on_vector(kt):
                                wait_sq(v, ga)
                            v.tensor_tensor(
                                ams[:, (ga % 3) * QB:(ga % 3 + 1) * QB],
                                a_s[:, (ga % 3) * QB:(ga % 3 + 1) * QB],
                                masks[:, jm * QB:(jm + 1) * QB],
                                ALU.mult).then_inc(S["mka"], 1)
                        if gq >= 1:
                            gp = gq - 1
                            gpg, gpj = divmod(gp, 4)
                            v.wait_ge(S["p2oz"],
                                      oz_end[(gpg % 8, (gpg // 8) * 4 + gpj)])
                            v.wait_ge(S["bcm"], gp + 1)
                            v.tensor_copy(recb[:, :], pBC[:, :]
                                          ).then_inc(S["recv"], 1)
                            v.drain()
                            if gp >= 2:
                                v.wait_ge(S["ost"], 16 * (gp - 1))
                            v.tensor_tensor(obs[:, (gp % 2) * QB:
                                                (gp % 2 + 1) * QB],
                                            pOs[gp % 2][:, :], recb[:, :],
                                            ALU.mult).then_inc(S["oev"], 1)
                        v.wait_ge(S["p2oz"], oz_end[(h, qb)])
                        v.tensor_copy(zrow[:, :], pZs[gq % 2][:, :]
                                      ).then_inc(S["zrv"], 1)
                        v.drain()
                        if gq >= 1:
                            v.wait_ge(S["rcs"], gq)
                        v.tensor_tensor(den2[:, :],
                                        den1[:, j * QB:(j + 1) * QB],
                                        zrow[:, :],
                                        ALU.add).then_inc(S["denv"], 1)
                gp = 63
                v.drain()
                v.wait_ge(S["bcm"], gp + 1)
                v.tensor_copy(recb[:, :], pBC[:, :]).then_inc(S["recv"], 1)
                v.drain()
                v.wait_ge(S["ost"], 16 * (gp - 1))
                v.tensor_tensor(obs[:, (gp % 2) * QB:(gp % 2 + 1) * QB],
                                pOs[gp % 2][:, :], recb[:, :],
                                ALU.mult).then_inc(S["oev"], 1)

            @block.tensor
            def _(t):
                pend = []

                def mmoz(grp, h, qb, gq, kt):
                    ga = ga_base[(h, qb)] + kt
                    n = nkt[qb]
                    wait_sq(t, ga)
                    if is_masked(qb, kt):
                        t.wait_ge(S["mka"], mk_idx[(h, qb, kt)] + 1)
                        asrc = ams
                    else:
                        asrc = a_s
                    if kt == 0 and gq >= 2:
                        t.wait_ge(S["oev"], gq - 1)
                        t.wait_ge(S["zrv"], gq - 1)
                    t.matmul(pOs[gq % 2][:, :],
                             vhs[:, (grp % 2) * 32 * 128 + kt * 128:
                                 (grp % 2) * 32 * 128 + (kt + 1) * 128],
                             asrc[:, (ga % 3) * QB:(ga % 3 + 1) * QB],
                             start=(kt == 0), stop=(kt == n - 1)
                             ).then_inc(S["p2oz"], 1)
                    t.matmul(pZs[gq % 2][:, :], onesc[:, :],
                             asrc[:, (ga % 3) * QB:(ga % 3 + 1) * QB],
                             start=(kt == 0), stop=(kt == n - 1)
                             ).then_inc(S["p2oz"], 1)

                for grp in range(16):
                    half, h = grp // 8, grp % 8
                    t.wait_ge(S["kvld"], 16 * (grp + 1))
                    t.wait_ge(S["kvv"], 16 * (grp + 1))
                    for j in range(4):
                        qb = half * 4 + j
                        gq = grp * 4 + j
                        t.wait_ge(S["qld"], 16 * (gq + 1))
                        n = nkt[qb]

                        def bc(gq=gq):
                            t.wait_ge(S["rcs"], gq)
                            if gq >= 2:
                                t.wait_ge(S["recv"], gq - 1)
                            t.matmul(pBC[:, :], onesrf[:, :], rcr[:, :],
                                     start=True, stop=True
                                     ).then_inc(S["bcm"], 1)

                        for step in range(n):
                            if step == 4 and gq >= 1:
                                bc()
                            kt = step
                            ga = ga_base[(h, qb)] + kt
                            if ga >= 3:
                                wait_sq(t, ga - 3)
                            t.matmul(pS[ga % 3][:, :],
                                     khs[:, (grp % 2) * T + kt * 128:
                                         (grp % 2) * T + (kt + 1) * 128],
                                     qbs[:, (gq % 4) * QB:(gq % 4 + 1) * QB],
                                     start=True, stop=True
                                     ).then_inc(S["p2ms"], 1)
                            if step < 2 and pend:
                                pend.pop(0)()
                            if step >= 2:
                                mmoz(grp, h, qb, gq, step - 2)
                        if n <= 4 and gq >= 1:
                            bc()
                        pend.append(
                            (lambda a, b, c, d, e:
                             lambda: mmoz(a, b, c, d, e))(grp, h, qb, gq, n - 2))
                        pend.append(
                            (lambda a, b, c, d, e:
                             lambda: mmoz(a, b, c, d, e))(grp, h, qb, gq, n - 1))
                for f in pend:
                    f()
                gq = 64
                t.wait_ge(S["rcs"], gq)
                t.wait_ge(S["recv"], gq - 1)
                t.matmul(pBC[:, :], onesrf[:, :], rcr[:, :],
                         start=True, stop=True).then_inc(S["bcm"], 1)

    if 3 in phases:
        NT3 = TH // 128
        with (
            nc.sbuf_tensor("oms", [128, 16 * TH], BF16) as oms,
            nc.sbuf_tensor("xrs", [128, 2 * DM], F32) as xrs,
            nc.sbuf_tensor("z1s", [128, DM], F32) as z1s,
            nc.sbuf_tensor("bns", [128, 24], F32) as bns,
            nc.sbuf_tensor("mvs", [128, 2], F32) as mvs,
            nc.sbuf_tensor("l3a", [128, 1], F32) as l3a,
            nc.sbuf_tensor("l3b", [128, 1], F32) as l3b,
            nc.sbuf_tensor("l3c", [128, 1], F32) as l3c,
            nc.sbuf_tensor("l3d", [128, 1], F32) as l3d,
            nc.sbuf_tensor("x1s", [128, 2 * DM], BF16) as x1s,
            nc.sbuf_tensor("x1t", [128, DM], BF16) as x1t,
            nc.psum_tensor("pW0", [128, 512], F32) as pW0,
            nc.psum_tensor("pW1", [128, 512], F32) as pW1,
            nc.psum_tensor("pW2", [128, 512], F32) as pW2,
            nc.psum_tensor("pW3", [128, 512], F32) as pW3,
            nc.psum_tensor("pT0", [128, 128], BF16) as pT0,
            nc.psum_tensor("pT1", [128, 128], BF16) as pT1,
            nc.Block() as block,
        ):
            pW = [pW0, pW1, pW2, pW3]
            pT = [pT0, pT1]

            @block.gpsimd
            def _(g):
                gsems = [S["gth"], S["gth1"], S["gth2"], S["gth3"]]
                cnt = [0, 0, 0, 0]
                prev_req = 0
                for idx, gi in enumerate(P3_A + P3_B + P3_C + P3_D):
                    req = 3 if idx < 8 else 4 if idx < 12 else \
                        5 if idx < 14 else 6
                    if req != prev_req:
                        g.wait_ge(S["agc"], req)
                        prev_req = req
                    sm = idx % 4 if idx < 12 else \
                        (idx - 12) if idx < 14 else (idx - 12)
                    if cnt[sm] >= 1:
                        g.wait_ge(gsems[sm], 16 * cnt[sm])
                    g.indirect_dma_start(
                        oms[:, gi * TH:(gi + 1) * TH], None,
                        agout[:, :],
                        IndirectOffsetOnAxis(ap=oidx[:, gi:gi + 1], axis=0),
                    ).then_inc(gsems[sm], 16)
                    cnt[sm] += 1

            @block.sync
            def _(sy):
                for tt in range(NT3):
                    if tt >= 2:
                        sy.wait_ge(S["z1v"], 4 * (tt - 1))
                    if tt >= 1:
                        sy.wait_ge(S["xrl"], 16 * tt)
                    sy.dma_start(xrs[:, (tt % 2) * DM:(tt % 2 + 1) * DM],
                                 xres[tt * 128:(tt + 1) * 128, :]
                                 ).then_inc(S["xrl"], 16)

            @block.tensor
            def _(t):
                for s4 in ("gth", "gth1", "gth2", "gth3"):
                    t.wait_ge(S[s4], 32)
                t.wait_ge(S["wol"], 16)
                for tt in range(NT3):
                    def wo_mm(eb, fc, start, stop):
                        t.matmul(pW[eb][:, :],
                                 oms[:, fc * TH + tt * 128:
                                     fc * TH + (tt + 1) * 128],
                                 wos[:, fc * DM + eb * 512:
                                     fc * DM + (eb + 1) * 512],
                                 start=start, stop=stop
                                 ).then_inc(S["p3mm"], 1)

                    for eb in range(4):
                        if tt >= 1:
                            t.wait_ge(S["z1v"], 4 * (tt - 1) + eb + 1)
                        for k, fc in enumerate(P3_A):
                            wo_mm(eb, fc, k == 0, False)
                    if tt == 0:
                        for s4 in ("gth", "gth1", "gth2", "gth3"):
                            t.wait_ge(S[s4], 48)
                    for eb in range(4):
                        for fc in P3_B:
                            wo_mm(eb, fc, False, False)
                    if tt == 0:
                        t.wait_ge(S["gth"], 64)
                        t.wait_ge(S["gth1"], 64)
                    for eb in range(4):
                        for fc in P3_C:
                            wo_mm(eb, fc, False, False)
                    if tt == 0:
                        t.wait_ge(S["gth2"], 64)
                        t.wait_ge(S["gth3"], 64)
                    for eb in range(4):
                        for k, fc in enumerate(P3_D):
                            wo_mm(eb, fc, False, k == len(P3_D) - 1)
                    if tt >= 2:
                        t.wait_ge(S["apv"], tt - 1)
                        for dc in range(16):
                            gtr = 16 * (tt - 2) + dc
                            if gtr >= 2:
                                t.wait_ge(S["trv"], gtr - 1)
                            t.transpose(pT[gtr % 2][:, :],
                                        x1s[:, ((tt - 2) % 2) * DM + dc * 128:
                                            ((tt - 2) % 2) * DM + (dc + 1) * 128],
                                        ident[:, :]).then_inc(S["trm"], 1)
                for tl in (NT3 - 2, NT3 - 1):
                    t.wait_ge(S["apv"], tl + 1)
                    for dc in range(16):
                        gtr = 16 * tl + dc
                        if gtr >= 2:
                            t.wait_ge(S["trv"], gtr - 1)
                        t.transpose(pT[gtr % 2][:, :],
                                    x1s[:, (tl % 2) * DM + dc * 128:
                                        (tl % 2) * DM + (dc + 1) * 128],
                                    ident[:, :]).then_inc(S["trm"], 1)

            @block.vector
            def _(v):
                for tt in range(NT3):
                    v.drain()
                    v.wait_ge(S["xrl"], 16 * (tt + 1))
                    for eb in range(4):
                        v.wait_ge(S["p3mm"], tt * 64 + 56 + 2 * (eb + 1))
                        v.tensor_tensor(z1s[:, eb * 512:(eb + 1) * 512],
                                        pW[eb][:, :],
                                        xrs[:, (tt % 2) * DM + eb * 512:
                                            (tt % 2) * DM + (eb + 1) * 512],
                                        ALU.add).then_inc(S["z1v"], 1)
                    v.drain()
                    for eb in range(4):
                        v.bn_stats(bns[:, eb * 6:(eb + 1) * 6],
                                   z1s[:, eb * 512:(eb + 1) * 512])
                    v.drain()
                    v.bn_aggr(mvs[:, :], bns[:, :])
                    v.drain()
                    v.tensor_scalar(l3a[:, :], mvs[:, 1:2], EPS, None,
                                    ALU.add, ALU.bypass).then_inc(S["lnv"], 1)
                    v.wait_ge(S["p3sq"], tt + 1)
                    v.reciprocal(l3c[:, :], l3b[:, :]).then_inc(S["l3u"], 1)
                    v.drain()
                    v.tensor_scalar(l3d[:, :], mvs[:, 0:1], l3c[:, :], -1.0,
                                    ALU.mult, ALU.mult)
                    v.drain()
                    if tt >= 2:
                        v.wait_ge(S["trm"], 16 * (tt - 1))
                    for eb in range(4):
                        v.tensor_scalar(x1s[:, (tt % 2) * DM + eb * 512:
                                            (tt % 2) * DM + (eb + 1) * 512],
                                        z1s[:, eb * 512:(eb + 1) * 512],
                                        l3c[:, :], l3d[:, :],
                                        ALU.mult, ALU.add)
                    v.engine_nop().then_inc(S["apv"], 1)


            @block.scalar
            def _(s):
                def trevac(s, tl):
                    if tl >= 1:
                        s.wait_ge(S["x1st"], 16 * tl)
                    for dc in range(16):
                        gtr = 16 * tl + dc
                        s.wait_ge(S["trm"], gtr + 1)
                        s.copy(x1t[:, dc * 128:(dc + 1) * 128],
                               pT[gtr % 2][:, :]).then_inc(S["trv"], 1)
                    s.wait_ge(S["trv"], 16 * (tl + 1))
                    s.dma_start(
                        _r(x1f[:, tl * 128:(tl + 1) * 128],
                           "(c p) t -> p c t", p=128),
                        _r(x1t[:, :], "p (c t) -> p c t", c=16)
                    ).then_inc(S["x1st"], 16)

                for tt in range(NT3):
                    s.wait_ge(S["lnv"], tt + 1)
                    if tt >= 1:
                        s.wait_ge(S["l3u"], tt)
                    s.sqrt(l3b[:, :], l3a[:, :]).then_inc(S["p3sq"], 1)
                    if tt >= 2:
                        trevac(s, tt - 2)
                trevac(s, NT3 - 2)
                trevac(s, NT3 - 1)
                s.wait_ge(S["x1st"], 16 * NT3)

    if wos_cm is not None:
        wos_cm.__exit__(None, None, None)

    if 4 in phases:
        NHC2 = 22
        NEB = 4

        def p4_dmm_after_chunk4(gw):
            tbw, rem = divmod(gw, NEB * 11)
            ebw, hgw = divmod(rem, 11)
            return tbw * 1536 + ebw * 384 + (hgw + 1) * 32

        def p4_dmm_group_end(tb, eb, tt):
            return tb * 1536 + eb * 384 + 352 + (tt + 1) * 4

        def gu_done(tb, hc2, t2, hsub, part):
            return tb * 2816 + hc2 * 128 + t2 * 64 + hsub * 32 + 16 * part

        with contextlib.ExitStack() as _st:
            _e = _st.enter_context
            identE = _e(nc.sbuf_tensor("identE_s", [128, 4 * 512], BF16))
            x1b = _e(nc.sbuf_tensor("x1b", [128, 16 * FTB], BF16))
            wgs = _e(nc.sbuf_tensor("wgs", [128, 3 * 16 * 256], BF16))
            wgu8 = _e(nc.sbuf_tensor("wgu8", [128, 16 * 256], U8))
            wus = _e(nc.sbuf_tensor("wus", [128, 3 * 16 * 256], BF16))
            wuu8 = _e(nc.sbuf_tensor("wuu8", [128, 16 * 256], U8))
            gsb = _e(nc.sbuf_tensor("gsb", [128, NHC * FTB], BF16))
            gst = _e(nc.sbuf_tensor("gst", [128, 2 * 512], BF16))
            ust = _e(nc.sbuf_tensor("ust", [128, 2 * 512], BF16))
            wds = _e(nc.sbuf_tensor("wds", [128, 2 * 4 * 512], BF16))
            wdu8 = _e(nc.sbuf_tensor("wdu8", [128, 4 * 512], U8))
            dstg = _e(nc.sbuf_tensor("dstg", [128, 2 * 512], F32))
            pP0 = _e(nc.psum_tensor("pP0", [128, 512], F32))
            pP1 = _e(nc.psum_tensor("pP1", [128, 512], F32))
            pP2 = _e(nc.psum_tensor("pP2", [128, 512], F32))
            pP3 = _e(nc.psum_tensor("pP3", [128, 512], F32))
            pP4 = _e(nc.psum_tensor("pP4", [128, 512], F32))
            pP5 = _e(nc.psum_tensor("pP5", [128, 512], F32))
            pP6 = _e(nc.psum_tensor("pP6", [128, 512], F32))
            pP7 = _e(nc.psum_tensor("pP7", [128, 512], F32))
            block = _e(nc.Block())
            True
            pP = [pP0, pP1, pP2, pP3, pP4, pP5, pP6, pP7]
            pGs = [pP0, pP1, pP2]
            pUs = [pP3, pP4, pP5]

            @block.scalar
            def _(s):
                s.dma_start(identE[:, :], identEd[:, :]).then_inc(S["ield"], 16)
                for tb in range(2):
                    if tb >= 1:
                        s.wait_ge(S["gumm"], 2816)
                        s.wait_ge(S["dmm"], 1536)
                        s.wait_ge(S["x1l"], 16)
                    s.dma_start(
                        _r(x1b[:, :], "p (c t) -> p c t", c=16),
                        _r(x1f[:, tb * FTB:(tb + 1) * FTB],
                           "(c p) t -> p c t", p=128)).then_inc(S["x1l"], 16)
                    for hc2 in range(NHC2):
                        for t2 in range(2):
                            for hsub in range(2):
                                gt = tb * 88 + hc2 * 4 + t2 * 2 + hsub
                                hc = hc2 * 2 + hsub
                                s.wait_ge(S["gumm"],
                                          gu_done(tb, hc2, t2, hsub, 1))
                                if gt >= 2:
                                    s.wait_ge(S["gml"], gt - 1)
                                s.activation(gst[:, (gt % 2) * 512:
                                                 (gt % 2 + 1) * 512],
                                             pGs[gt % 3][:, :], AFT.Silu,
                                             bias=0.0, scale=sg[:, hc:hc + 1]
                                             ).then_inc(S["sil"], 1)

            @block.gpsimd
            def _(g):
                for tb in range(2):
                    for hc2 in range(NHC2):
                        gc = tb * NHC2 + hc2
                        if gc >= 1:
                            g.wait_ge(S["deq"], gc)
                            g.wait_ge(S["wldg"], 32 * gc)
                        sl8 = 0
                        g.dma_start(
                            wgu8[:, sl8:sl8 + 16 * 256],
                            wgT[:, hc2 * 16 * 256:(hc2 + 1) * 16 * 256]
                            ).then_inc(S["wldg"], 16)
                        g.dma_start(
                            wuu8[:, sl8:sl8 + 16 * 256],
                            wuT[:, hc2 * 16 * 256:(hc2 + 1) * 16 * 256]
                            ).then_inc(S["wldg"], 16)
                    for eb in range(NEB):
                        for hg4 in range(11):
                            gw = tb * NEB * 11 + eb * 11 + hg4
                            if gw >= 1:
                                g.wait_ge(S["wdc"], gw)
                                g.wait_ge(S["wdl"], 16 * gw)
                            g.dma_start(
                                wdu8[:, 0:2048],
                                wdT[:, (eb * NHC + hg4 * 4) * 512:
                                    (eb * NHC + hg4 * 4 + 4) * 512]
                                ).then_inc(S["wdl"], 16)

            @block.vector
            def _(v):
                def deq_chunk(gc):
                    v.wait_ge(S["wldg"], 32 * (gc + 1))
                    if gc >= 3:
                        v.wait_ge(S["gumm"], (gc - 2) * 128)
                    sl8 = 0
                    slb = (gc % 3) * 16 * 256
                    v.tensor_scalar(wgs[:, slb:slb + 16 * 256],
                                    wgu8[:, sl8:sl8 + 16 * 256],
                                    -128.0, None, ALU.add, ALU.bypass)
                    v.drain()
                    v.tensor_scalar(wus[:, slb:slb + 16 * 256],
                                    wuu8[:, sl8:sl8 + 16 * 256],
                                    -128.0, None, ALU.add, ALU.bypass
                                    ).then_inc(S["deq"], 1)

                for tb in range(2):
                    deq_chunk(tb * NHC2 + 0)
                    deq_chunk(tb * NHC2 + 1)
                    deq_chunk(tb * NHC2 + 2)
                    for hc2 in range(NHC2):
                        gc = tb * NHC2 + hc2
                        for t2 in range(2):
                            for hsub in range(2):
                                gt = tb * 88 + hc2 * 4 + t2 * 2 + hsub
                                hc = hc2 * 2 + hsub
                                v.wait_ge(S["gumm"],
                                          gu_done(tb, hc2, t2, hsub, 2))
                                if gt >= 2:
                                    v.wait_ge(S["gml"], gt - 1)
                                v.tensor_scalar(ust[:, (gt % 2) * 512:
                                                    (gt % 2 + 1) * 512],
                                                pUs[gt % 3][:, :],
                                                su[:, hc:hc + 1], None,
                                                ALU.mult, ALU.bypass)
                                v.wait_ge(S["sil"], gt + 1)
                                v.drain()
                                v.tensor_tensor(
                                    gsb[:, hc * FTB + t2 * 512:
                                        hc * FTB + (t2 + 1) * 512],
                                    gst[:, (gt % 2) * 512:(gt % 2 + 1) * 512],
                                    ust[:, (gt % 2) * 512:(gt % 2 + 1) * 512],
                                    ALU.mult).then_inc(S["gml"], 1)
                        if hc2 + 3 < NHC2:
                            deq_chunk(gc + 3)
                    for eb in range(NEB):
                        for hg4 in range(11):
                            gw = tb * NEB * 11 + eb * 11 + hg4
                            v.wait_ge(S["wdl"], 16 * (gw + 1))
                            if gw >= 2:
                                v.wait_ge(S["dmm"],
                                          p4_dmm_after_chunk4(gw - 2))
                            sw = (gw % 2) * 2048
                            v.tensor_scalar(wds[:, sw:sw + 2048],
                                            wdu8[:, 0:2048],
                                            -128.0, None, ALU.add, ALU.bypass
                                            ).then_inc(S["wdc"], 1)
                            v.drain()
                            for q4 in range(4):
                                v.tensor_tensor(
                                    wds[:, sw + q4 * 512:sw + (q4 + 1) * 512],
                                    wds[:, sw + q4 * 512:sw + (q4 + 1) * 512],
                                    sdbc[:, eb * 512:(eb + 1) * 512],
                                    ALU.mult).then_inc(S["wdq"], 1)
                        for tt in range(8):
                            ge = tb * 32 + eb * 8 + tt
                            v.wait_ge(S["dmm"], p4_dmm_group_end(tb, eb, tt))
                            if ge >= 2:
                                v.wait_ge(S["dst"], 16 * (ge - 1))
                            sidx = ((tb * 8 + tt) * 4 + eb) * 6
                            v.bn_stats(bst5[:, sidx:sidx + 6], pP[tt][:, :])
                            v.drain()
                            v.tensor_copy(dstg[:, (ge % 2) * 512:
                                              (ge % 2 + 1) * 512],
                                          pP[tt][:, :]).then_inc(S["dev"], 1)

            @block.tensor
            def _(t):
                t.wait_ge(S["ield"], 16)
                for tb in range(2):
                    t.wait_ge(S["x1l"], 16 * (tb + 1))
                    if tb >= 1:
                        t.wait_ge(S["dev"], 32 * tb)
                    for hc2 in range(NHC2):
                        gc = tb * NHC2 + hc2
                        t.wait_ge(S["deq"], gc + 1)
                        sl = (gc % 3) * 16 * 256
                        for t2 in range(2):
                            for hsub in range(2):
                                gt = tb * 88 + hc2 * 4 + t2 * 2 + hsub
                                if gt >= 3:
                                    t.wait_ge(S["sil"], gt - 2)
                                    t.wait_ge(S["gml"], gt - 2)
                                for fc in range(16):
                                    t.matmul(pGs[gt % 3][:, :],
                                             wgs[:, sl + fc * 256 + hsub * 128:
                                                 sl + fc * 256 + (hsub + 1) * 128],
                                             x1b[:, fc * FTB + t2 * 512:
                                                 fc * FTB + (t2 + 1) * 512],
                                             start=(fc == 0), stop=(fc == 15)
                                             ).then_inc(S["gumm"], 1)
                                for fc in range(16):
                                    t.matmul(pUs[gt % 3][:, :],
                                             wus[:, sl + fc * 256 + hsub * 128:
                                                 sl + fc * 256 + (hsub + 1) * 128],
                                             x1b[:, fc * FTB + t2 * 512:
                                                 fc * FTB + (t2 + 1) * 512],
                                             start=(fc == 0), stop=(fc == 15)
                                             ).then_inc(S["gumm"], 1)
                    for eb in range(NEB):
                        for hc in range(NHC):
                            hg4, q4 = divmod(hc, 4)
                            gw = tb * NEB * 11 + eb * 11 + hg4
                            t.wait_ge(S["wdq"], 4 * gw + q4 + 1)
                            if hc == 0:
                                geb = tb * NEB + eb
                                if geb >= 1:
                                    t.wait_ge(S["dev"], (geb - 1) * 8 + 8)
                                t.wait_ge(S["gml"], tb * 88 + 88)
                            sw = (gw % 2) * 2048 + q4 * 512
                            for tt in range(8):
                                t.matmul(pP[tt][:, :],
                                         gsb[:, hc * FTB + tt * 128:
                                             hc * FTB + (tt + 1) * 128],
                                         wds[:, sw:sw + 512],
                                         start=(hc == 0), stop=False
                                         ).then_inc(S["dmm"], 1)
                        for tt in range(8):
                            for j in range(4):
                                dc = eb * 4 + j
                                t.matmul(pP[tt][:, :],
                                         x1b[:, dc * FTB + tt * 128:
                                             dc * FTB + (tt + 1) * 128],
                                         identE[:, j * 512:(j + 1) * 512],
                                         start=False, stop=(j == 3)
                                         ).then_inc(S["dmm"], 1)

            @block.sync
            def _(sy):
                for tb in range(2):
                    for eb in range(NEB):
                        for tt in range(8):
                            ge = tb * 32 + eb * 8 + tt
                            sy.wait_ge(S["dev"], ge + 1)
                            if ge >= 1:
                                sy.wait_ge(S["dst"], 16 * ge)
                            sy.dma_start(
                                opre[tb * FTB + tt * 128:
                                     tb * FTB + (tt + 1) * 128,
                                     eb * 512:(eb + 1) * 512],
                                dstg[:, (ge % 2) * 512:(ge % 2 + 1) * 512]
                                ).then_inc(S["dst"], 16)
                sy.wait_ge(S["dst"], 16 * 64)

    if 5 in phases:
        NT5 = TH // 128
        with (
            nc.sbuf_tensor("ops", [128, 2 * DM], F32) as ops,
            nc.sbuf_tensor("bn5", [128, 24], F32) as bn5,
            nc.sbuf_tensor("mv5", [128, 2], F32) as mv5,
            nc.sbuf_tensor("l5a", [128, 1], F32) as l5a,
            nc.sbuf_tensor("l5b", [128, 1], F32) as l5b,
            nc.sbuf_tensor("l5c", [128, 1], F32) as l5c,
            nc.sbuf_tensor("l5d", [128, 1], F32) as l5d,
            nc.sbuf_tensor("o5s", [128, 2 * DM], F32) as o5s,
            nc.Block() as block,
        ):
            @block.sync
            def _(sy):
                for tt in range(NT5):
                    if tt >= 1:
                        sy.wait_ge(S["p5ap"], tt)
                        if tt >= 2:
                            sy.wait_ge(S["p5st"], 16 * (tt - 1))
                        sy.dma_start(out[(tt - 1) * 128: tt * 128, :],
                                     o5s[:, ((tt - 1) % 2) * DM:
                                         ((tt - 1) % 2 + 1) * DM]
                                     ).then_inc(S["p5st"], 16)
                sy.wait_ge(S["p5ap"], NT5)
                sy.wait_ge(S["p5st"], 16 * (NT5 - 1))
                sy.dma_start(out[(NT5 - 1) * 128: NT5 * 128, :],
                             o5s[:, ((NT5 - 1) % 2) * DM:
                                 ((NT5 - 1) % 2 + 1) * DM]
                             ).then_inc(S["p5st"], 16)
                sy.wait_ge(S["p5st"], 16 * NT5)

            @block.vector
            def _(v):
                for tt in range(NT5):
                    v.drain()
                    v.wait_ge(S["p5l"], 16 * (tt + 1))
                    v.bn_aggr(mv5[:, :], bst5[:, tt * 24:(tt + 1) * 24])
                    v.drain()
                    v.tensor_scalar(l5a[:, :], mv5[:, 1:2], EPS, None,
                                    ALU.add, ALU.bypass).then_inc(S["p5bn"], 1)
                    v.wait_ge(S["p5sq"], tt + 1)
                    v.reciprocal(l5c[:, :], l5b[:, :]).then_inc(S["l5u"], 1)
                    v.drain()
                    v.tensor_scalar(l5d[:, :], mv5[:, 0:1], l5c[:, :], -1.0,
                                    ALU.mult, ALU.mult)
                    v.drain()
                    if tt >= 2:
                        v.wait_ge(S["p5st"], 16 * (tt - 1))
                    for eb in range(4):
                        v.tensor_scalar(o5s[:, (tt % 2) * DM + eb * 512:
                                            (tt % 2) * DM + (eb + 1) * 512],
                                        ops[:, (tt % 2) * DM + eb * 512:
                                            (tt % 2) * DM + (eb + 1) * 512],
                                        l5c[:, :], l5d[:, :],
                                        ALU.mult, ALU.add)
                    v.engine_nop().then_inc(S["p5ap"], 1)

            @block.scalar
            def _(s):
                for tt in range(NT5):
                    if tt >= 2:
                        s.wait_ge(S["p5ap"], tt - 1)
                    if tt >= 1:
                        s.wait_ge(S["p5l"], 16 * tt)
                    s.dma_start(ops[:, (tt % 2) * DM:(tt % 2 + 1) * DM],
                                opre[tt * 128:(tt + 1) * 128, :]
                                ).then_inc(S["p5l"], 16)
                    s.wait_ge(S["p5bn"], tt + 1)
                    if tt >= 1:
                        s.wait_ge(S["l5u"], tt)
                    s.sqrt(l5b[:, :], l5a[:, :]).then_inc(S["p5sq"], 1)

    if dbg:
        with nc.Block() as block:
            @block.gpsimd
            def _(g):
                n = 0
                for name, (dt_, srcd) in dbg.items():
                    if n:
                        g.wait_ge(S["dbgs"], n)
                    g.dma_start(dt_[:, :], srcd[:, :]).then_inc(S["dbgs"], 16)
                    n += 16
                g.wait_ge(S["dbgs"], n)

    return nc



def _consts():
    eye = np.eye(128, dtype=np.float32)
    cm = np.asarray(eye - 1.0 / 128.0, np.float32).astype(ml_dtypes.bfloat16)
    idm = eye.astype(ml_dtypes.bfloat16)
    r = np.arange(128)[:, None]
    c = np.arange(QB)[None, :]
    masks = np.concatenate(
        [(c >= r + 128 * j).astype(np.float32) for j in range(4)],
        axis=1).astype(ml_dtypes.bfloat16)
    identE = np.zeros((128, 4 * 512), np.float32)
    for j in range(4):
        identE[np.arange(128), j * 512 + j * 128 + np.arange(128)] = 1.0
    identE = identE.astype(ml_dtypes.bfloat16)
    onescol = np.ones((128, 1), ml_dtypes.bfloat16)
    onesrow = np.ones((1, 128), ml_dtypes.bfloat16)
    return cm, idm, masks, identE, onescol, onesrow


def make_in_maps(inputs):
    x = np.asarray(inputs["L0_attn_hidden_states"], np.float32)
    Wq = np.asarray(inputs["L0_attn_Wq"], np.float32)
    Wk = np.asarray(inputs["L0_attn_Wk"], np.float32)
    Wv = np.asarray(inputs["L0_attn_Wv"], np.float32)
    Wo = np.asarray(inputs["L0_attn_Wo"], np.float32)
    gamma = np.asarray(inputs["L0_attn_gamma"], np.float32)
    beta = np.asarray(inputs["L0_attn_beta"], np.float32)
    wqg = np.asarray(inputs["L0_ffn_wq_gate"], np.int32)
    sgv = np.asarray(inputs["L0_ffn_s_gate"], np.float32)
    wqu = np.asarray(inputs["L0_ffn_wq_up"], np.int32)
    suv = np.asarray(inputs["L0_ffn_s_up"], np.float32)
    wqd = np.asarray(inputs["L0_ffn_wq_down"], np.int32)
    sdv = np.asarray(inputs["L0_ffn_s_down"], np.float32)

    cm, idm, masks, identE, onescol, onesrow = _consts()
    woTh = np.ascontiguousarray(
        Wo.T.reshape(16, 128, DM).transpose(1, 0, 2).reshape(128, 16 * DM))
    wgTh = np.ascontiguousarray(
        wqg.T.reshape(16, 128, 22, 256).transpose(1, 2, 0, 3)
        .reshape(128, 22 * 16 * 256).astype(np.uint8))
    wuTh = np.ascontiguousarray(
        wqu.T.reshape(16, 128, 22, 256).transpose(1, 2, 0, 3)
        .reshape(128, 22 * 16 * 256).astype(np.uint8))
    wdTh = np.ascontiguousarray(
        wqd.T.reshape(NHC, 128, 4, 512).transpose(1, 2, 0, 3)
        .reshape(128, 4 * NHC * 512).astype(np.uint8))
    sg2 = np.ascontiguousarray(sgv.reshape(NHC, 128).T)
    su2 = np.ascontiguousarray(suv.reshape(NHC, 128).T)
    sdr = np.ascontiguousarray(sdv[None, :])

    maps = []
    for c in range(8):
        b, hg = c // 2, c % 2
        p = np.arange(128, dtype=np.int64)[:, None]
        i = np.arange(16, dtype=np.int64)[None, :]
        slot = 2 * b + i // 8
        cum = np.asarray(AG_CUM, dtype=np.int64)
        grp_row = hg * 8 + (i % 8)
        ci = np.searchsorted(cum, grp_row, side="right") - 1
        rows = (cum[ci + 1] - cum[ci]) * 128
        oix = (cum[ci] * 128 * 8 + slot * rows
               + (grp_row - cum[ci]) * 128 + p).astype(np.uint32)
        maps.append({
            "xT": np.ascontiguousarray(
                x[b].T.reshape(16, 128, NTB, TB).transpose(1, 2, 0, 3)
                .reshape(128, NTB * 16 * TB)),
            "xres": np.ascontiguousarray(x[b, hg * TH:(hg + 1) * TH]),
            "wq": np.ascontiguousarray(Wq[hg * EQ:(hg + 1) * EQ]),
            "wk": np.ascontiguousarray(Wk[hg * EQ:(hg + 1) * EQ]),
            "wvT": np.ascontiguousarray(
                Wv[hg * EQ:(hg + 1) * EQ].T.reshape(16, 128, EQ)
                .transpose(1, 0, 2).reshape(128, 16 * EQ)),
            "woT": woTh, "wgT": wgTh, "wuT": wuTh, "wdT": wdTh,
            "sg": sg2, "su": su2, "sdrow": sdr,
            "gam": np.ascontiguousarray(gamma[:, None]),
            "bet": np.ascontiguousarray(beta[:, None]),
            "cmat": cm, "identm": idm, "masks": masks, "identE": identE,
            "onescol": onescol, "onesrow": onesrow,
            "onesrowf": np.ones((1, 128), np.float32), "oidx": oix,
        })
    return maps


def kernel(**inputs):
    maps = make_in_maps(inputs)
    nc = build()
    r = run_bass_kernel_spmd(nc, maps, core_ids=list(range(8)))
    outf = np.empty((4, T, DM), np.float32)
    for c in range(8):
        b, hg = c // 2, c % 2
        outf[b, hg * TH:(hg + 1) * TH] = np.asarray(r.results[c]["out"])
    return outf
